# version 04
# baseline (speedup 1.0000x reference)
import math
from collections import defaultdict
from contextlib import ExitStack

import numpy as np
import ml_dtypes
import concourse.bass as bass
import concourse.mybir as mybir
from concourse.bass_utils import run_bass_kernel_spmd

F32 = mybir.dt.float32
BF16 = mybir.dt.bfloat16
AF = mybir.ActivationFunctionType
ALU = mybir.AluOpType
AX = mybir.AxisListType

D = 1024
T = 4096
NC = 8
NT = 8
NS = 32
H = 16
HS = 64
HID = 4096
NHC = 32


class Ctx:

    def __init__(self, nc, stack):
        self.nc = nc
        self.stack = stack
        self.sems = {}
        self.cnt = defaultdict(int)

    def sem(self, name):
        if name not in self.sems:
            self.sems[name] = self.stack.enter_context(self.nc.semaphore(name))
        return self.sems[name]

    def inc(self, inst, name, n=1):
        inst.then_inc(self.sem(name), n)
        self.cnt[name] += n
        return inst

    def v(self, name):
        self.sem(name)
        return self.cnt[name]

    def barrier(self):
        nc = self.nc
        names = [n for n in ("mm", "ea", "ev", "vv", "gpc", "ld", "ldx", "st")
                 if n in self.sems]
        vals = [(self.sem(n), self.cnt[n]) for n in names]
        with nc.Block() as block:
            def waits(eng):
                for sm, v in vals:
                    eng.wait_ge(sm, v)
            block.tensor(lambda te: waits(te))
            block.vector(lambda ve: waits(ve))
            block.scalar(lambda se: waits(se))
            block.gpsimd(lambda gp: waits(gp))
            block.sync(lambda sy: waits(sy))

    def seq(self, eng, name, inst):
        self.inc(inst, name)
        eng.wait_ge(self.sem(name), self.cnt[name])
        return inst


def _bf(x):
    return np.asarray(x, dtype=np.float32).astype(ml_dtypes.bfloat16)


def host_consts():
    c = {}
    ones2 = np.zeros((D, H), dtype=np.float32)
    for fc in range(NC):
        ones2[128 * fc:128 * fc + 64, 2 * fc] = 1.0
        ones2[128 * fc + 64:128 * fc + 128, 2 * fc + 1] = 1.0
    c["C_ones2"] = ones2
    sel = np.zeros((H, D), dtype=np.float32)
    for f in range(D):
        sel[f // HS, f] = 1.0
    c["C_sel"] = sel
    c["C_stair"] = (np.arange(896)[None, :] >= np.arange(128)[:, None] + 384).astype(np.float32)
    c["C_stril"] = (np.arange(128)[:, None] > np.arange(128)[None, :]).astype(np.float32)
    c["C_triu_incl"] = (np.arange(128)[:, None] <= np.arange(128)[None, :]).astype(np.float32)
    c["C_stril4"] = np.tile(c["C_stril"], (1, 4))
    c["C_triu4"] = np.tile(c["C_triu_incl"], (1, 4))
    c["C_eye"] = np.eye(128, dtype=np.float32)
    return c


def to_pc(vec):
    v = np.asarray(vec, dtype=np.float32).reshape(-1)
    n = v.shape[0] // 128
    return np.ascontiguousarray(v.reshape(n, 128).T)


def mm_accum(te, ps, pairs):
    inst = None
    n = len(pairs)
    for i, (l, r) in enumerate(pairs):
        inst = te.matmul(ps, l, r, start=(i == 0), stop=(i == n - 1))
    return inst


def emit_proj_headnorm(ctx, bufs, w_sb, out_ap, mode, gcol=None, bcol=None, eps=1e-5):
    nc = ctx.nc
    (x_sb, ones2_sb, sel_sb, pre32, sq_sb, val_bf, st16, st16b,
     rb_sb, t1_sb, out_bf, ps_a, ps_b, ps_ss, ps_mb, ps_rb) = bufs
    need_mean = (mode == "qkln")
    OPS = 3 if need_mean else 2

    for tt in range(NT):
        t0, t1 = 512 * tt, 512 * (tt + 1)
        b_mm = ctx.v("mm")
        b_ea = ctx.v("ea")
        b_ev = ctx.v("ev")
        b_st = ctx.v("st")

        with nc.Block() as block:
            @block.tensor
            def _(te):
                te.wait_ge(ctx.sem("ea"), b_ea)
                te.wait_ge(ctx.sem("ev"), b_ev)
                for fc in range(NC):
                    ps = ps_a if fc % 2 == 0 else ps_b
                    if fc >= 2:
                        te.wait_ge(ctx.sem("ea"), b_ea + OPS * (fc - 1))
                    inst = mm_accum(te, ps[:, :],
                                    [(w_sb[:, D * kc + 128 * fc:D * kc + 128 * (fc + 1)],
                                      x_sb[:, T * kc + t0:T * kc + t1]) for kc in range(NC)])
                    ctx.inc(inst, "mm")

            @block.scalar
            def _(se):
                se.wait_ge(ctx.sem("st"), b_st)
                se.wait_ge(ctx.sem("ev"), b_ev)
                for fc in range(NC):
                    ps = ps_a if fc % 2 == 0 else ps_b
                    se.wait_ge(ctx.sem("mm"), b_mm + fc + 1)
                    if mode == "qkln":
                        se.copy(pre32[:, 512 * fc:512 * (fc + 1)], ps[:, :])
                        se.activation(sq_sb[:, 512 * fc:512 * (fc + 1)], ps[:, :], AF.Square)
                        inst = se.copy(val_bf[:, 512 * fc:512 * (fc + 1)], ps[:, :])
                    else:
                        ctx.seq(se, "ea", se.activation(pre32[:, 512 * fc:512 * (fc + 1)],
                                                        ps[:, :], AF.Silu))
                        inst = se.activation(sq_sb[:, 512 * fc:512 * (fc + 1)],
                                             pre32[:, 512 * fc:512 * (fc + 1)], AF.Square)
                    ctx.inc(inst, "ea", OPS - (1 if mode != "qkln" else 0))

            @block.tensor
            def _(te):
                for fc in range(NC):
                    te.wait_ge(ctx.sem("ea"), b_ea + OPS * (fc + 1))
                    inst = te.matmul(ps_ss[:, :], ones2_sb[:, H * fc:H * (fc + 1)],
                                     sq_sb[:, 512 * fc:512 * (fc + 1)],
                                     start=(fc == 0), stop=(fc == NC - 1))
                    if need_mean:
                        inst = te.matmul(ps_mb[0:H, :], ones2_sb[:, H * fc:H * (fc + 1)],
                                         val_bf[:, 512 * fc:512 * (fc + 1)],
                                         start=(fc == 0), stop=(fc == NC - 1))
                ctx.inc(inst, "mm")

            @block.vector
            def _(ve):
                ve.wait_ge(ctx.sem("mm"), b_mm + NC + 1)
                if need_mean:
                    ctx.seq(ve, "vv", ve.tensor_scalar_mul(st16[:, 0:512], ps_mb[0:H, :], 1.0 / HS))
                    ctx.seq(ve, "vv", ve.tensor_mul(st16[:, 1024:1536], st16[:, 0:512], st16[:, 0:512]))
                    ctx.seq(ve, "vv", ve.scalar_tensor_tensor(st16[:, 512:1024], ps_ss[:, :], 1.0 / HS,
                                                              st16[:, 1024:1536], ALU.mult, ALU.subtract))
                else:
                    ctx.seq(ve, "vv", ve.tensor_copy(st16[:, 512:1024], ps_ss[:, :]))
                inst = ve.tensor_scalar_add(st16[:, 512:1024], st16[:, 512:1024], eps)
                ctx.inc(inst, "ev")

            @block.scalar
            def _(se):
                se.wait_ge(ctx.sem("ev"), b_ev + 1)
                inst = se.activation(st16[:, 512:1024], st16[:, 512:1024], AF.Sqrt)
                ctx.inc(inst, "ea")

            @block.vector
            def _(ve):
                ve.wait_ge(ctx.sem("ea"), b_ea + OPS * NC + 1)
                ctx.seq(ve, "vv", ve.reciprocal(st16[:, 512:1024], st16[:, 512:1024]))
                inst = ve.tensor_copy(st16b[:, :], st16[:, 0:1024])
                ctx.inc(inst, "ev")

            for fc in range(NC):
                c_mm = ctx.v("mm")
                c_ea = ctx.v("ea")
                c_ev = ctx.v("ev")

                @block.tensor
                def _(te, fc=fc, c_ev=c_ev):
                    te.wait_ge(ctx.sem("ev"), b_ev + 2)
                    if fc > 0:
                        te.wait_ge(ctx.sem("ev"), c_ev)
                    if need_mean:
                        te.matmul(ps_mb[:, :], sel_sb[:, 128 * fc:128 * (fc + 1)],
                                  st16b[:, 0:512], start=True, stop=True)
                    inst = te.matmul(ps_rb[:, :], sel_sb[:, 128 * fc:128 * (fc + 1)],
                                     st16b[:, 512:1024], start=True, stop=True)
                    ctx.inc(inst, "mm")

                @block.scalar
                def _(se, c_mm=c_mm):
                    se.wait_ge(ctx.sem("mm"), c_mm + 1)
                    inst = se.copy(rb_sb[:, :], ps_rb[:, :])
                    ctx.inc(inst, "ea")

                @block.vector
                def _(ve, fc=fc, c_mm=c_mm, c_ea=c_ea):
                    ve.wait_ge(ctx.sem("ea"), c_ea + 1)
                    if need_mean:
                        ctx.seq(ve, "vv", ve.tensor_sub(t1_sb[:, :],
                                                        pre32[:, 512 * fc:512 * (fc + 1)], ps_mb[:, :]))
                        inst = ve.tensor_mul(t1_sb[:, :], t1_sb[:, :], rb_sb[:, :])
                    else:
                        inst = ve.tensor_mul(t1_sb[:, :], pre32[:, 512 * fc:512 * (fc + 1)],
                                             rb_sb[:, :])
                    ctx.inc(inst, "ev")

                @block.scalar
                def _(se, fc=fc, c_ev=c_ev):
                    se.wait_ge(ctx.sem("ev"), c_ev + 1)
                    if mode == "qkln":
                        inst = se.activation(out_bf[:, 512 * fc:512 * (fc + 1)], t1_sb[:, :],
                                             AF.Identity, bias=bcol[:, 0:1], scale=gcol[:, 0:1])
                    else:
                        inst = se.copy(out_bf[:, 512 * fc:512 * (fc + 1)], t1_sb[:, :])
                    ctx.inc(inst, "ea")

            last_ea = ctx.v("ea")

            @block.sync
            def _(sy):
                sy.wait_ge(ctx.sem("ea"), last_ea)
                for fc in range(NC):
                    inst = sy.dma_start(out=out_ap[128 * fc:128 * (fc + 1), t0:t1],
                                        in_=out_bf[:, 512 * fc:512 * (fc + 1)])
                    ctx.inc(inst, "st", 16)


def emit_vproj(ctx, bufs, w_sb, v_tm, act=None):
    nc = ctx.nc
    (x_sb, v_bf, ps_a, ps_b) = bufs
    for ts in range(NS):
        b_mm = ctx.v("mm")
        b_ea = ctx.v("ea")
        b_st = ctx.v("st")
        with nc.Block() as block:
            @block.tensor
            def _(te, ts=ts):
                te.wait_ge(ctx.sem("ea"), b_ea)
                for fh in range(2):
                    ps = ps_a if fh == 0 else ps_b
                    inst = mm_accum(te, ps[:, :],
                                    [(x_sb[:, T * kc + 128 * ts:T * kc + 128 * (ts + 1)],
                                      w_sb[:, D * kc + 512 * fh:D * kc + 512 * (fh + 1)])
                                     for kc in range(NC)])
                    ctx.inc(inst, "mm")

            @block.scalar
            def _(se):
                se.wait_ge(ctx.sem("st"), b_st)
                for fh in range(2):
                    ps = ps_a if fh == 0 else ps_b
                    se.wait_ge(ctx.sem("mm"), b_mm + fh + 1)
                    if act is None:
                        inst = se.copy(v_bf[:, 512 * fh:512 * (fh + 1)], ps[:, :])
                    else:
                        inst = se.activation(v_bf[:, 512 * fh:512 * (fh + 1)], ps[:, :], act)
                    ctx.inc(inst, "ea")

            last_ea = ctx.v("ea")

            @block.sync
            def _(sy, ts=ts):
                sy.wait_ge(ctx.sem("ea"), last_ea)
                dst = bass.AP(v_tm.tensor, v_tm.offset + 128 * ts * HS,
                              [[HS, 128], [T * HS, H], [1, HS]])
                src = bass.AP(v_bf, 0, [[1024, 128], [HS, H], [1, HS]])
                inst = sy.dma_start(out=dst, in_=src)
                ctx.inc(inst, "st", 16)


def emit_p1(ctx, stack, io):
    nc = ctx.nc
    ctx.cur_stack = stack

    xT = io["xT"]
    WqT = io["WqT"]
    WkT = io["WkT"]
    WvT = io["WvT"]
    ones2_in = io["C_ones2"]
    sel_in = io["C_sel"]
    qg2 = io["qg2"]
    qb2 = io["qb2"]
    kg2 = io["kg2"]
    kb2 = io["kb2"]

    q_fm = io["q_fm"]
    k_fm = io["k_fm"]
    v_tm = io["v_tm"]

    x_sb = stack.enter_context(nc.sbuf_tensor("x_sb", [128, NC * T], BF16))
    wq_sb = stack.enter_context(nc.sbuf_tensor("wq_sb", [128, NC * D], BF16))
    wk_sb = stack.enter_context(nc.sbuf_tensor("wk_sb", [128, NC * D], BF16))
    wv_sb = stack.enter_context(nc.sbuf_tensor("wv_sb", [128, NC * D], BF16))
    ones2_sb = stack.enter_context(nc.sbuf_tensor("ones2_sb", [128, NC * H], BF16))
    sel_sb = stack.enter_context(nc.sbuf_tensor("sel_sb", [H, D], BF16))
    qg2_sb = stack.enter_context(nc.sbuf_tensor("qg2_sb", [128, 1], F32))
    qb2_sb = stack.enter_context(nc.sbuf_tensor("qb2_sb", [128, 1], F32))
    kg2_sb = stack.enter_context(nc.sbuf_tensor("kg2_sb", [128, 1], F32))
    kb2_sb = stack.enter_context(nc.sbuf_tensor("kb2_sb", [128, 1], F32))

    pre32 = stack.enter_context(nc.sbuf_tensor("pre32", [128, NC * 512], F32))
    sq_sb = stack.enter_context(nc.sbuf_tensor("sq_sb", [128, NC * 512], BF16))
    val_bf = stack.enter_context(nc.sbuf_tensor("val_bf", [128, NC * 512], BF16))
    st16 = stack.enter_context(nc.sbuf_tensor("st16", [H, 3 * 512], F32))
    st16b = stack.enter_context(nc.sbuf_tensor("st16b", [H, 2 * 512], BF16))
    rb_sb = stack.enter_context(nc.sbuf_tensor("rb_sb", [128, 512], F32))
    t1_sb = stack.enter_context(nc.sbuf_tensor("t1_sb", [128, 512], F32))
    out_bf = stack.enter_context(nc.sbuf_tensor("out_bf", [128, NC * 512], BF16))
    v_bf = stack.enter_context(nc.sbuf_tensor("v_bf", [128, 2 * 512], BF16))

    ps_a = stack.enter_context(nc.psum_tensor("ps_a", [128, 512], F32))
    ps_b = stack.enter_context(nc.psum_tensor("ps_b", [128, 512], F32))
    ps_ss = stack.enter_context(nc.psum_tensor("ps_ss", [H, 512], F32))
    ps_mb = stack.enter_context(nc.psum_tensor("ps_mb", [128, 512], F32))
    ps_rb = stack.enter_context(nc.psum_tensor("ps_rb", [128, 512], F32))

    if True:
        with nc.Block() as block:
            @block.gpsimd
            def _(gp):
                for i in range(NC):
                    ctx.inc(gp.dma_start(out=x_sb[:, T * i:T * (i + 1)], in_=xT[128 * i:128 * (i + 1), :]), "ld", 16)
                    ctx.inc(gp.dma_start(out=wq_sb[:, D * i:D * (i + 1)], in_=WqT[128 * i:128 * (i + 1), :]), "ld", 16)
                    ctx.inc(gp.dma_start(out=wk_sb[:, D * i:D * (i + 1)], in_=WkT[128 * i:128 * (i + 1), :]), "ld", 16)
                    ctx.inc(gp.dma_start(out=wv_sb[:, D * i:D * (i + 1)], in_=WvT[128 * i:128 * (i + 1), :]), "ld", 16)
                    ctx.inc(gp.dma_start(out=ones2_sb[:, H * i:H * (i + 1)],
                                         in_=ones2_in[128 * i:128 * (i + 1), :]), "ld", 16)
                ctx.inc(gp.dma_start(out=sel_sb[:, :], in_=sel_in), "ld", 16)
                ctx.inc(gp.dma_start(out=qg2_sb[:, :], in_=qg2), "ld", 16)
                ctx.inc(gp.dma_start(out=qb2_sb[:, :], in_=qb2), "ld", 16)
                ctx.inc(gp.dma_start(out=kg2_sb[:, :], in_=kg2), "ld", 16)
                ctx.inc(gp.dma_start(out=kb2_sb[:, :], in_=kb2), "ld", 16)

        ld_done = ctx.v("ld")
        with nc.Block() as block:
            @block.tensor
            def _(te):
                te.wait_ge(ctx.sem("ld"), ld_done)

        bufs = (x_sb, ones2_sb, sel_sb, pre32, sq_sb, val_bf, st16, st16b,
                rb_sb, t1_sb, out_bf, ps_a, ps_b, ps_ss, ps_mb, ps_rb)
        emit_proj_headnorm(ctx, bufs, wq_sb, q_fm, "qkln", qg2_sb, qb2_sb, eps=1e-5)
        emit_proj_headnorm(ctx, bufs, wk_sb, k_fm, "qkln", kg2_sb, kb2_sb, eps=1e-5)
        emit_vproj(ctx, (x_sb, v_bf, ps_a, ps_b), wv_sb, v_tm)

        st_all = ctx.v("st")
        with nc.Block() as block:
            @block.sync
            def _(sy):
                sy.wait_ge(ctx.sem("st"), st_all)

    return



def emit_m0_prep(ctx, row, negM0_col, ps_small, ones_bf, scale2):
    nc = ctx.nc
    b_ev = ctx.v("ev")
    b_ea = ctx.v("ea")
    b_mm = ctx.v("mm")
    with nc.Block() as block:
        @block.scalar
        def _(se):
            inst = se.activation(row[0:1, 256:512], row[0:1, 0:256], AF.Square)
            ctx.inc(inst, "ea")

        @block.vector
        def _(ve):
            ve.wait_ge(ctx.sem("ea"), b_ea + 1)
            ve.reduce_max(row[0:1, 512:513], row[0:1, 0:64], axis=AX.X, apply_absolute_value=True)
            ve.reduce_max(row[0:1, 514:515], row[0:1, 128:192], axis=AX.X, apply_absolute_value=True)
            ve.reduce_sum(row[0:1, 513:514], row[0:1, 320:384], axis=AX.X)
            inst = ve.reduce_sum(row[0:1, 515:516], row[0:1, 448:512], axis=AX.X)
            ctx.inc(inst, "ev")

        @block.scalar
        def _(se):
            se.wait_ge(ctx.sem("ev"), b_ev + 1)
            inst = se.activation(row[0:1, 513:514], row[0:1, 513:514], AF.Sqrt)
            inst = se.activation(row[0:1, 515:516], row[0:1, 515:516], AF.Sqrt)
            ctx.inc(inst, "ea")

        @block.vector
        def _(ve):
            ve.wait_ge(ctx.sem("ea"), b_ea + 2)
            ctx.seq(ve, "vv", ve.scalar_tensor_tensor(row[0:1, 516:517], row[0:1, 512:513], 8.0,
                                                      row[0:1, 513:514], ALU.mult, ALU.add))
            ctx.seq(ve, "vv", ve.scalar_tensor_tensor(row[0:1, 517:518], row[0:1, 514:515], 8.0,
                                                      row[0:1, 515:516], ALU.mult, ALU.add))
            ctx.seq(ve, "vv", ve.tensor_mul(row[0:1, 518:519], row[0:1, 516:517], row[0:1, 517:518]))
            ctx.seq(ve, "vv", ve.tensor_scalar_mul(row[0:1, 518:519], row[0:1, 518:519], scale2))
            inst = ve.tensor_copy(ones_bf[0:1, 64:65], row[0:1, 518:519])
            ctx.inc(inst, "ev")

        @block.tensor
        def _(te):
            te.wait_ge(ctx.sem("ev"), b_ev + 2)
            inst = te.matmul(ps_small[:, 0:1], ones_bf[0:1, 0:64],
                             ones_bf[0:1, 64:65], start=True, stop=True)
            ctx.inc(inst, "mm")

        @block.scalar
        def _(se):
            se.wait_ge(ctx.sem("mm"), b_mm + 1)
            se.activation(negM0_col[0:64, 0:1], ps_small[0:64, 0:1], AF.Identity, scale=-1.0)
            inst = se.activation(negM0_col[64:128, 0:1], ps_small[0:64, 0:1],
                                 AF.Identity, scale=-1.0)
            ctx.inc(inst, "ea")


def emit_flash_head(ctx, bufs, h, par, q_fm, k_fm, v_tm, attn_out, load_wait_mm):
    nc = ctx.nc
    (q_sb, k_sb, vb_sb, e_bf, l_sb, rhl, rb_sb, ao_sb, stair_sb, negM0_col,
     ps_sA, ps_sB, ps_O, ps_rb) = bufs
    q_s = q_sb[64 * par:64 * (par + 1), :] if False else q_sb

    with nc.Block() as block:
        @block.gpsimd
        def _(gp):
            gp.wait_ge(ctx.sem("mm"), load_wait_mm)
            ctx.inc(gp.dma_start(out=q_sb[:, :], in_=q_fm[64 * h:64 * (h + 1), :]), "ld", 16)
            ctx.inc(gp.dma_start(out=k_sb[:, :], in_=k_fm[64 * h:64 * (h + 1), :]), "ld", 16)
            src = bass.AP(v_tm.tensor, v_tm.offset + h * T * HS,
                          [[HS, 128], [128 * HS, NS], [1, HS]])
            dst = bass.AP(vb_sb, 0, [[65 * NS, 128], [65, NS], [1, HS]])
            ctx.inc(gp.dma_start(out=dst, in_=src), "ld", 16)

    ld_done = ctx.v("ld")

    for qt in range(NT):
        nkt = 4 * qt + 4
        b_mm = ctx.v("mm")
        b_ea = ctx.v("ea")
        b_ev = ctx.v("ev")
        b_st = ctx.v("st")

        plan_te = []
        for kt in range(nkt):
            plan_te.append(("S", kt))
            if kt >= 1:
                plan_te.append(("AV", kt - 1))
            if kt == nkt - 1:
                plan_te.append(("AV", kt))
        mm_S = {}
        mm_AV = {}
        for i, (kind, kt) in enumerate(plan_te):
            (mm_S if kind == "S" else mm_AV)[kt] = b_mm + i + 1
        ea_exp = {kt: b_ea + kt + 1 for kt in range(nkt)}
        ev_msk = {kt: b_ev + (kt - 4 * qt) + 1 for kt in range(4 * qt, nkt)}

        with nc.Block() as block:
            @block.tensor
            def _(te, qt=qt, nkt=nkt, plan_te=plan_te):
                te.wait_ge(ctx.sem("ld"), ld_done)
                te.wait_ge(ctx.sem("ea"), b_ea)
                te.wait_ge(ctx.sem("ev"), b_ev)
                for kind, kt in plan_te:
                    if kind == "S":
                        ps = ps_sA if kt % 2 == 0 else ps_sB
                        if kt >= 2:
                            te.wait_ge(ctx.sem("ea"), ea_exp[kt - 2])
                        inst = te.matmul(ps[:, :], k_sb[:, 128 * kt:128 * (kt + 1)],
                                         q_sb[:, 512 * qt:512 * (qt + 1)],
                                         start=True, stop=True)
                    else:
                        kt_ = kt
                        te.wait_ge(ctx.sem("ea"), ea_exp[kt_])
                        if kt_ in ev_msk:
                            te.wait_ge(ctx.sem("ev"), ev_msk[kt_])
                        inst = te.matmul(ps_O[:, :], vb_sb[:, 65 * kt_:65 * (kt_ + 1)],
                                         e_bf[:, 512 * (kt_ % 2):512 * (kt_ % 2 + 1)],
                                         start=(kt_ == 0), stop=(kt_ == nkt - 1))
                    ctx.inc(inst, "mm")

            @block.scalar
            def _(se, qt=qt, nkt=nkt):
                for kt in range(nkt):
                    se.wait_ge(ctx.sem("mm"), mm_S[kt])
                    if kt >= 2:
                        se.wait_ge(ctx.sem("mm"), mm_AV[kt - 2])
                    inst = se.activation(e_bf[:, 512 * (kt % 2):512 * (kt % 2 + 1)],
                                         (ps_sA if kt % 2 == 0 else ps_sB)[:, :],
                                         AF.Exp, bias=negM0_col[:, 0:1])
                    ctx.inc(inst, "ea")
                assert ctx.v("ea") == ea_exp[nkt - 1]

            @block.vector
            def _(ve, qt=qt, nkt=nkt):
                for kt in range(4 * qt, nkt):
                    j = kt - 4 * qt
                    ve.wait_ge(ctx.sem("ea"), ea_exp[kt])
                    sl = e_bf[:, 512 * (kt % 2):512 * (kt % 2 + 1)]
                    inst = ve.tensor_mul(sl, sl, stair_sb[:, 384 - 128 * j:896 - 128 * j])
                    ctx.inc(inst, "ev")
                assert ctx.v("ev") == ev_msk[nkt - 1]

            e_mm = ctx.v("mm")

            @block.scalar
            def _(se):
                se.wait_ge(ctx.sem("mm"), e_mm)
                inst = se.copy(l_sb[0:1, 0:512], ps_O[64:65, :])
                ctx.inc(inst, "ea")

            c_ea = ctx.v("ea")

            @block.vector
            def _(ve):
                ve.wait_ge(ctx.sem("ea"), c_ea)
                ctx.seq(ve, "vv", ve.tensor_scalar_add(l_sb[0:1, 0:512], l_sb[0:1, 0:512], 1e-30))
                ctx.seq(ve, "vv", ve.reciprocal(l_sb[0:1, 512:1024], l_sb[0:1, 0:512]))
                ctx.seq(ve, "vv", ve.tensor_copy(rhl[0:1, 0:512], l_sb[0:1, 512:1024]))
                inst = ve.tensor_sub(rhl[0:1, 512:1024], l_sb[0:1, 512:1024], rhl[0:1, 0:512])
                ctx.inc(inst, "ev")

            c_ev = ctx.v("ev")

            @block.tensor
            def _(te):
                te.wait_ge(ctx.sem("ev"), c_ev)
                te.wait_ge(ctx.sem("ea"), c_ea + 0)
                te.matmul(ps_rb[:, :], rhl[0:1, 1024:1088], rhl[0:1, 0:512],
                          start=True, stop=False)
                inst = te.matmul(ps_rb[:, :], rhl[0:1, 1024:1088], rhl[0:1, 512:1024],
                                 start=False, stop=True)
                ctx.inc(inst, "mm")

            c_mm = ctx.v("mm")

            @block.scalar
            def _(se):
                se.wait_ge(ctx.sem("mm"), c_mm)
                se.wait_ge(ctx.sem("st"), b_st)
                inst = se.copy(rb_sb[:, :], ps_rb[:, :])
                ctx.inc(inst, "ea")

            c_ea2 = ctx.v("ea")

            @block.vector
            def _(ve):
                ve.wait_ge(ctx.sem("ea"), c_ea2)
                inst = ve.tensor_mul(ao_sb[:, :], ps_O[0:64, :], rb_sb[:, :])
                ctx.inc(inst, "ev")

            c_ev2 = ctx.v("ev")

            @block.sync
            def _(sy, qt=qt, h=h):
                sy.wait_ge(ctx.sem("ev"), c_ev2)
                inst = sy.dma_start(out=attn_out[64 * h:64 * (h + 1), 512 * qt:512 * (qt + 1)],
                                    in_=ao_sb[:, :])
                ctx.inc(inst, "st", 16)


def emit_l0_attention(ctx, q_fm, k_fm, v_tm, gbrows, stair_in, attn_out):
    nc = ctx.nc
    q_sb = [ctx.cur_stack.enter_context(nc.sbuf_tensor(f"fa_q{i}", [64, T], BF16)) for i in range(2)]
    k_sb = [ctx.cur_stack.enter_context(nc.sbuf_tensor(f"fa_k{i}", [64, T], BF16)) for i in range(2)]
    vb_sb = [ctx.cur_stack.enter_context(nc.sbuf_tensor(f"fa_v{i}", [128, 65 * NS], BF16)) for i in range(2)]
    e_bf = ctx.cur_stack.enter_context(nc.sbuf_tensor("fa_e", [128, 2 * 512], BF16))
    l_sb = ctx.cur_stack.enter_context(nc.sbuf_tensor("fa_l", [1, 1024], F32))
    rhl = ctx.cur_stack.enter_context(nc.sbuf_tensor("fa_rhl", [1, 1088], BF16))
    rb_sb = ctx.cur_stack.enter_context(nc.sbuf_tensor("fa_rb", [64, 512], F32))
    ao_sb = ctx.cur_stack.enter_context(nc.sbuf_tensor("fa_ao", [64, 512], BF16))
    stair_sb = ctx.cur_stack.enter_context(nc.sbuf_tensor("fa_stair", [128, 896], BF16))
    negM0_col = ctx.cur_stack.enter_context(nc.sbuf_tensor("fa_negM0", [128, 1], F32))
    row = ctx.cur_stack.enter_context(nc.sbuf_tensor("fa_m0row", [1, 1024], F32))
    ones_bf = ctx.cur_stack.enter_context(nc.sbuf_tensor("fa_ones", [1, 65], BF16))

    ps_sA = ctx.cur_stack.enter_context(nc.psum_tensor("fa_psA", [128, 512], F32))
    ps_sB = ctx.cur_stack.enter_context(nc.psum_tensor("fa_psB", [128, 512], F32))
    ps_O = ctx.cur_stack.enter_context(nc.psum_tensor("fa_psO", [65, 512], F32))
    ps_rb = ctx.cur_stack.enter_context(nc.psum_tensor("fa_psrb", [64, 512], F32))
    ps_small = ctx.cur_stack.enter_context(nc.psum_tensor("fa_pssm", [64, 1], F32))

    with nc.Block() as block:
        @block.gpsimd
        def _(gp):
            ctx.inc(gp.dma_start(out=row[0:1, 0:256], in_=gbrows), "ld", 16)
            ctx.inc(gp.dma_start(out=stair_sb[:, :], in_=stair_in), "ld", 16)
            gp.memset(vb_sb[0][:, :], 1.0)
            gp.memset(vb_sb[1][:, :], 1.0)
            gp.memset(rhl[0:1, 1024:1088], 1.0)
            inst = gp.memset(ones_bf[:, :], 1.0)
            ctx.inc(inst, "gpc")

    ldv = ctx.v("ld")
    gpv = ctx.v("gpc")
    with nc.Block() as block:
        @block.scalar
        def _(se):
            se.wait_ge(ctx.sem("ld"), ldv)
            se.wait_ge(ctx.sem("gpc"), gpv)
        @block.tensor
        def _(te):
            te.wait_ge(ctx.sem("ld"), ldv)
            te.wait_ge(ctx.sem("gpc"), gpv)

    emit_m0_prep(ctx, row, negM0_col, ps_small, ones_bf, float(D) ** (-0.5))

    head_mm_start = []
    for h in range(H):
        par = h % 2
        bufs = (q_sb[par], k_sb[par], vb_sb[par], e_bf, l_sb, rhl, rb_sb, ao_sb,
                stair_sb, negM0_col, ps_sA, ps_sB, ps_O, ps_rb)
        load_wait = head_mm_start[h - 1] if h >= 1 else 0
        head_mm_start.append(ctx.v("mm"))
        emit_flash_head(ctx, bufs, h, par, q_fm, k_fm, v_tm, attn_out, load_wait)


def emit_p2(ctx, stack, io):
    nc = ctx.nc
    ctx.cur_stack = stack

    q_fm = io["q_fm"]
    k_fm = io["k_fm"]
    v_tm = io["v_tm"]
    gbrows = io["gbrows"]
    stair_in = io["C_stair"]
    attn_out = io["attn_out"]

    if True:
        emit_l0_attention(ctx, q_fm, k_fm, v_tm, gbrows, stair_in, attn_out)
        with nc.Block() as block:
            @block.sync
            def _(sy):
                sy.wait_ge(ctx.sem("st"), ctx.v("st"))
    return



class PostLN:
    def __init__(self, ctx, name, n_add=1, bias_pc=None, eps=1e-5, scale_rows=None,
                 with_producer=True, ps_share=None, sb_share=None):
        nc = ctx.nc
        _alloc = ctx.cur_stack.enter_context
        self.ctx = ctx
        self.name = name
        self.n_add = n_add
        self.bias_pc = bias_pc
        self.eps = eps
        self.with_producer = with_producer
        self.scale_rows = scale_rows
        sb_share = sb_share or {}
        self.res_sb = [sb_share.get(f"res{i}") or
                       _alloc(nc.sbuf_tensor(f"{name}_res{i}", [128, NC * 512], F32))
                       for i in range(n_add)]
        self.s32 = sb_share.get("s32") or _alloc(nc.sbuf_tensor(f"{name}_s32", [128, NC * 512], F32))
        sb_share0 = sb_share or {}
        self.sbf = sb_share0.get("sbf") or _alloc(nc.sbuf_tensor(f"{name}_sbf", [128, 512], BF16))
        self.sqbf = sb_share0.get("sqbf") or _alloc(nc.sbuf_tensor(f"{name}_sqbf", [128, 512], BF16))
        self.row = _alloc(nc.sbuf_tensor(f"{name}_row", [1, (4 if scale_rows is not None else 3) * 512], F32))
        self.rowb = sb_share.get("rowb") or _alloc(nc.sbuf_tensor(f"{name}_rowb", [1, 4 * 512], BF16))
        self.ones128 = _alloc(nc.sbuf_tensor(f"{name}_ones", [128, 1], BF16))
        self.onesrow = _alloc(nc.sbuf_tensor(f"{name}_onesrow", [1, 128], BF16))
        self.rb32 = sb_share.get("rb32") or _alloc(nc.sbuf_tensor(f"{name}_rb32", [128, 512], F32))
        self.sb_share_ = sb_share = sb_share or {}
        self.xbf = sb_share.get("xbf") or _alloc(nc.sbuf_tensor(f"{name}_xbf", [128, NC * 512], BF16))
        self.srow = (sb_share.get("srow") or _alloc(nc.sbuf_tensor(f"{name}_srow", [128, 512], F32))) if scale_rows is not None else None
        self.debug_sum_ap = None
        self.debug_row_ap = None
        if ps_share is None:
            ps_share = {}
        if with_producer:
            self.ps_x = ps_share.get("ps_x") or [
                _alloc(nc.psum_tensor(f"{name}_psx{i}", [128, 512], F32)) for i in range(2)]
        else:
            self.ps_x = None
        self.ps_st = ps_share.get("ps_st") or _alloc(nc.psum_tensor(f"{name}_psst", [1, 1024], F32))
        self.ps_mb = ps_share.get("ps_mb") or _alloc(nc.psum_tensor(f"{name}_psmb", [128, 512], F32))
        self.ps_rb = ps_share.get("ps_rb") or _alloc(nc.psum_tensor(f"{name}_psrb", [128, 512], F32))

    def emit_init(self):
        ctx = self.ctx
        with ctx.nc.Block() as block:
            @block.gpsimd
            def _(gp):
                gp.memset(self.ones128[:, :], 1.0)
                inst = gp.memset(self.onesrow[:, :], 1.0)
                ctx.inc(inst, "gpc")

    def emit(self, producer, rhs_loader, add_srcs, out32_ap, outbf_ap, bias_pc_sb=None,
             pre_tile=None, extra_gp_wait=None, scale_pc_sb=None, tts=None):
        ctx = self.ctx
        nc = ctx.nc
        gpc_v = ctx.v("gpc")
        for tt in (range(NT) if tts is None else tts):
            if pre_tile is not None:
                pre_tile(tt)
            t0, t1 = 512 * tt, 512 * (tt + 1)
            b_mm = ctx.v("mm")
            b_ea = ctx.v("ea")
            b_ev = ctx.v("ev")
            b_st = ctx.v("st")
            b_ld = ctx.v("ldx")
            ld_mid = [None]

            with nc.Block() as block:
                @block.gpsimd
                def _(gp, tt=tt):
                    gp.wait_ge(ctx.sem("ev"), b_ev)
                    if extra_gp_wait is not None:
                        nm, val = extra_gp_wait()
                        gp.wait_ge(ctx.sem(nm), val)
                    for i, src in enumerate(add_srcs):
                        for fc in range(NC):
                            inst = gp.dma_start(out=self.res_sb[i][:, 512 * fc:512 * (fc + 1)],
                                                in_=src[128 * fc:128 * (fc + 1), t0:t1])
                            ctx.inc(inst, "ldx", 16)
                    if self.scale_rows is not None:
                        inst = gp.dma_start(out=self.row[0:1, 1536:2048],
                                            in_=self.scale_rows[0:1, t0:t1])
                        ctx.inc(inst, "ldx", 16)
                    ld_mid[0] = ctx.v("ldx")
                    rhs_loader(gp, tt)

                ld_v = ld_mid[0]

                if self.scale_rows is not None:
                    @block.vector
                    def _(ve, tt=tt):
                        ve.wait_ge(ctx.sem("ldx"), ld_v)
                        inst = ve.tensor_copy(self.rowb[0:1, 1536:2048], self.row[0:1, 1536:2048])
                        ctx.inc(inst, "ev")

                    @block.tensor
                    def _(te, tt=tt):
                        te.wait_ge(ctx.sem("ev"), ctx.v("ev"))
                        te.wait_ge(ctx.sem("gpc"), gpc_v)
                        te.wait_ge(ctx.sem("ea"), b_ea)
                        inst = te.matmul(self.ps_rb[:, :], self.onesrow[0:1, :],
                                         self.rowb[0:1, 1536:2048], start=True, stop=True)
                        ctx.inc(inst, "mm")

                    @block.scalar
                    def _(se, tt=tt):
                        se.wait_ge(ctx.sem("mm"), ctx.v("mm"))
                        inst = se.copy(self.srow[:, :], self.ps_rb[:, :])
                        ctx.inc(inst, "ea")

                b2_mm = ctx.v("mm")
                b2_ea = ctx.v("ea")

                if self.with_producer:
                    @block.tensor
                    def _(te, tt=tt):
                        te.wait_ge(ctx.sem("ldx"), ld_v)
                        te.wait_ge(ctx.sem("gpc"), gpc_v)
                        te.wait_ge(ctx.sem("ea"), b_ea)
                        for fc in range(NC):
                            ps = self.ps_x[fc % 2]
                            if fc >= 2:
                                te.wait_ge(ctx.sem("ea"), b2_ea + (fc - 1))
                            inst = producer(te, fc, tt, ps)
                            ctx.inc(inst, "mm")

                    @block.scalar
                    def _(se, tt=tt):
                        se.wait_ge(ctx.sem("st"), b_st)
                        se.wait_ge(ctx.sem("ev"), b_ev)
                        for fc in range(NC):
                            ps = self.ps_x[fc % 2]
                            se.wait_ge(ctx.sem("mm"), b2_mm + fc + 1)
                            if bias_pc_sb is not None:
                                inst = se.activation(self.s32[:, 512 * fc:512 * (fc + 1)], ps[:, :],
                                                     AF.Identity, bias=bias_pc_sb[:, fc:fc + 1])
                            elif scale_pc_sb is not None:
                                inst = se.activation(self.s32[:, 512 * fc:512 * (fc + 1)], ps[:, :],
                                                     AF.Identity, scale=scale_pc_sb[:, fc:fc + 1])
                            else:
                                inst = se.copy(self.s32[:, 512 * fc:512 * (fc + 1)], ps[:, :])
                            ctx.inc(inst, "ea", 1)

                @block.vector
                def _(ve, tt=tt):
                    stats_mm0 = b2_mm + (NC if self.with_producer else 0)
                    ve.wait_ge(ctx.sem("ldx"), ld_v)
                    for fc in range(NC):
                        if self.with_producer:
                            ve.wait_ge(ctx.sem("ea"), b2_ea + fc + 1)
                        if fc >= 1:
                            ve.wait_ge(ctx.sem("mm"), stats_mm0 + fc)
                        sl = self.s32[:, 512 * fc:512 * (fc + 1)]
                        if self.scale_rows is not None:
                            ctx.seq(ve, "vv", ve.tensor_mul(sl, sl, self.srow[:, :]))
                        for i in range(len(add_srcs)):
                            ctx.seq(ve, "vv", ve.tensor_add(sl, sl,
                                                            self.res_sb[i][:, 512 * fc:512 * (fc + 1)]))
                        inst = ve.tensor_copy(self.sbf[:, :], sl)
                        ctx.inc(inst, "ev", 1)
                        inst = ve.tensor_mul(self.sqbf[:, :], sl, sl)
                        ctx.inc(inst, "ev", 1)
                    pass

                @block.tensor
                def _(te, tt=tt):
                    for fc in range(NC):
                        te.wait_ge(ctx.sem("ev"), b_ev + (1 if self.scale_rows is not None else 0) + 2 * (fc + 1))
                        te.matmul(self.ps_st[0:1, 0:512], self.ones128[:, 0:1],
                                  self.sbf[:, :], start=(fc == 0), stop=(fc == NC - 1))
                        inst = te.matmul(self.ps_st[0:1, 512:1024], self.ones128[:, 0:1],
                                         self.sqbf[:, :], start=(fc == 0), stop=(fc == NC - 1))
                        ctx.inc(inst, "mm")

                c_mm = ctx.v("mm")

                if self.debug_sum_ap is not None:
                    dbg_ev = b_ev + (1 if self.scale_rows is not None else 0) + 2 * NC

                    @block.sync
                    def _(sy, tt=tt):
                        sy.wait_ge(ctx.sem("ev"), dbg_ev)
                        for fc in range(NC):
                            inst = sy.dma_start(
                                out=self.debug_sum_ap[128 * fc:128 * (fc + 1), t0:t1],
                                in_=self.s32[:, 512 * fc:512 * (fc + 1)])
                            ctx.inc(inst, "st", 16)
                dbg_st = ctx.v("st")

                @block.vector
                def _(ve, tt=tt):
                    ve.wait_ge(ctx.sem("mm"), c_mm)
                    if self.debug_sum_ap is not None:
                        ve.wait_ge(ctx.sem("st"), dbg_st)
                    ctx.seq(ve, "vv", ve.tensor_scalar_mul(self.row[0:1, 0:512],
                                                           self.ps_st[0:1, 0:512], 1.0 / D))
                    ctx.seq(ve, "vv", ve.tensor_mul(self.row[0:1, 512:1024], self.row[0:1, 0:512],
                                                    self.row[0:1, 0:512]))
                    ctx.seq(ve, "vv", ve.scalar_tensor_tensor(self.row[0:1, 512:1024],
                                                              self.ps_st[0:1, 512:1024],
                                                              1.0 / D, self.row[0:1, 512:1024],
                                                              ALU.mult, ALU.subtract))
                    inst = ve.tensor_scalar_add(self.row[0:1, 512:1024],
                                                self.row[0:1, 512:1024], self.eps)
                    ctx.inc(inst, "ev")

                c_ev = ctx.v("ev")

                @block.scalar
                def _(se, tt=tt):
                    se.wait_ge(ctx.sem("ev"), c_ev)
                    inst = se.activation(self.row[0:1, 512:1024], self.row[0:1, 512:1024], AF.Sqrt)
                    ctx.inc(inst, "ea")

                c_ea = ctx.v("ea")

                @block.vector
                def _(ve, tt=tt):
                    ve.wait_ge(ctx.sem("ea"), c_ea)
                    ctx.seq(ve, "vv", ve.reciprocal(self.row[0:1, 1024:1536],
                                                    self.row[0:1, 512:1024]))
                    ctx.seq(ve, "vv", ve.tensor_copy(self.rowb[0:1, 0:512], self.row[0:1, 0:512]))
                    ctx.seq(ve, "vv", ve.tensor_sub(self.rowb[0:1, 512:1024], self.row[0:1, 0:512],
                                                    self.rowb[0:1, 0:512]))
                    ctx.seq(ve, "vv", ve.tensor_copy(self.rowb[0:1, 1024:1536],
                                                     self.row[0:1, 1024:1536]))
                    inst = ve.tensor_sub(self.rowb[0:1, 1536:2048], self.row[0:1, 1024:1536],
                                         self.rowb[0:1, 1024:1536])
                    ctx.inc(inst, "ev")

                c_ev2 = ctx.v("ev")

                if self.debug_row_ap is not None:
                    @block.sync
                    def _(sy, tt=tt):
                        sy.wait_ge(ctx.sem("ev"), c_ev2)
                        inst = sy.dma_start(out=self.debug_row_ap[tt:tt + 1, 0:1536],
                                            in_=self.row[0:1, 0:1536])
                        ctx.inc(inst, "st", 16)

                @block.tensor
                def _(te, tt=tt):
                    te.wait_ge(ctx.sem("ev"), c_ev2)
                    te.wait_ge(ctx.sem("ea"), b2_ea + (NC if self.with_producer else 0))
                    te.matmul(self.ps_mb[:, :], self.onesrow[0:1, :], self.rowb[0:1, 0:512],
                              start=True, stop=False)
                    te.matmul(self.ps_mb[:, :], self.onesrow[0:1, :], self.rowb[0:1, 512:1024],
                              start=False, stop=True)
                    te.matmul(self.ps_rb[:, :], self.onesrow[0:1, :], self.rowb[0:1, 1024:1536],
                              start=True, stop=False)
                    inst = te.matmul(self.ps_rb[:, :], self.onesrow[0:1, :],
                                     self.rowb[0:1, 1536:2048], start=False, stop=True)
                    ctx.inc(inst, "mm")

                c_mm2 = ctx.v("mm")

                @block.scalar
                def _(se, tt=tt):
                    se.wait_ge(ctx.sem("mm"), c_mm2)
                    inst = se.copy(self.rb32[:, :], self.ps_rb[:, :])
                    ctx.inc(inst, "ea")

                c_ea2 = ctx.v("ea")

                @block.vector
                def _(ve, tt=tt):
                    ve.wait_ge(ctx.sem("ea"), c_ea2)
                    for fc in range(NC):
                        sl = self.s32[:, 512 * fc:512 * (fc + 1)]
                        ctx.seq(ve, "vv", ve.tensor_sub(sl, sl, self.ps_mb[:, :]))
                        if outbf_ap is not None:
                            ctx.seq(ve, "vv", ve.tensor_mul(sl, sl, self.rb32[:, :]))
                            inst = ve.tensor_copy(self.xbf[:, 512 * fc:512 * (fc + 1)], sl)
                        else:
                            inst = ve.tensor_mul(sl, sl, self.rb32[:, :])
                        ctx.inc(inst, "ev", 1)

                c_ev3 = ctx.v("ev")

                @block.sync
                def _(sy, tt=tt):
                    sy.wait_ge(ctx.sem("ev"), c_ev3)
                    for fc in range(NC):
                        if out32_ap is not None:
                            inst = sy.dma_start(out=out32_ap[128 * fc:128 * (fc + 1), t0:t1],
                                                in_=self.s32[:, 512 * fc:512 * (fc + 1)])
                            ctx.inc(inst, "st", 16)
                        if outbf_ap is not None:
                            inst = sy.dma_start(out=outbf_ap[128 * fc:128 * (fc + 1), t0:t1],
                                                in_=self.xbf[:, 512 * fc:512 * (fc + 1)])
                            ctx.inc(inst, "st", 16)



def emit_p3(ctx, stack, io):
    nc = ctx.nc
    ctx.cur_stack = stack

    attn_out = io["attn_out"]
    WoT = io["WoT"]
    bo_pc = io["bo_pc"]
    xT = io["xT"]
    x1_32 = io["x1_32"]
    x1_bf = io["x1_bf"]

    wo_sb = stack.enter_context(nc.sbuf_tensor("wo_sb", [128, NC * D], BF16))
    bo_sb = stack.enter_context(nc.sbuf_tensor("bo_sb", [128, NC], F32))
    ao_sb = stack.enter_context(nc.sbuf_tensor("ao_sb", [128, NC * 512], BF16))

    if True:
        with nc.Block() as block:
            @block.gpsimd
            def _(gp):
                for i in range(NC):
                    ctx.inc(gp.dma_start(out=wo_sb[:, D * i:D * (i + 1)],
                                         in_=WoT[128 * i:128 * (i + 1), :]), "ld", 16)
                ctx.inc(gp.dma_start(out=bo_sb[:, :], in_=bo_pc), "ld", 16)

        ldv = ctx.v("ld")
        pl = PostLN(ctx, "p3", n_add=1)
        import os
        if os.environ.get("P3DEBUG"):
            pl.debug_sum_ap = io["dbg_sum"]
            pl.debug_row_ap = io["dbg_row"]
        pl.emit_init()

        p3_state = {}

        def producer(te, fc, tt, ps):
            te.wait_ge(ctx.sem("ld"), ldv)
            te.wait_ge(ctx.sem("ldx"), p3_state["ld"])
            return mm_accum(te, ps[:, :],
                            [(wo_sb[:, D * ec + 128 * fc:D * ec + 128 * (fc + 1)],
                              ao_sb[:, 512 * ec:512 * (ec + 1)]) for ec in range(NC)])

        def rhs_loader(gp, tt):
            for ec in range(NC):
                ctx.inc(gp.dma_start(out=ao_sb[:, 512 * ec:512 * (ec + 1)],
                                     in_=attn_out[128 * ec:128 * (ec + 1),
                                                  512 * tt:512 * (tt + 1)]), "ldx", 16)
            p3_state["ld"] = ctx.v("ldx")

        pl.emit(producer, rhs_loader, [xT], x1_32, x1_bf, bias_pc_sb=bo_sb)

        with nc.Block() as block:
            @block.sync
            def _(sy):
                sy.wait_ge(ctx.sem("st"), ctx.v("st"))
    return



def emit_p4a(ctx, stack, io):
    nc = ctx.nc
    ctx.cur_stack = stack

    slot_flat = io["slot_flat"]
    gs_row = io["gs_row"]
    gx_pc = io["gx_pc"]
    x1_bf = io["x1_bf"]

    se_tm = io["se_tm"]
    se_fm = io["se_fm"]
    xn_fm = io["xn_fm"]
    xn_tm = io["xn_tm"]
    m1_out = io["m1_out"]

    grow_sb = stack.enter_context(nc.sbuf_tensor("grow_sb", [1, D], F32))
    growb_sb = stack.enter_context(nc.sbuf_tensor("growb_sb", [1, D], BF16))
    onesrow = stack.enter_context(nc.sbuf_tensor("p4_onesrow", [1, 128], BF16))
    ones128 = stack.enter_context(nc.sbuf_tensor("p4_ones128", [128, 1], BF16))
    gsl32 = stack.enter_context(nc.sbuf_tensor("gsl32", [128, D], F32))
    gx_sb = stack.enter_context(nc.sbuf_tensor("gx_sb", [128, NC], F32))
    eye_sb = stack.enter_context(nc.sbuf_tensor("p4_eye", [128, 128], BF16))

    tile32 = stack.enter_context(nc.sbuf_tensor("tile32", [128, D], F32))
    sq32 = stack.enter_context(nc.sbuf_tensor("sq32", [128, D], F32))
    col_sb = stack.enter_context(nc.sbuf_tensor("col_sb", [128, 4], F32))
    sebf_sb = stack.enter_context(nc.sbuf_tensor("sebf_sb", [128, D], BF16))

    x1t_sb = stack.enter_context(nc.sbuf_tensor("x1t_sb", [128, NC * 512], BF16))
    sqbf_sb = stack.enter_context(nc.sbuf_tensor("p4_sqbf", [128, 512], BF16))
    row_sb = stack.enter_context(nc.sbuf_tensor("p4_row", [1, 4 * 512], F32))
    rowb_sb = stack.enter_context(nc.sbuf_tensor("p4_rowb", [1, 2 * 512], BF16))
    rb32 = stack.enter_context(nc.sbuf_tensor("p4_rb32", [128, 512], F32))
    t1_sb = stack.enter_context(nc.sbuf_tensor("p4_t1", [128, 512], F32))
    xnbf_sb = stack.enter_context(nc.sbuf_tensor("xnbf_sb", [128, NC * 512], BF16))

    sef_sb = [stack.enter_context(nc.sbuf_tensor(f"sef{i}", [128, T], BF16)) for i in range(NC)]
    xnf_sb = stack.enter_context(nc.sbuf_tensor("xnf_sb", [128, NC * 512], BF16))
    trans_sb = stack.enter_context(nc.sbuf_tensor("trans_sb", [128, 1024], BF16))
    m1cols = stack.enter_context(nc.sbuf_tensor("m1cols", [128, 32], F32))
    m1colb = stack.enter_context(nc.sbuf_tensor("m1colb", [128, 32], BF16))
    m1T_sb = stack.enter_context(nc.sbuf_tensor("m1T_sb", [32, 128], F32))
    mx_sb = stack.enter_context(nc.sbuf_tensor("mx_sb", [128, 1], F32))

    ps_a = stack.enter_context(nc.psum_tensor("p4_psa", [128, 512], F32))
    ps_b = stack.enter_context(nc.psum_tensor("p4_psb", [128, 512], F32))
    ps_g = stack.enter_context(nc.psum_tensor("p4_psg", [128, 1024], F32))
    ps_st = stack.enter_context(nc.psum_tensor("p4_psst", [1, 512], F32))
    ps_tr = stack.enter_context(nc.psum_tensor("p4_pstr", [32, 128], BF16))

    EPS = 1e-12
    RT = 32.0

    if True:
        with nc.Block() as block:
            @block.gpsimd
            def _(gp):
                ctx.inc(gp.dma_start(out=grow_sb[:, :], in_=gs_row), "ld", 16)
                ctx.inc(gp.dma_start(out=gx_sb[:, :], in_=gx_pc), "ld", 16)
                gp.memset(onesrow[:, :], 1.0)
                gp.memset(ones128[:, :], 1.0)
                gp.memset(eye_sb[:, :], 0.0)
                inst = gp.memset(m1cols[:, :], -3.0e38)
                ctx.inc(inst, "gpc")

        eye_in = io["C_eye"]
        with nc.Block() as block:
            @block.gpsimd
            def _(gp):
                ctx.inc(gp.dma_start(out=eye_sb[:, :], in_=eye_in), "ld", 16)

        ldv = ctx.v("ld")
        gpv = ctx.v("gpc")

        with nc.Block() as block:
            @block.vector
            def _(ve):
                ve.wait_ge(ctx.sem("ld"), ldv)
                inst = ve.tensor_copy(growb_sb[:, :], grow_sb[:, :])
                ctx.inc(inst, "ev")

            c_ev = ctx.v("ev")

            @block.tensor
            def _(te):
                te.wait_ge(ctx.sem("ev"), c_ev)
                te.wait_ge(ctx.sem("gpc"), gpv)
                te.matmul(ps_g[:, 0:512], onesrow[0:1, :], growb_sb[0:1, 0:512],
                          start=True, stop=True)
                inst = te.matmul(ps_g[:, 512:1024], onesrow[0:1, :], growb_sb[0:1, 512:1024],
                                 start=True, stop=True)
                ctx.inc(inst, "mm")

            c_mm = ctx.v("mm")

            @block.scalar
            def _(se):
                se.wait_ge(ctx.sem("mm"), c_mm)
                inst = se.copy(gsl32[:, :], ps_g[:, :])
                ctx.inc(inst, "ea")

        for ts in range(NS):
            b_ld = ctx.v("ldx")
            b_ea = ctx.v("ea")
            b_ev = ctx.v("ev")
            b_st = ctx.v("st")
            with nc.Block() as block:
                @block.gpsimd
                def _(gp, ts=ts):
                    gp.wait_ge(ctx.sem("ev"), b_ev)
                    inst = gp.dma_start(out=tile32[:, :],
                                        in_=slot_flat[128 * ts:128 * (ts + 1), :])
                    ctx.inc(inst, "ldx", 16)

                ld_v = ctx.v("ldx")

                @block.scalar
                def _(se, ts=ts):
                    se.wait_ge(ctx.sem("ldx"), ld_v)
                    inst = se.activation(sq32[:, :], tile32[:, :], AF.Square)
                    ctx.inc(inst, "ea")

                @block.vector
                def _(ve, ts=ts):
                    ve.wait_ge(ctx.sem("ea"), b_ea + 1)
                    inst = ve.reduce_sum(col_sb[:, 0:1], sq32[:, :], axis=AX.X)
                    ctx.inc(inst, "ev")

                @block.scalar
                def _(se, ts=ts):
                    se.wait_ge(ctx.sem("ev"), b_ev + 1)
                    inst = se.activation(col_sb[:, 1:2], col_sb[:, 0:1], AF.Sqrt)
                    ctx.inc(inst, "ea")

                @block.vector
                def _(ve, ts=ts):
                    ve.wait_ge(ctx.sem("ea"), b_ea + 2)
                    ctx.seq(ve, "vv", ve.tensor_scalar_max(col_sb[:, 1:2], col_sb[:, 1:2], EPS))
                    ctx.seq(ve, "vv", ve.reciprocal(col_sb[:, 2:3], col_sb[:, 1:2]))
                    inst = ve.tensor_scalar_mul(col_sb[:, 2:3], col_sb[:, 2:3], RT)
                    ctx.inc(inst, "ev")

                @block.scalar
                def _(se, ts=ts):
                    se.wait_ge(ctx.sem("ev"), b_ev + 2)
                    inst = se.activation(tile32[:, :], tile32[:, :], AF.Identity,
                                         scale=col_sb[:, 2:3])
                    ctx.inc(inst, "ea")

                @block.vector
                def _(ve, ts=ts):
                    ve.wait_ge(ctx.sem("ea"), b_ea + 3)
                    ve.wait_ge(ctx.sem("st"), b_st)
                    inst = ve.tensor_mul(sebf_sb[:, :], tile32[:, :], gsl32[:, :])
                    ctx.inc(inst, "ev")

                c_ev2 = ctx.v("ev")

                @block.sync
                def _(sy, ts=ts):
                    sy.wait_ge(ctx.sem("ev"), c_ev2)
                    inst = sy.dma_start(out=se_tm[128 * ts:128 * (ts + 1), :], in_=sebf_sb[:, :])
                    ctx.inc(inst, "st", 16)

        st_se = ctx.v("st")
        for dc in range(NC):
            with nc.Block() as block:
                @block.sync
                def _(sy, dc=dc):
                    if dc == 0:
                        sy.wait_ge(ctx.sem("st"), st_se)
                    inst = sy.dma_start_transpose(sef_sb[dc][:, :],
                                                  se_tm[:, 128 * dc:128 * (dc + 1)])
                    ctx.inc(inst, "ldx", 16)

                ld_v = ctx.v("ldx")

                @block.sync
                def _(sy, dc=dc):
                    sy.wait_ge(ctx.sem("ldx"), ld_v)
                    inst = sy.dma_start(out=se_fm[128 * dc:128 * (dc + 1), :], in_=sef_sb[dc][:, :])
                    ctx.inc(inst, "st", 16)

        for tt in range(NT):
            t0, t1 = 512 * tt, 512 * (tt + 1)
            b_mm = ctx.v("mm")
            b_ea = ctx.v("ea")
            b_ev = ctx.v("ev")
            b_st = ctx.v("st")
            with nc.Block() as block:
                @block.gpsimd
                def _(gp, tt=tt):
                    gp.wait_ge(ctx.sem("mm"), b_mm)
                    gp.wait_ge(ctx.sem("ev"), b_ev)
                    for fc in range(NC):
                        inst = gp.dma_start(out=x1t_sb[:, 512 * fc:512 * (fc + 1)],
                                            in_=x1_bf[128 * fc:128 * (fc + 1), t0:t1])
                        ctx.inc(inst, "ldx", 16)

                ld_v = ctx.v("ldx")

                @block.scalar
                def _(se, tt=tt):
                    se.wait_ge(ctx.sem("ldx"), ld_v)
                    for fc in range(NC):
                        if fc >= 1:
                            se.wait_ge(ctx.sem("mm"), b_mm + fc)
                        inst = se.activation(sqbf_sb[:, :], x1t_sb[:, 512 * fc:512 * (fc + 1)],
                                             AF.Square)
                        ctx.inc(inst, "ea")

                @block.tensor
                def _(te, tt=tt):
                    for fc in range(NC):
                        te.wait_ge(ctx.sem("ea"), b_ea + fc + 1)
                        inst = te.matmul(ps_st[0:1, :], ones128[:, 0:1], sqbf_sb[:, :],
                                         start=(fc == 0), stop=(fc == NC - 1))
                        ctx.inc(inst, "mm")

                c_mm = ctx.v("mm")

                @block.scalar
                def _(se, tt=tt):
                    se.wait_ge(ctx.sem("mm"), c_mm)
                    inst = se.activation(row_sb[0:1, 0:512], ps_st[0:1, :], AF.Sqrt)
                    ctx.inc(inst, "ea")

                c_ea = ctx.v("ea")

                @block.vector
                def _(ve, tt=tt):
                    ve.wait_ge(ctx.sem("ea"), c_ea)
                    ctx.seq(ve, "vv", ve.tensor_scalar_max(row_sb[0:1, 0:512], row_sb[0:1, 0:512], EPS))
                    ctx.seq(ve, "vv", ve.reciprocal(row_sb[0:1, 512:1024], row_sb[0:1, 0:512]))
                    ctx.seq(ve, "vv", ve.tensor_scalar_mul(row_sb[0:1, 512:1024],
                                                           row_sb[0:1, 512:1024], RT))
                    ctx.seq(ve, "vv", ve.tensor_copy(rowb_sb[0:1, 0:512], row_sb[0:1, 512:1024]))
                    inst = ve.tensor_sub(rowb_sb[0:1, 512:1024], row_sb[0:1, 512:1024],
                                         rowb_sb[0:1, 0:512])
                    ctx.inc(inst, "ev")

                c_ev = ctx.v("ev")

                @block.tensor
                def _(te, tt=tt):
                    te.wait_ge(ctx.sem("ev"), c_ev)
                    te.matmul(ps_a[:, :], onesrow[0:1, :], rowb_sb[0:1, 0:512],
                              start=True, stop=False)
                    inst = te.matmul(ps_a[:, :], onesrow[0:1, :], rowb_sb[0:1, 512:1024],
                                     start=False, stop=True)
                    ctx.inc(inst, "mm")

                c_mm2 = ctx.v("mm")

                @block.scalar
                def _(se, tt=tt):
                    se.wait_ge(ctx.sem("mm"), c_mm2)
                    inst = se.copy(rb32[:, :], ps_a[:, :])
                    ctx.inc(inst, "ea")

                c_ea2 = ctx.v("ea")

                @block.vector
                def _(ve, tt=tt):
                    for fc in range(NC):
                        ve.wait_ge(ctx.sem("ea"), c_ea2 + fc)
                        inst = ve.tensor_mul(t1_sb[:, :], x1t_sb[:, 512 * fc:512 * (fc + 1)],
                                             rb32[:, :])
                        ctx.inc(inst, "ev")

                @block.scalar
                def _(se, tt=tt):
                    for fc in range(NC):
                        se.wait_ge(ctx.sem("ev"), c_ev + fc + 1)
                        inst = se.activation(xnbf_sb[:, 512 * fc:512 * (fc + 1)], t1_sb[:, :],
                                             AF.Identity, scale=gx_sb[:, fc:fc + 1])
                        ctx.inc(inst, "ea")

                last_ea = ctx.v("ea")

                @block.sync
                def _(sy, tt=tt):
                    sy.wait_ge(ctx.sem("ea"), last_ea)
                    for fc in range(NC):
                        inst = sy.dma_start(out=xn_fm[128 * fc:128 * (fc + 1), t0:t1],
                                            in_=xnbf_sb[:, 512 * fc:512 * (fc + 1)])
                        ctx.inc(inst, "st", 16)

        st_xn = ctx.v("st")
        for ts in range(NS):
            b_st_ts = ctx.v("st")
            with nc.Block() as block:
                @block.sync
                def _(sy, ts=ts):
                    sy.wait_ge(ctx.sem("st"), b_st_ts if ts > 0 else st_xn)
                    inst = sy.dma_start_transpose(trans_sb[:, :],
                                                  xn_fm[:, 128 * ts:128 * (ts + 1)])
                    ctx.inc(inst, "ldx", 16)

                ld_v = ctx.v("ldx")

                @block.sync
                def _(sy, ts=ts):
                    sy.wait_ge(ctx.sem("ldx"), ld_v)
                    inst = sy.dma_start(out=xn_tm[128 * ts:128 * (ts + 1), :], in_=trans_sb[:, :])
                    ctx.inc(inst, "st", 16)

        for sg in range(NT):
            t0, t1 = 512 * sg, 512 * (sg + 1)
            b_mm = ctx.v("mm")
            b_ev = ctx.v("ev")
            with nc.Block() as block:
                @block.gpsimd
                def _(gp, sg=sg):
                    gp.wait_ge(ctx.sem("mm"), b_mm)
                    for dc in range(NC):
                        inst = gp.dma_start(out=xnf_sb[:, 512 * dc:512 * (dc + 1)],
                                            in_=xn_fm[128 * dc:128 * (dc + 1), t0:t1])
                        ctx.inc(inst, "ldx", 16)

                ld_v = ctx.v("ldx")

                @block.tensor
                def _(te, sg=sg):
                    te.wait_ge(ctx.sem("ldx"), ld_v)
                    for nc_i in range(32):
                        ps = ps_a if nc_i % 2 == 0 else ps_b
                        if nc_i >= 2:
                            te.wait_ge(ctx.sem("ev"), b_ev + nc_i - 1)
                        inst = mm_accum(te, ps[:, :],
                                        [(sef_sb[dc][:, 128 * nc_i:128 * (nc_i + 1)],
                                          xnf_sb[:, 512 * dc:512 * (dc + 1)]) for dc in range(NC)])
                        ctx.inc(inst, "mm")

                @block.vector
                def _(ve, sg=sg):
                    ve.wait_ge(ctx.sem("gpc"), gpv)
                    for nc_i in range(32):
                        ps = ps_a if nc_i % 2 == 0 else ps_b
                        ve.wait_ge(ctx.sem("mm"), b_mm + nc_i + 1)
                        ctx.seq(ve, "vv", ve.reduce_max(mx_sb[:, 0:1], ps[:, :], axis=AX.X))
                        inst = ve.tensor_max(m1cols[:, nc_i:nc_i + 1], m1cols[:, nc_i:nc_i + 1],
                                             mx_sb[:, 0:1])
                        ctx.inc(inst, "ev")

        b_ev = ctx.v("ev")
        with nc.Block() as block:
            @block.vector
            def _(ve):
                inst = ve.tensor_copy(m1colb[:, :], m1cols[:, :])
                ctx.inc(inst, "ev")

            @block.tensor
            def _(te):
                te.wait_ge(ctx.sem("ev"), b_ev + 1)
                inst = te.transpose(ps_tr[:, :], m1colb[:, :], eye_sb[:, :])
                ctx.inc(inst, "mm")

            c_mm = ctx.v("mm")

            @block.scalar
            def _(se):
                se.wait_ge(ctx.sem("mm"), c_mm)
                inst = se.copy(m1T_sb[:, :], ps_tr[:, :])
                ctx.inc(inst, "ea")

            c_ea = ctx.v("ea")

            @block.sync
            def _(sy):
                sy.wait_ge(ctx.sem("ea"), c_ea)
                inst = sy.dma_start(out=m1_out, in_=m1T_sb[:, :])
                ctx.inc(inst, "st", 16)

        with nc.Block() as block:
            @block.sync
            def _(sy):
                sy.wait_ge(ctx.sem("st"), ctx.v("st"))
    return



def emit_p4b(ctx, stack, io):
    nc = ctx.nc
    ctx.cur_stack = stack

    se_fm = io["se_fm"]
    xn_fm = io["xn_fm"]
    xn_tm = io["xn_tm"]
    m1_row = io["m1_row"]
    w1_in = io["w1"]
    w2_in = io["w2"]
    x1_32 = io["x1_32"]
    eye_in = io["C_eye"]

    x2_32 = io["x2_32"]
    x2_bf = io["x2_bf"]

    mx2_sc = io["mx2_sc"]
    rs2_sc = io["rs2_sc"]

    ptm_sb = stack.enter_context(nc.sbuf_tensor("ptm_sb", [128, 32 * D], BF16))
    ebuf = stack.enter_context(nc.sbuf_tensor("ebuf", [128, 32 * 512], BF16))
    zg_sb = stack.enter_context(nc.sbuf_tensor("zg_sb", [128, NC * 512], BF16))
    seg_sb = stack.enter_context(nc.sbuf_tensor("seg_sb", [128, NC * 512], BF16))
    xnbuf = [stack.enter_context(nc.sbuf_tensor(f"xnbuf{i}", [128, NC * 128], BF16)) for i in range(2)]
    xtm_one = stack.enter_context(nc.sbuf_tensor("xtm_one", [128, 32 * 128], BF16))
    xtm_sb = [xtm_one, xtm_one]
    wbuf = [stack.enter_context(nc.sbuf_tensor(f"wbuf{i}", [128, 1024], BF16)) for i in range(2)]
    lm32 = [stack.enter_context(nc.sbuf_tensor(f"lm32_{i}", [128, 512], F32)) for i in range(2)]
    m1b_sb = stack.enter_context(nc.sbuf_tensor("m1b_sb", [128, 512], F32))
    rz_sb = stack.enter_context(nc.sbuf_tensor("rz_sb", [128, 512], F32))
    mx2cols = stack.enter_context(nc.sbuf_tensor("mx2cols", [128, 32], F32))
    mx_tmp = stack.enter_context(nc.sbuf_tensor("mx_tmp", [128, 1], F32))
    mx2b = stack.enter_context(nc.sbuf_tensor("mx2b", [128, 32], BF16))
    mx2T = stack.enter_context(nc.sbuf_tensor("mx2T", [32, 128], F32))
    m1r_sb = stack.enter_context(nc.sbuf_tensor("m1r_sb", [1, T], BF16))
    row4 = stack.enter_context(nc.sbuf_tensor("row4", [1, 2 * 512], F32))
    rowb4 = stack.enter_context(nc.sbuf_tensor("rowb4", [1, 2 * 512], BF16))
    onesrow = stack.enter_context(nc.sbuf_tensor("m_onesrow", [1, 128], BF16))
    ones128 = stack.enter_context(nc.sbuf_tensor("m_ones128", [128, 1], BF16))
    eye_sb = stack.enter_context(nc.sbuf_tensor("m_eye", [128, 128], BF16))
    xnf_sb = stack.enter_context(nc.sbuf_tensor("m_xnf", [128, NC * 512], BF16))

    psA = stack.enter_context(nc.psum_tensor("m_psA", [128, 512], F32))
    psB = stack.enter_context(nc.psum_tensor("m_psB", [128, 512], F32))
    psCD = stack.enter_context(nc.psum_tensor("m_psCD", [128, 1024], F32))
    ps_row = stack.enter_context(nc.psum_tensor("m_psrow", [1, 512], F32))
    ps_rb = stack.enter_context(nc.psum_tensor("m_psrb", [128, 512], F32))
    ps_tr = stack.enter_context(nc.psum_tensor("m_pstr", [32, 128], BF16))
    ps_mb = stack.enter_context(nc.psum_tensor("m_psmb", [128, 512], F32))

    if True:
        with nc.Block() as block:
            @block.gpsimd
            def _(gp):
                ctx.inc(gp.dma_start(out=m1r_sb[:, :], in_=m1_row), "ld", 16)
                ctx.inc(gp.dma_start(out=eye_sb[:, :], in_=eye_in), "ld", 16)
                gp.memset(onesrow[:, :], 1.0)
                gp.memset(ones128[:, :], 1.0)
                inst = gp.memset(mx2cols[:, :], -3.0e38)
                ctx.inc(inst, "gpc")

        ld0 = ctx.v("ld")
        gp0 = ctx.v("gpc")

        for g in range(NT):
            n0 = 512 * g
            b_mm = ctx.v("mm")
            b_ev = ctx.v("ev")
            b_ea = ctx.v("ea")
            with nc.Block() as block:
                @block.gpsimd
                def _(gp, g=g):
                    gp.wait_ge(ctx.sem("mm"), b_mm)
                    for dc in range(NC):
                        inst = gp.dma_start(out=seg_sb[:, 512 * dc:512 * (dc + 1)],
                                            in_=se_fm[128 * dc:128 * (dc + 1), n0:n0 + 512])
                        ctx.inc(inst, "ldx", 16)

                ld_se = ctx.v("ldx")

                @block.vector
                def _(ve, g=g):
                    ve.wait_ge(ctx.sem("ld"), ld0)
                    ve.wait_ge(ctx.sem("ea"), b_ea)
                    ve.wait_ge(ctx.sem("ev"), b_ev)
                    inst = ve.tensor_copy(rowb4[0:1, 0:512], m1r_sb[0:1, n0:n0 + 512])
                    ctx.inc(inst, "ev")

                @block.tensor
                def _(te, g=g):
                    te.wait_ge(ctx.sem("ev"), ctx.v("ev"))
                    te.wait_ge(ctx.sem("gpc"), gp0)
                    te.wait_ge(ctx.sem("ea"), b_ea)
                    inst = te.matmul(ps_rb[:, :], onesrow[0:1, :], rowb4[0:1, 0:512],
                                     start=True, stop=True)
                    ctx.inc(inst, "mm")

                @block.scalar
                def _(se, g=g):
                    se.wait_ge(ctx.sem("mm"), ctx.v("mm"))
                    inst = se.copy(m1b_sb[:, :], ps_rb[:, :])
                    ctx.inc(inst, "ea")

            b2_mm = ctx.v("mm")
            b2_ea = ctx.v("ea")
            b2_ev = ctx.v("ev")
            b2_ld = ctx.v("ldx")
            with nc.Block() as block:
                @block.gpsimd
                def _(gp, g=g):
                    for si in range(NS):
                        if si >= 2:
                            gp.wait_ge(ctx.sem("mm"), b2_mm + 2 * (si - 1))
                        for half in range(2):
                            inst = gp.dma_start(
                                out=bass.AP(xnbuf[si % 2], 512 * half,
                                            [[1024, 128], [128, 4], [1, 128]]),
                                in_=bass.AP(xn_fm.tensor, xn_fm.offset + 128 * si
                                            + half * 4 * T * 128,
                                            [[T, 128], [T * 128, 4], [1, 128]]))
                            ctx.inc(inst, "ldx", 16)

                @block.tensor
                def _(te, g=g):
                    te.wait_ge(ctx.sem("ldx"), ld_se)
                    te.wait_ge(ctx.sem("ea"), b2_ea)
                    for si in range(NS):
                        te.wait_ge(ctx.sem("ldx"), b2_ld + 32 * (si + 1))
                        ps = psA if si % 2 == 0 else psB
                        if si >= 2:
                            te.wait_ge(ctx.sem("ev"), b2_ev + 2 * (si - 1))
                        inst = mm_accum(te, ps[:, :],
                                        [(xnbuf[si % 2][:, 128 * dc:128 * (dc + 1)],
                                          seg_sb[:, 512 * dc:512 * (dc + 1)]) for dc in range(NC)])
                        ctx.inc(inst, "mm")
                        te.wait_ge(ctx.sem("ea"), b2_ea + si + 1)
                        inst = te.matmul(ps_row[0:1, :], ones128[:, 0:1],
                                         ebuf[:, 512 * si:512 * (si + 1)],
                                         start=(si == 0), stop=(si == NS - 1))
                        ctx.inc(inst, "mm")

                @block.vector
                def _(ve, g=g):
                    for si in range(NS):
                        ps = psA if si % 2 == 0 else psB
                        ve.wait_ge(ctx.sem("mm"), b2_mm + 2 * si + 1)
                        ctx.seq(ve, "vv", ve.reduce_max(mx_tmp[:, 0:1], ps[:, :], axis=AX.X))
                        ctx.seq(ve, "vv", ve.tensor_max(mx2cols[:, si:si + 1],
                                                        mx2cols[:, si:si + 1], mx_tmp[:, 0:1]))
                        inst = ve.tensor_sub(lm32[si % 2][:, :], ps[:, :], m1b_sb[:, :])
                        ctx.inc(inst, "ev", 2)

                @block.scalar
                def _(se, g=g):
                    for si in range(NS):
                        se.wait_ge(ctx.sem("ev"), b2_ev + 2 * (si + 1))
                        if si >= 2:
                            se.wait_ge(ctx.sem("mm"), b2_mm + 2 * (si - 1))
                        inst = se.activation(ebuf[:, 512 * si:512 * (si + 1)], lm32[si % 2][:, :],
                                             AF.Exp)
                        ctx.inc(inst, "ea")

            c_mm = ctx.v("mm")
            c_ea = ctx.v("ea")
            c_ev = ctx.v("ev")
            with nc.Block() as block:
                @block.scalar
                def _(se, g=g):
                    se.wait_ge(ctx.sem("mm"), c_mm)
                    inst = se.copy(row4[0:1, 0:512], ps_row[0:1, :])
                    ctx.inc(inst, "ea")

                @block.vector
                def _(ve, g=g):
                    ve.wait_ge(ctx.sem("ea"), c_ea + 1)
                    ctx.seq(ve, "vv", ve.reciprocal(row4[0:1, 512:1024], row4[0:1, 0:512]))
                    ctx.seq(ve, "vv", ve.tensor_copy(rowb4[0:1, 0:512], row4[0:1, 512:1024]))
                    inst = ve.tensor_sub(rowb4[0:1, 512:1024], row4[0:1, 512:1024],
                                         rowb4[0:1, 0:512])
                    ctx.inc(inst, "ev")

                @block.tensor
                def _(te, g=g):
                    te.wait_ge(ctx.sem("ev"), c_ev + 1)
                    te.wait_ge(ctx.sem("ea"), c_ea + 1)
                    te.matmul(ps_rb[:, :], onesrow[0:1, :], rowb4[0:1, 0:512],
                              start=True, stop=False)
                    inst = te.matmul(ps_rb[:, :], onesrow[0:1, :], rowb4[0:1, 512:1024],
                                     start=False, stop=True)
                    ctx.inc(inst, "mm")

                @block.scalar
                def _(se, g=g):
                    se.wait_ge(ctx.sem("mm"), c_mm + 1)
                    inst = se.copy(rz_sb[:, :], ps_rb[:, :])
                    ctx.inc(inst, "ea")

            d_mm = ctx.v("mm")
            d_ea = ctx.v("ea")
            d_ev = ctx.v("ev")
            d_ld = ctx.v("ldx")
            with nc.Block() as block:
                @block.gpsimd
                def _(gp, g=g):
                    for dc in range(NC):
                        if dc >= 1:
                            gp.wait_ge(ctx.sem("mm"), d_mm + dc)
                        for half in range(2):
                            inst = gp.dma_start(
                                out=bass.AP(xtm_sb[dc % 2], 2048 * half,
                                            [[4096, 128], [128, 16], [1, 128]]),
                                in_=bass.AP(xn_tm.tensor, xn_tm.offset + 128 * dc
                                            + half * 16 * D * 128,
                                            [[D, 128], [D * 128, 16], [1, 128]]))
                            ctx.inc(inst, "ldx", 16)

                @block.tensor
                def _(te, g=g):
                    for dc in range(NC):
                        te.wait_ge(ctx.sem("ldx"), d_ld + 32 * (dc + 1))
                        ps = psA if dc % 2 == 0 else psB
                        if dc >= 2:
                            te.wait_ge(ctx.sem("ev"), d_ev + (dc - 1))
                        inst = None
                        for si in range(NS):
                            inst = te.matmul(ps[:, :], xtm_sb[dc % 2][:, 128 * si:128 * (si + 1)],
                                             ebuf[:, 512 * si:512 * (si + 1)],
                                             start=(si == 0), stop=(si == NS - 1))
                        ctx.inc(inst, "mm")

                @block.vector
                def _(ve, g=g):
                    ve.wait_ge(ctx.sem("ea"), d_ea)
                    for dc in range(NC):
                        ps = psA if dc % 2 == 0 else psB
                        ve.wait_ge(ctx.sem("mm"), d_mm + dc + 1)
                        inst = ve.tensor_mul(zg_sb[:, 512 * dc:512 * (dc + 1)], ps[:, :],
                                             rz_sb[:, :])
                        ctx.inc(inst, "ev")

            e_mm = ctx.v("mm")
            e_ea = ctx.v("ea")
            e_ev = ctx.v("ev")
            e_ld = ctx.v("ldx")
            with nc.Block() as block:
                @block.gpsimd
                def _(gp, g=g):
                    for hc in range(NHC):
                        if hc >= 2:
                            gp.wait_ge(ctx.sem("mm"), e_mm + (hc - 1))
                        for half in range(2):
                            inst = gp.dma_start(
                                out=bass.AP(wbuf[hc % 2], 512 * half,
                                            [[1024, 128], [128, 4], [1, 128]]),
                                in_=bass.AP(w1_in.tensor, w1_in.offset + 128 * hc
                                            + half * 4 * HID * 128,
                                            [[HID, 128], [HID * 128, 4], [1, 128]]))
                            ctx.inc(inst, "ldx", 16)

                @block.tensor
                def _(te, g=g):
                    te.wait_ge(ctx.sem("ev"), e_ev)
                    for hc in range(NHC):
                        te.wait_ge(ctx.sem("ldx"), e_ld + 32 * (hc + 1))
                        ps = psA if hc % 2 == 0 else psB
                        if hc >= 2:
                            te.wait_ge(ctx.sem("ea"), e_ea + (hc - 1))
                        inst = mm_accum(te, ps[:, :],
                                        [(wbuf[hc % 2][:, 128 * dc:128 * (dc + 1)],
                                          zg_sb[:, 512 * dc:512 * (dc + 1)]) for dc in range(NC)])
                        ctx.inc(inst, "mm")

                @block.scalar
                def _(se, g=g):
                    for hc in range(NHC):
                        ps = psA if hc % 2 == 0 else psB
                        se.wait_ge(ctx.sem("mm"), e_mm + hc + 1)
                        inst = se.activation(ebuf[:, 512 * hc:512 * (hc + 1)], ps[:, :], AF.Gelu)
                        ctx.inc(inst, "ea")

            f_mm = ctx.v("mm")
            f_ea = ctx.v("ea")
            f_ld = ctx.v("ldx")
            with nc.Block() as block:
                @block.gpsimd
                def _(gp, g=g):
                    for half in range(2):
                        for hc in range(NHC):
                            k = NHC * half + hc
                            gp.wait_ge(ctx.sem("mm"), f_mm + (4 * (k - 1) if k >= 2 else 0))
                            inst = gp.dma_start(out=wbuf[hc % 2][:, :],
                                                in_=w2_in[128 * hc:128 * (hc + 1), :])
                            ctx.inc(inst, "ldx", 16)

                @block.tensor
                def _(te, g=g):
                    te.wait_ge(ctx.sem("ea"), f_ea)
                    for half in range(2):
                        if half == 1:
                            te.wait_ge(ctx.sem("ea"), f_ea + 4)
                        for hc in range(NHC):
                            te.wait_ge(ctx.sem("ldx"), f_ld + 16 * (NHC * half + hc + 1))
                            for nci2 in range(2):
                                nci = 2 * half + nci2
                                for dh in range(2):
                                    ps = (psA[:, :], psB[:, :], psCD[:, 0:512], psCD[:, 512:1024])[2 * nci2 + dh]
                                    inst = te.matmul(
                                        ps, ebuf[:, 512 * hc + 128 * nci:512 * hc + 128 * (nci + 1)],
                                        wbuf[hc % 2][:, 512 * dh:512 * (dh + 1)],
                                        start=(hc == 0), stop=(hc == NHC - 1))
                            ctx.inc(inst, "mm", 4)

                @block.scalar
                def _(se, g=g):
                    for half in range(2):
                        se.wait_ge(ctx.sem("mm"), f_mm + 4 * NHC * (half + 1))
                        for nci2 in range(2):
                            nci = 2 * half + nci2
                            nc2 = 4 * g + nci
                            for dh in range(2):
                                ps = (psA[:, :], psB[:, :], psCD[:, 0:512], psCD[:, 512:1024])[2 * nci2 + dh]
                                inst = se.copy(ptm_sb[:, 1024 * nc2 + 512 * dh:
                                                      1024 * nc2 + 512 * (dh + 1)], ps)
                                ctx.inc(inst, "ea")

        g_ev = ctx.v("ev")
        g_mm = ctx.v("mm")
        g_ea = ctx.v("ea")
        g_ld = ctx.v("ldx")
        with nc.Block() as block:
            @block.vector
            def _(ve):
                inst = ve.tensor_copy(mx2b[:, :], mx2cols[:, :])
                ctx.inc(inst, "ev")

            @block.tensor
            def _(te):
                te.wait_ge(ctx.sem("ev"), g_ev + 1)
                te.wait_ge(ctx.sem("ea"), g_ea)
                inst = te.transpose(ps_tr[:, :], mx2b[:, :], eye_sb[:, :])
                ctx.inc(inst, "mm")

            @block.scalar
            def _(se):
                se.wait_ge(ctx.sem("mm"), g_mm + 1)
                inst = se.copy(mx2T[:, :], ps_tr[:, :])
                ctx.inc(inst, "ea")

            @block.sync
            def _(sy):
                sy.wait_ge(ctx.sem("ea"), g_ea + 1)
                inst = sy.dma_start(out=mx2_sc, in_=mx2T[:, :])
                ctx.inc(inst, "st", 16)

            st_v = ctx.v("st")

            @block.gpsimd
            def _(gp):
                gp.wait_ge(ctx.sem("st"), st_v)
                inst = gp.dma_start(out=row4[0:1, 0:512], in_=bass.AP(mx2_sc.tensor, 0, [[1, 1], [1, 512]]))
                ctx.inc(inst, "ldx", 16)
                for c in range(8):
                    inst = gp.dma_start(out=m1r_sb[0:1, 512 * c:512 * (c + 1)],
                                        in_=bass.AP(mx2_sc.tensor, 512 * c, [[1, 1], [1, 512]]))
                    ctx.inc(inst, "ldx", 16)

            ld_v2 = ctx.v("ldx")

            @block.vector
            def _(ve):
                ve.wait_ge(ctx.sem("ldx"), ld_v2)
                inst = ve.tensor_scalar_mul(m1r_sb[0:1, :], m1r_sb[0:1, :], -1.0)
                ctx.inc(inst, "ev")

        pl = PostLN(ctx, "p4f", n_add=1, eps=1e-5, scale_rows=rs2_sc,
                    with_producer=True,
                    ps_share={"ps_x": [psA, psB], "ps_st": psCD, "ps_mb": ps_mb,
                              "ps_rb": ps_rb},
                    sb_share={"xbf": zg_sb, "srow": rz_sb})
        pl.emit_init()

        ec_state = {}

        def pre_tile(sg):
            t0 = 512 * sg
            h_mm = ctx.v("mm")
            h_ea = ctx.v("ea")
            h_ev = ctx.v("ev")
            h_ld = ctx.v("ldx")
            with nc.Block() as block:
                @block.gpsimd
                def _(gp, sg=sg):
                    gp.wait_ge(ctx.sem("mm"), h_mm)
                    for dc in range(NC):
                        inst = gp.dma_start(out=xnf_sb[:, 512 * dc:512 * (dc + 1)],
                                            in_=xn_fm[128 * dc:128 * (dc + 1), t0:t0 + 512])
                        ctx.inc(inst, "ldx", 16)

                ld_a = ctx.v("ldx")

                @block.gpsimd
                def _(gp, sg=sg):
                    for nc2 in range(32):
                        if nc2 >= 2:
                            gp.wait_ge(ctx.sem("mm"), h_mm + 2 * (nc2 - 1))
                        for half in range(2):
                            inst = gp.dma_start(
                                out=xnbuf[nc2 % 2][:, 512 * half:512 * (half + 1)],
                                in_=bass.AP(se_fm.tensor, se_fm.offset + 128 * nc2
                                            + half * 4 * T * 128,
                                            [[T, 128], [T * 128, 4], [1, 128]]))
                            ctx.inc(inst, "ldx", 16)

                @block.tensor
                def _(te, sg=sg):
                    te.wait_ge(ctx.sem("ldx"), ld_a)
                    te.wait_ge(ctx.sem("ea"), h_ea)
                    te.wait_ge(ctx.sem("ev"), h_ev)
                    for nc2 in range(32):
                        te.wait_ge(ctx.sem("ldx"), ld_a + 32 * (nc2 + 1))
                        ps = psA if nc2 % 2 == 0 else psB
                        if nc2 >= 2:
                            te.wait_ge(ctx.sem("ea"), h_ea + (nc2 - 1))
                        for dc in range(NC):
                            te.matmul(ps[:, :], xnbuf[nc2 % 2][:, 128 * dc:128 * (dc + 1)],
                                      xnf_sb[:, 512 * dc:512 * (dc + 1)],
                                      start=(dc == 0), stop=False)
                        inst = te.matmul(ps[:, :], onesrow[0:1, :], m1r_sb[0:1, t0:t0 + 512],
                                         start=False, stop=True)
                        ctx.inc(inst, "mm")
                        te.wait_ge(ctx.sem("ea"), h_ea + nc2 + 1)
                        inst = te.matmul(ps_row[0:1, :], ones128[:, 0:1],
                                         ebuf[:, 512 * nc2:512 * (nc2 + 1)],
                                         start=(nc2 == 0), stop=(nc2 == 31))
                        ctx.inc(inst, "mm")

                @block.scalar
                def _(se, sg=sg):
                    for nc2 in range(32):
                        ps = psA if nc2 % 2 == 0 else psB
                        se.wait_ge(ctx.sem("mm"), h_mm + 2 * nc2 + 1)
                        inst = se.activation(ebuf[:, 512 * nc2:512 * (nc2 + 1)], ps[:, :], AF.Exp)
                        ctx.inc(inst, "ea")

                i_mm = ctx.v("mm")
                i_ea = ctx.v("ea")
                i_ev = ctx.v("ev")

                @block.scalar
                def _(se, sg=sg):
                    se.wait_ge(ctx.sem("mm"), i_mm)
                    inst = se.copy(row4[0:1, 0:512], ps_row[0:1, :])
                    ctx.inc(inst, "ea")

                @block.vector
                def _(ve, sg=sg):
                    ve.wait_ge(ctx.sem("ea"), i_ea + 1)
                    inst = ve.reciprocal(row4[0:1, 512:1024], row4[0:1, 0:512])
                    ctx.inc(inst, "ev")

                @block.sync
                def _(sy, sg=sg):
                    sy.wait_ge(ctx.sem("ev"), i_ev + 1)
                    inst = sy.dma_start(out=rs2_sc[0:1, t0:t0 + 512], in_=row4[0:1, 512:1024])
                    ctx.inc(inst, "st", 16)

            ec_state["ea"] = ctx.v("ea")
            ec_state["st"] = ctx.v("st")

        def producer(te, fc, tt, ps):
            te.wait_ge(ctx.sem("ea"), ec_state["ea"])
            te.wait_ge(ctx.sem("st"), ec_state["st"])
            inst = None
            for nc2 in range(32):
                inst = te.matmul(ps[:, :], ptm_sb[:, 1024 * nc2 + 128 * fc:1024 * nc2 + 128 * (fc + 1)],
                                 ebuf[:, 512 * nc2:512 * (nc2 + 1)],
                                 start=(nc2 == 0), stop=(nc2 == 31))
            return inst

        def rhs_loader(gp, tt):
            pass

        pl.emit(producer, rhs_loader, [x1_32], x2_32, x2_bf, bias_pc_sb=None,
                pre_tile=pre_tile, extra_gp_wait=lambda: ("st", ec_state["st"]))

        with nc.Block() as block:
            @block.sync
            def _(sy):
                sy.wait_ge(ctx.sem("st"), ctx.v("st"))
    return


def emit_p6(ctx, stack, io):
    nc = ctx.nc
    ctx.cur_stack = stack

    x3_bf = io["x3_bf"]
    x3_32 = io["x3_32"]
    w1h = io["w1h"]
    w2h = io["w2h"]
    h1h = io["h1h"]
    h2h = io["h2h"]
    w3h = io["w3h"]
    s1_pc = io["s1_pc"]
    s2_pc = io["s2_pc"]
    sh1_pc = io["sh1_pc"]
    sh2_pc = io["sh2_pc"]
    s3_pc = io["s3_pc"]
    x4_32 = io["x4_32"]

    a_sb = stack.enter_context(nc.sbuf_tensor("p6_a", [128, NHC * 1024], BF16))
    b_sb = stack.enter_context(nc.sbuf_tensor("p6_b", [128, NHC * 1024], BF16))
    x3t_sb = stack.enter_context(nc.sbuf_tensor("p6_x3", [128, NC * 512], BF16))
    wA = [stack.enter_context(nc.sbuf_tensor(f"p6_wA{i}", [128, NC * 128], BF16)) for i in range(2)]
    wB = [stack.enter_context(nc.sbuf_tensor(f"p6_wB{i}", [128, NC * 128], BF16)) for i in range(2)]
    wH = [stack.enter_context(nc.sbuf_tensor(f"p6_wH{i}", [128, NHC * 128], BF16)) for i in range(2)]
    g1_sb = stack.enter_context(nc.sbuf_tensor("p6_g1", [128, 512], F32))
    scl = {}
    for nm in ("s1", "s2", "sh1", "sh2"):
        scl[nm] = stack.enter_context(nc.sbuf_tensor(f"p6_{nm}", [128, NHC], F32))
    scl["s3"] = stack.enter_context(nc.sbuf_tensor("p6_s3", [128, NC], F32))

    psA = stack.enter_context(nc.psum_tensor("p6_psA", [128, 512], F32))
    psB = stack.enter_context(nc.psum_tensor("p6_psB", [128, 512], F32))

    if True:
        with nc.Block() as block:
            @block.gpsimd
            def _(gp):
                for nm, ap in (("s1", s1_pc), ("s2", s2_pc), ("sh1", sh1_pc),
                               ("sh2", sh2_pc), ("s3", s3_pc)):
                    ctx.inc(gp.dma_start(out=scl[nm][:, :], in_=ap), "ld", 16)
        ld0 = ctx.v("ld")

        pl = PostLN(ctx, "p6", n_add=1, eps=1e-5,
                    ps_share={"ps_x": [psA, psB]}, sb_share={"xbf": b_sb, "sbf": wA[0][:, 0:512], "sqbf": wA[1][:, 0:512]})
        pl.emit_init()

        for tb in range(4):
            tb0 = 1024 * tb
            for tg in range(2):
                t0g = tb0 + 512 * tg
                b_mm = ctx.v("mm")
                with nc.Block() as block:
                    @block.gpsimd
                    def _(gp, t0g=t0g, b_mm=b_mm):
                        gp.wait_ge(ctx.sem("mm"), b_mm)
                        for dc in range(NC):
                            inst = gp.dma_start(out=x3t_sb[:, 512 * dc:512 * (dc + 1)],
                                                in_=x3_bf[128 * dc:128 * (dc + 1), t0g:t0g + 512])
                            ctx.inc(inst, "ldx", 16)
                ld_x3 = ctx.v("ldx")

                a_mm = ctx.v("mm")
                a_ea = ctx.v("ea")
                a_ev = ctx.v("ev")
                a_ld = ctx.v("ldx")
                with nc.Block() as block:
                    @block.gpsimd
                    def _(gp, a_mm=a_mm):
                        for hc in range(NHC):
                            if hc >= 2:
                                gp.wait_ge(ctx.sem("mm"), a_mm + 2 * (hc - 1))
                            ctx.inc(gp.dma_start(out=wA[hc % 2][:, :], in_=w1h[hc]), "ldx", 16)
                            ctx.inc(gp.dma_start(out=wB[hc % 2][:, :], in_=w2h[hc]), "ldx", 16)

                    @block.tensor
                    def _(te, a_ld=a_ld, a_ea=a_ea, a_ev=a_ev):
                        te.wait_ge(ctx.sem("ldx"), ld_x3)
                        te.wait_ge(ctx.sem("ld"), ld0)
                        for hc in range(NHC):
                            te.wait_ge(ctx.sem("ldx"), a_ld + 32 * (hc + 1))
                            te.wait_ge(ctx.sem("ea"), a_ea + hc)
                            te.wait_ge(ctx.sem("ev"), a_ev + hc)
                            for w_sb, ps in ((wA[hc % 2], psA), (wB[hc % 2], psB)):
                                inst = mm_accum(te, ps[:, :],
                                                [(w_sb[:, 128 * dc:128 * (dc + 1)],
                                                  x3t_sb[:, 512 * dc:512 * (dc + 1)])
                                                 for dc in range(NC)])
                                ctx.inc(inst, "mm")

                    @block.scalar
                    def _(se, a_mm=a_mm, a_ev=a_ev):
                        for hc in range(NHC):
                            se.wait_ge(ctx.sem("mm"), a_mm + 2 * hc + 1)
                            if hc >= 1:
                                se.wait_ge(ctx.sem("ev"), a_ev + hc)
                            inst = se.activation(g1_sb[:, 0:512], psA[:, :],
                                                 AF.Silu, scale=scl["s1"][:, hc:hc + 1])
                            ctx.inc(inst, "ea")

                    @block.vector
                    def _(ve, a_mm=a_mm, a_ea=a_ea, tg=tg):
                        for hc in range(NHC):
                            ve.wait_ge(ctx.sem("mm"), a_mm + 2 * hc + 2)
                            ve.wait_ge(ctx.sem("ea"), a_ea + hc + 1)
                            inst = ve.scalar_tensor_tensor(
                                a_sb[:, 1024 * hc + 512 * tg:1024 * hc + 512 * (tg + 1)],
                                psB[:, :], scl["s2"][:, hc:hc + 1],
                                g1_sb[:, 0:512], ALU.mult, ALU.mult)
                            ctx.inc(inst, "ev")

            for src_sb, dst_sb, wdram, snm in ((a_sb, b_sb, h1h, "sh1"),
                                               (b_sb, a_sb, h2h, "sh2")):
                c_mm = ctx.v("mm")
                c_ea = ctx.v("ea")
                c_ld = ctx.v("ldx")
                with nc.Block() as block:
                    @block.gpsimd
                    def _(gp, wdram=wdram, c_mm=c_mm):
                        for oc in range(NHC):
                            if oc >= 2:
                                gp.wait_ge(ctx.sem("mm"), c_mm + 2 * (oc - 1))
                            ctx.inc(gp.dma_start(out=wH[oc % 2][:, :], in_=wdram[oc]), "ldx", 16)

                    @block.tensor
                    def _(te, src_sb=src_sb, c_ld=c_ld, c_ea=c_ea):
                        for oc in range(NHC):
                            te.wait_ge(ctx.sem("ldx"), c_ld + 16 * (oc + 1))
                            if oc >= 1:
                                te.wait_ge(ctx.sem("ea"), c_ea + 2 * oc)
                            for tg in range(2):
                                ps = psA if tg == 0 else psB
                                inst = mm_accum(te, ps[:, :],
                                                [(wH[oc % 2][:, 128 * ic:128 * (ic + 1)],
                                                  src_sb[:, 1024 * ic + 512 * tg:
                                                         1024 * ic + 512 * (tg + 1)])
                                                 for ic in range(NHC)])
                                ctx.inc(inst, "mm")

                    @block.scalar
                    def _(se, dst_sb=dst_sb, snm=snm, c_mm=c_mm):
                        for oc in range(NHC):
                            for tg in range(2):
                                ps = psA if tg == 0 else psB
                                se.wait_ge(ctx.sem("mm"), c_mm + 2 * oc + tg + 1)
                                inst = se.activation(
                                    dst_sb[:, 1024 * oc + 512 * tg:1024 * oc + 512 * (tg + 1)],
                                    ps[:, :], AF.Gelu, scale=scl[snm][:, oc:oc + 1])
                                ctx.inc(inst, "ea")

            w3_state = {}

            def rhs_loader(gp, tt, w3_state=w3_state):
                base_mm = ctx.v("mm")
                w3_state["ld"] = ctx.v("ldx")
                for fc in range(NC):
                    gp.wait_ge(ctx.sem("mm"), base_mm + (fc - 1 if fc >= 2 else 0))
                    ctx.inc(gp.dma_start(out=wH[fc % 2][:, :], in_=w3h[fc]), "ldx", 16)

            def producer(te, fc, tt, ps, tb=tb, w3_state=w3_state):
                te.wait_ge(ctx.sem("ldx"), w3_state["ld"] + 16 * (fc + 1))
                tg = tt - 2 * tb
                return mm_accum(te, ps[:, :],
                                [(wH[fc % 2][:, 128 * ic:128 * (ic + 1)],
                                  a_sb[:, 1024 * ic + 512 * tg:1024 * ic + 512 * (tg + 1)])
                                 for ic in range(NHC)])

            pl.emit(producer, rhs_loader, [x3_32], x4_32, None,
                    scale_pc_sb=scl["s3"], tts=range(2 * tb, 2 * tb + 2))

        with nc.Block() as block:
            @block.sync
            def _(sy):
                sy.wait_ge(ctx.sem("st"), ctx.v("st"))
    return


def emit_p5a(ctx, stack, io):
    nc = ctx.nc
    ctx.cur_stack = stack

    x2_bf = io["x2_bf"]
    ln1_pc = io["ln1_pc"]
    Wq_in = io["Wq"]
    Wk_in = io["Wk"]
    Wv_in = io["Wv"]
    Wb_in = io["Wb"]
    ones2_in = io["C_ones2"]
    sel_in = io["C_sel"]

    q_fm = io["q_fm"]
    k_fm = io["k_fm"]
    v_tm = io["v_tm"]
    k_tm = io["k_tm"]
    beta_tm = io["beta_tm"]
    xn1_out = io["xn1_bf"]

    x2t_sb = stack.enter_context(nc.sbuf_tensor("p5_x2", [128, NC * 512], BF16))
    xn1_sb = stack.enter_context(nc.sbuf_tensor("p5_xn1", [128, NC * T], BF16))
    wq_sb = stack.enter_context(nc.sbuf_tensor("p5_wq", [128, NC * D], BF16))
    wk_sb = stack.enter_context(nc.sbuf_tensor("p5_wk", [128, NC * D], BF16))
    wv_sb = stack.enter_context(nc.sbuf_tensor("p5_wv", [128, NC * D], BF16))
    wb_sb = stack.enter_context(nc.sbuf_tensor("p5_wb", [128, NC * H], BF16))
    ones2_sb = stack.enter_context(nc.sbuf_tensor("p5_ones2", [128, NC * H], BF16))
    sel_sb = stack.enter_context(nc.sbuf_tensor("p5_sel", [H, D], BF16))
    ln1_sb = stack.enter_context(nc.sbuf_tensor("p5_ln1", [128, NC], F32))
    onesrow = stack.enter_context(nc.sbuf_tensor("p5_onesrow", [1, 128], BF16))
    ones128 = stack.enter_context(nc.sbuf_tensor("p5_ones128", [128, 1], BF16))
    sqbf_sb = stack.enter_context(nc.sbuf_tensor("p5_sqbf", [128, 512], BF16))
    row_sb = stack.enter_context(nc.sbuf_tensor("p5_row", [1, 1024], F32))
    rowb_sb = stack.enter_context(nc.sbuf_tensor("p5_rowb", [1, 1024], BF16))
    rb32 = stack.enter_context(nc.sbuf_tensor("p5_rb32", [128, 512], F32))
    t1_sb = stack.enter_context(nc.sbuf_tensor("p5_t1", [128, 512], F32))

    pre32 = stack.enter_context(nc.sbuf_tensor("p5_pre32", [128, NC * 512], F32))
    sq_sb = stack.enter_context(nc.sbuf_tensor("p5_sq", [128, NC * 512], BF16))
    val_bf = stack.enter_context(nc.sbuf_tensor("p5_val", [128, NC * 512], BF16))
    st16 = stack.enter_context(nc.sbuf_tensor("p5_st16", [H, 3 * 512], F32))
    st16b = stack.enter_context(nc.sbuf_tensor("p5_st16b", [H, 2 * 512], BF16))
    out_bf = stack.enter_context(nc.sbuf_tensor("p5_outbf", [128, NC * 512], BF16))
    v_bf = stack.enter_context(nc.sbuf_tensor("p5_vbf", [128, 2 * 512], BF16))
    trans_sb = stack.enter_context(nc.sbuf_tensor("p5_trans", [128, 1024], BF16))

    ps_a = stack.enter_context(nc.psum_tensor("p5_psa", [128, 512], F32))
    ps_b = stack.enter_context(nc.psum_tensor("p5_psb", [128, 512], F32))
    ps_ss = stack.enter_context(nc.psum_tensor("p5_psss", [H, 512], F32))
    ps_mb = stack.enter_context(nc.psum_tensor("p5_psmb", [128, 512], F32))
    ps_rb = stack.enter_context(nc.psum_tensor("p5_psrb", [128, 512], F32))
    ps_st = stack.enter_context(nc.psum_tensor("p5_psst", [1, 512], F32))

    if True:
        with nc.Block() as block:
            @block.gpsimd
            def _(gp):
                for i in range(NC):
                    ctx.inc(gp.dma_start(out=wq_sb[:, D * i:D * (i + 1)], in_=Wq_in[128 * i:128 * (i + 1), :]), "ld", 16)
                    ctx.inc(gp.dma_start(out=wk_sb[:, D * i:D * (i + 1)], in_=Wk_in[128 * i:128 * (i + 1), :]), "ld", 16)
                    ctx.inc(gp.dma_start(out=wv_sb[:, D * i:D * (i + 1)], in_=Wv_in[128 * i:128 * (i + 1), :]), "ld", 16)
                    ctx.inc(gp.dma_start(out=wb_sb[:, H * i:H * (i + 1)], in_=Wb_in[128 * i:128 * (i + 1), :]), "ld", 16)
                    ctx.inc(gp.dma_start(out=ones2_sb[:, H * i:H * (i + 1)], in_=ones2_in[128 * i:128 * (i + 1), :]), "ld", 16)
                ctx.inc(gp.dma_start(out=sel_sb[:, :], in_=sel_in), "ld", 16)
                ctx.inc(gp.dma_start(out=ln1_sb[:, :], in_=ln1_pc), "ld", 16)
                gp.memset(onesrow[:, :], 1.0)
                inst = gp.memset(ones128[:, :], 1.0)
                ctx.inc(inst, "gpc")
        ld0 = ctx.v("ld")
        gp0 = ctx.v("gpc")

        for tt in range(NT):
            t0, t1 = 512 * tt, 512 * (tt + 1)
            b_mm = ctx.v("mm")
            b_ea = ctx.v("ea")
            b_ev = ctx.v("ev")
            with nc.Block() as block:
                @block.gpsimd
                def _(gp, tt=tt, t0=t0, t1=t1, b_mm=b_mm, b_ev=b_ev):
                    gp.wait_ge(ctx.sem("mm"), b_mm)
                    gp.wait_ge(ctx.sem("ev"), b_ev)
                    for fc in range(NC):
                        inst = gp.dma_start(out=x2t_sb[:, 512 * fc:512 * (fc + 1)],
                                            in_=x2_bf[128 * fc:128 * (fc + 1), t0:t1])
                        ctx.inc(inst, "ldx", 16)
                ld_v = ctx.v("ldx")

                @block.scalar
                def _(se, b_mm=b_mm, ld_v=ld_v):
                    se.wait_ge(ctx.sem("ldx"), ld_v)
                    for fc in range(NC):
                        if fc >= 1:
                            se.wait_ge(ctx.sem("mm"), b_mm + fc)
                        inst = se.activation(sqbf_sb[:, :], x2t_sb[:, 512 * fc:512 * (fc + 1)],
                                             AF.Square)
                        ctx.inc(inst, "ea")

                @block.tensor
                def _(te, b_ea=b_ea, gp0=gp0):
                    te.wait_ge(ctx.sem("gpc"), gp0)
                    for fc in range(NC):
                        te.wait_ge(ctx.sem("ea"), b_ea + fc + 1)
                        inst = te.matmul(ps_st[0:1, :], ones128[:, 0:1], sqbf_sb[:, :],
                                         start=(fc == 0), stop=(fc == NC - 1))
                        ctx.inc(inst, "mm")
                c_mm = ctx.v("mm")

                @block.vector
                def _(ve, c_mm=c_mm):
                    ve.wait_ge(ctx.sem("mm"), c_mm)
                    ctx.seq(ve, "vv", ve.tensor_scalar_mul(row_sb[0:1, 0:512], ps_st[0:1, :], 1.0 / D))
                    inst = ve.tensor_scalar_add(row_sb[0:1, 0:512], row_sb[0:1, 0:512], 1e-6)
                    ctx.inc(inst, "ev")
                c_ev = ctx.v("ev")

                @block.scalar
                def _(se, c_ev=c_ev):
                    se.wait_ge(ctx.sem("ev"), c_ev)
                    inst = se.activation(row_sb[0:1, 0:512], row_sb[0:1, 0:512], AF.Sqrt)
                    ctx.inc(inst, "ea")
                c_ea = ctx.v("ea")

                @block.vector
                def _(ve, c_ea=c_ea):
                    ve.wait_ge(ctx.sem("ea"), c_ea)
                    ctx.seq(ve, "vv", ve.reciprocal(row_sb[0:1, 512:1024], row_sb[0:1, 0:512]))
                    ctx.seq(ve, "vv", ve.tensor_copy(rowb_sb[0:1, 0:512], row_sb[0:1, 512:1024]))
                    inst = ve.tensor_sub(rowb_sb[0:1, 512:1024], row_sb[0:1, 512:1024],
                                         rowb_sb[0:1, 0:512])
                    ctx.inc(inst, "ev")
                c_ev2 = ctx.v("ev")

                @block.tensor
                def _(te, c_ev2=c_ev2, b_ea=b_ea):
                    te.wait_ge(ctx.sem("ev"), c_ev2)
                    te.wait_ge(ctx.sem("ea"), b_ea + NC + 1)
                    te.matmul(ps_a[:, :], onesrow[0:1, :], rowb_sb[0:1, 0:512],
                              start=True, stop=False)
                    inst = te.matmul(ps_a[:, :], onesrow[0:1, :], rowb_sb[0:1, 512:1024],
                                     start=False, stop=True)
                    ctx.inc(inst, "mm")
                c_mm2 = ctx.v("mm")

                @block.scalar
                def _(se, c_mm2=c_mm2):
                    se.wait_ge(ctx.sem("mm"), c_mm2)
                    inst = se.copy(rb32[:, :], ps_a[:, :])
                    ctx.inc(inst, "ea")
                c_ea2 = ctx.v("ea")

                @block.vector
                def _(ve, c_ea2=c_ea2):
                    for fc in range(NC):
                        ve.wait_ge(ctx.sem("ea"), c_ea2 + fc)
                        inst = ve.tensor_mul(t1_sb[:, :], x2t_sb[:, 512 * fc:512 * (fc + 1)],
                                             rb32[:, :])
                        ctx.inc(inst, "ev")

                @block.scalar
                def _(se, c_ev2=c_ev2, tt=tt, t0=t0, t1=t1):
                    for fc in range(NC):
                        se.wait_ge(ctx.sem("ev"), c_ev2 + fc + 1)
                        inst = se.activation(xn1_sb[:, T * fc + t0:T * fc + t1], t1_sb[:, :],
                                             AF.Identity, scale=ln1_sb[:, fc:fc + 1])
                        ctx.inc(inst, "ea")
                last_ea = ctx.v("ea")

                @block.sync
                def _(sy, last_ea=last_ea, tt=tt, t0=t0, t1=t1):
                    sy.wait_ge(ctx.sem("ea"), last_ea)
                    for fc in range(NC):
                        inst = sy.dma_start(out=xn1_out[128 * fc:128 * (fc + 1), t0:t1],
                                            in_=xn1_sb[:, T * fc + t0:T * fc + t1])
                        ctx.inc(inst, "st", 16)

        bufs = (xn1_sb, ones2_sb, sel_sb, pre32, sq_sb, val_bf, st16, st16b,
                rb32, t1_sb, out_bf, ps_a, ps_b, ps_ss, ps_mb, ps_rb)
        emit_proj_headnorm(ctx, bufs, wq_sb, q_fm, "l2", eps=1e-6)
        emit_proj_headnorm(ctx, bufs, wk_sb, k_fm, "l2", eps=1e-6)

        for ts in range(NS):
            b_mm = ctx.v("mm")
            b_ea = ctx.v("ea")
            b_st = ctx.v("st")
            with nc.Block() as block:
                @block.tensor
                def _(te, ts=ts, b_ea=b_ea):
                    te.wait_ge(ctx.sem("ea"), b_ea)
                    for fh in range(2):
                        ps = ps_a if fh == 0 else ps_b
                        inst = mm_accum(te, ps[:, :],
                                        [(xn1_sb[:, T * kc + 128 * ts:T * kc + 128 * (ts + 1)],
                                          wv_sb[:, D * kc + 512 * fh:D * kc + 512 * (fh + 1)])
                                         for kc in range(NC)])
                        ctx.inc(inst, "mm")
                    inst = mm_accum(te, ps_rb[:, 0:H],
                                    [(xn1_sb[:, T * kc + 128 * ts:T * kc + 128 * (ts + 1)],
                                      wb_sb[:, H * kc:H * (kc + 1)]) for kc in range(NC)])
                    ctx.inc(inst, "mm")

                @block.scalar
                def _(se, ts=ts, b_mm=b_mm, b_st=b_st):
                    se.wait_ge(ctx.sem("st"), b_st)
                    for fh in range(2):
                        ps = ps_a if fh == 0 else ps_b
                        se.wait_ge(ctx.sem("mm"), b_mm + fh + 1)
                        inst = se.activation(v_bf[:, 512 * fh:512 * (fh + 1)], ps[:, :], AF.Silu)
                        ctx.inc(inst, "ea")
                    se.wait_ge(ctx.sem("mm"), b_mm + 3)
                    inst = se.activation(t1_sb[:, 0:H], ps_rb[:, 0:H], AF.Sigmoid)
                    ctx.inc(inst, "ea")
                last_ea = ctx.v("ea")

                @block.sync
                def _(sy, ts=ts, last_ea=last_ea):
                    sy.wait_ge(ctx.sem("ea"), last_ea)
                    inst = sy.dma_start(out=v_tm[128 * ts:128 * (ts + 1), :], in_=v_bf[:, :])
                    ctx.inc(inst, "st", 16)
                    inst = sy.dma_start(out=beta_tm[128 * ts:128 * (ts + 1), :], in_=t1_sb[:, 0:H])
                    ctx.inc(inst, "st", 16)

        st_k = ctx.v("st")
        for ts in range(NS):
            b_st_ts = ctx.v("st")
            with nc.Block() as block:
                @block.sync
                def _(sy, ts=ts, b_st_ts=b_st_ts):
                    sy.wait_ge(ctx.sem("st"), b_st_ts if ts > 0 else st_k)
                    inst = sy.dma_start_transpose(trans_sb[:, :],
                                                  k_fm[:, 128 * ts:128 * (ts + 1)])
                    ctx.inc(inst, "ldx", 16)
                ld_v = ctx.v("ldx")

                @block.sync
                def _(sy, ts=ts, ld_v=ld_v):
                    sy.wait_ge(ctx.sem("ldx"), ld_v)
                    inst = sy.dma_start(out=k_tm[128 * ts:128 * (ts + 1), :], in_=trans_sb[:, :])
                    ctx.inc(inst, "st", 16)

        with nc.Block() as block:
            @block.sync
            def _(sy):
                sy.wait_ge(ctx.sem("st"), ctx.v("st"))
    return


def emit_p5b(ctx, stack, io):
    nc = ctx.nc
    ctx.cur_stack = stack

    q_fm = io["q_fm"]
    k_fm = io["k_fm"]
    k_tm = io["k_tm"]
    v_tm = io["v_tm"]
    beta_tm = io["beta_tm"]
    x2_32 = io["x2_32"]
    Wo_in = io["Wo"]
    ln2_pc = io["ln2_pc"]
    wgh = io["wgh"]
    wuh = io["wuh"]
    wdh = io["wdh"]
    stril4_in = io["C_stril4"]
    triu4_in = io["C_triu4"]
    eye_in = io["C_eye"]

    x3_32 = io["x3_32"]
    x3_bf = io["x3_bf"]
    hh_32 = io["hh_32"]
    y_dram = io["y_bfd"]

    o_fm = stack.enter_context(nc.sbuf_tensor("d_ofm", [128, NC * T], BF16))
    kf_c = stack.enter_context(nc.sbuf_tensor("d_kf", [64, 2048], BF16))
    qf_c = stack.enter_context(nc.sbuf_tensor("d_qf", [64, 2048], BF16))
    kt_c = stack.enter_context(nc.sbuf_tensor("d_kt", [128, 1024], BF16))
    vt_c = stack.enter_context(nc.sbuf_tensor("d_vt", [128, 1024], BF16))
    beta_c = stack.enter_context(nc.sbuf_tensor("d_beta", [128, H], F32))
    N_all = stack.enter_context(nc.sbuf_tensor("d_N", [128, H * 128], F32))
    P_all = stack.enter_context(nc.sbuf_tensor("d_P", [128, H * 128], F32))
    Y_all = stack.enter_context(nc.sbuf_tensor("d_Y", [128, H * 128], F32))
    Gm_all = stack.enter_context(nc.sbuf_tensor("d_Gm", [128, H * 128], BF16))
    TKf = stack.enter_context(nc.sbuf_tensor("d_TKf", [64, H * 128], BF16))
    Vt_all = stack.enter_context(nc.sbuf_tensor("d_Vt", [128, H * 64], BF16))
    S32 = stack.enter_context(nc.sbuf_tensor("d_S32", [64, H * 64], F32))
    S_bf = stack.enter_context(nc.sbuf_tensor("d_Sbf", [64, H * 64], BF16))
    stril4 = stack.enter_context(nc.sbuf_tensor("d_stril4", [128, 512], BF16))
    triu4 = stack.enter_context(nc.sbuf_tensor("d_triu4", [128, 512], BF16))
    eye_sb = stack.enter_context(nc.sbuf_tensor("d_eye", [128, 128], BF16))
    eye32_sb = stack.enter_context(nc.sbuf_tensor("d_eye32", [128, 128], F32))

    psG = stack.enter_context(nc.psum_tensor("d_psG", [128, 512], F32))
    psX = stack.enter_context(nc.psum_tensor("d_psX", [128, 512], F32))
    psY = stack.enter_context(nc.psum_tensor("d_psY", [128, 512], F32))
    psT = stack.enter_context(nc.psum_tensor("d_psT", [128, 512], F32))
    psV = stack.enter_context(nc.psum_tensor("d_psV", [128, 512], F32))
    psS = stack.enter_context(nc.psum_tensor("d_psS", [64, 1024], F32))
    psO = stack.enter_context(nc.psum_tensor("d_psO", [64, 512], F32))

    if True:
        with nc.Block() as block:
            @block.gpsimd
            def _(gp):
                ctx.inc(gp.dma_start(out=stril4[:, :], in_=stril4_in), "ld", 16)
                ctx.inc(gp.dma_start(out=triu4[:, :], in_=triu4_in), "ld", 16)
                ctx.inc(gp.dma_start(out=eye_sb[:, :], in_=eye_in), "ld", 16)
                ctx.inc(gp.dma_start(out=eye32_sb[:, :], in_=eye_in), "ld", 16)
                gp.memset(S32[:, :], 0.0)
                inst = gp.memset(S_bf[:, :], 0.0)
                ctx.inc(inst, "gpc")
        ld0 = ctx.v("ld")
        gp0 = ctx.v("gpc")
        with nc.Block() as block:
            @block.tensor
            def _(te):
                te.wait_ge(ctx.sem("ld"), ld0)
                te.wait_ge(ctx.sem("gpc"), gp0)
            @block.vector
            def _(ve):
                ve.wait_ge(ctx.sem("ld"), ld0)
                ve.wait_ge(ctx.sem("gpc"), gp0)
            @block.scalar
            def _(se):
                se.wait_ge(ctx.sem("ld"), ld0)
                se.wait_ge(ctx.sem("gpc"), gp0)

        for c in range(NS):
            c0 = 128 * c
            b_mm = ctx.v("mm")
            b_ea = ctx.v("ea")
            with nc.Block() as block:
                @block.gpsimd
                def _(gp, c0=c0, b_mm=b_mm, b_ea=b_ea):
                    gp.wait_ge(ctx.sem("mm"), b_mm)
                    gp.wait_ge(ctx.sem("ea"), b_ea)
                    for half in range(2):
                        inst = gp.dma_start(
                            out=bass.AP(kf_c, 1024 * half, [[2048, 64], [128, 8], [1, 128]]),
                            in_=bass.AP(k_fm.tensor, k_fm.offset + c0 + half * 8 * 64 * T,
                                        [[T, 64], [64 * T, 8], [1, 128]]))
                        ctx.inc(inst, "ldx", 16)
                        inst = gp.dma_start(
                            out=bass.AP(qf_c, 1024 * half, [[2048, 64], [128, 8], [1, 128]]),
                            in_=bass.AP(q_fm.tensor, q_fm.offset + c0 + half * 8 * 64 * T,
                                        [[T, 64], [64 * T, 8], [1, 128]]))
                        ctx.inc(inst, "ldx", 16)
                    ctx.inc(gp.dma_start(out=kt_c[:, :], in_=k_tm[c0:c0 + 128, :]), "ldx", 16)
                    ctx.inc(gp.dma_start(out=vt_c[:, :], in_=v_tm[c0:c0 + 128, :]), "ldx", 16)
                    ctx.inc(gp.dma_start(out=beta_c[:, :], in_=beta_tm[c0:c0 + 128, :]), "ldx", 16)
            ld_c = ctx.v("ldx")

            for q4 in range(4):
                with nc.Block() as block:
                    @block.tensor
                    def _(te, q4=q4, ld_c=ld_c):
                        te.wait_ge(ctx.sem("ldx"), ld_c)
                        te.wait_ge(ctx.sem("ev"), ctx.v("ev"))
                        te.wait_ge(ctx.sem("ea"), ctx.v("ea"))
                        for i in range(4):
                            h = 4 * q4 + i
                            sl = kf_c[:, 128 * h:128 * (h + 1)]
                            inst = te.matmul(psG[:, 128 * i:128 * (i + 1)], sl, sl,
                                             start=True, stop=True)
                        ctx.inc(inst, "mm")
                    v_mm = ctx.v("mm")

                    @block.scalar
                    def _(se, q4=q4):
                        for i in range(4):
                            h = 4 * q4 + i
                            se.activation(Y_all[:, 128 * h:128 * h + 64],
                                          vt_c[:, 64 * h:64 * h + 64],
                                          AF.Identity, scale=beta_c[:, h:h + 1])
                            inst = se.activation(Y_all[:, 128 * h + 64:128 * h + 128],
                                                 kt_c[:, 64 * h:64 * h + 64],
                                                 AF.Identity, scale=beta_c[:, h:h + 1])
                        ctx.inc(inst, "ea")

                    @block.vector
                    def _(ve, q4=q4, v_mm=v_mm):
                        ve.wait_ge(ctx.sem("mm"), v_mm)
                        ctx.seq(ve, "vv", ve.scalar_tensor_tensor(
                            N_all[:, 512 * q4:512 * (q4 + 1)], psG[:, :], -1.0,
                            stril4[:, :], ALU.mult, ALU.mult))
                        bexp = bass.AP(beta_c, 4 * q4, [[H, 128], [1, 4], [0, 128]])
                        inst = ve.tensor_mul(N_all[:, 512 * q4:512 * (q4 + 1)],
                                             N_all[:, 512 * q4:512 * (q4 + 1)], bexp)
                        ctx.inc(inst, "ev")
                    v_ev = ctx.v("ev")

                    @block.tensor
                    def _(te, q4=q4, v_ev=v_ev):
                        te.wait_ge(ctx.sem("ev"), v_ev)
                        for i in range(4):
                            h = 4 * q4 + i
                            inst = te.matmul(psY[:, 128 * i:128 * (i + 1)],
                                             kf_c[:, 128 * h:128 * (h + 1)],
                                             qf_c[:, 128 * h:128 * (h + 1)],
                                             start=True, stop=True)
                        ctx.inc(inst, "mm")
                    kq_mm = ctx.v("mm")

                    @block.vector
                    def _(ve, q4=q4, kq_mm=kq_mm):
                        ve.wait_ge(ctx.sem("mm"), kq_mm)
                        inst = ve.tensor_mul(Gm_all[:, 512 * q4:512 * (q4 + 1)],
                                             psY[:, :], triu4[:, :])
                        ctx.inc(inst, "ev")

                    @block.tensor
                    def _(te, q4=q4, v_ev=v_ev):
                        te.wait_ge(ctx.sem("ev"), v_ev)
                        for i in range(4):
                            h = 4 * q4 + i
                            inst = te.transpose(psT[:, 128 * i:128 * (i + 1)],
                                                N_all[:, 128 * h:128 * (h + 1)], eye32_sb[:, :])
                        ctx.inc(inst, "mm")
                    t_mm = ctx.v("mm")

                    @block.vector
                    def _(ve, q4=q4, t_mm=t_mm):
                        ve.wait_ge(ctx.sem("mm"), t_mm)
                        inst = ve.tensor_copy(P_all[:, 512 * q4:512 * (q4 + 1)], psT[:, :])
                        ctx.inc(inst, "ev")

            for j in range(7):
                for q4 in range(4):
                    with nc.Block() as block:
                        @block.tensor
                        def _(te, j=j, q4=q4):
                            te.wait_ge(ctx.sem("ev"), ctx.v("ev"))
                            te.wait_ge(ctx.sem("ea"), ctx.v("ea"))
                            for i in range(4):
                                h = 4 * q4 + i
                                inst = te.matmul(psX[:, 128 * i:128 * (i + 1)],
                                                 P_all[:, 128 * h:128 * (h + 1)],
                                                 Y_all[:, 128 * h:128 * (h + 1)],
                                                 start=True, stop=True)
                            ctx.inc(inst, "mm")
                            if j < 6:
                                for i in range(4):
                                    h = 4 * q4 + i
                                    inst = te.matmul(psY[:, 128 * i:128 * (i + 1)],
                                                     P_all[:, 128 * h:128 * (h + 1)],
                                                     N_all[:, 128 * h:128 * (h + 1)],
                                                     start=True, stop=True)
                                ctx.inc(inst, "mm")
                                for i in range(4):
                                    h = 4 * q4 + i
                                    inst = te.matmul(psV[:, 128 * i:128 * (i + 1)],
                                                     N_all[:, 128 * h:128 * (h + 1)],
                                                     P_all[:, 128 * h:128 * (h + 1)],
                                                     start=True, stop=True)
                                ctx.inc(inst, "mm")
                        v_mm = ctx.v("mm")

                        @block.vector
                        def _(ve, j=j, q4=q4, v_mm=v_mm):
                            ve.wait_ge(ctx.sem("mm"), v_mm)
                            ctx.seq(ve, "vv", ve.tensor_add(Y_all[:, 512 * q4:512 * (q4 + 1)],
                                                            Y_all[:, 512 * q4:512 * (q4 + 1)],
                                                            psX[:, :]))
                            if j < 6:
                                ctx.seq(ve, "vv", ve.tensor_copy(
                                    N_all[:, 512 * q4:512 * (q4 + 1)], psY[:, :]))
                                inst = ve.tensor_copy(P_all[:, 512 * q4:512 * (q4 + 1)],
                                                      psV[:, :])
                            else:
                                inst = ve.tensor_copy(
                                    bass.AP(Vt_all, 256 * q4, [[1024, 128], [64, 4], [1, 64]]),
                                    bass.AP(Y_all, 512 * q4, [[2048, 128], [128, 4], [1, 64]]))
                            ctx.inc(inst, "ev")

            for q4 in range(4):
                with nc.Block() as block:
                    @block.tensor
                    def _(te, q4=q4):
                        te.wait_ge(ctx.sem("ev"), ctx.v("ev"))
                        for i in range(4):
                            h = 4 * q4 + i
                            inst = te.transpose(psT[0:64, 128 * i:128 * (i + 1)],
                                                Y_all[:, 128 * h + 64:128 * h + 128],
                                                eye32_sb[:, :])
                        ctx.inc(inst, "mm")
                    t_mm = ctx.v("mm")

                    @block.vector
                    def _(ve, q4=q4, t_mm=t_mm):
                        ve.wait_ge(ctx.sem("mm"), t_mm)
                        inst = ve.tensor_copy(TKf[0:64, 512 * q4:512 * (q4 + 1)], psT[0:64, :])
                        ctx.inc(inst, "ev")

            with nc.Block() as block:
                @block.tensor
                def _(te):
                    te.wait_ge(ctx.sem("ev"), ctx.v("ev"))
                    te.wait_ge(ctx.sem("ea"), ctx.v("ea"))
                    for h in range(8):
                        inst = te.matmul(psV[:, 64 * h:64 * (h + 1)],
                                         TKf[0:64, 128 * h:128 * (h + 1)],
                                         S_bf[:, 64 * h:64 * (h + 1)], start=True, stop=True)
                    ctx.inc(inst, "mm")
                    for h in range(8, H):
                        inst = te.matmul(psG[:, 64 * (h - 8):64 * (h - 7)],
                                         TKf[0:64, 128 * h:128 * (h + 1)],
                                         S_bf[:, 64 * h:64 * (h + 1)], start=True, stop=True)
                    ctx.inc(inst, "mm")
                v_mm = ctx.v("mm")

                @block.vector
                def _(ve, v_mm=v_mm):
                    ve.wait_ge(ctx.sem("mm"), v_mm)
                    yv0 = bass.AP(Y_all, 0, [[2048, 128], [128, 8], [1, 64]])
                    yv1 = bass.AP(Y_all, 128 * 8, [[2048, 128], [128, 8], [1, 64]])
                    ctx.seq(ve, "vv", ve.tensor_sub(Vt_all[:, 0:512], yv0, psV[:, :]))
                    inst = ve.tensor_sub(Vt_all[:, 512:1024], yv1, psG[:, :])
                    ctx.inc(inst, "ev")
                v_ev = ctx.v("ev")

                ea_pred = ctx.v("ea")

                @block.tensor
                def _(te, v_ev=v_ev, ea_pred=ea_pred):
                    te.wait_ge(ctx.sem("ev"), v_ev)
                    for q4 in range(4):
                        if q4 > 0:
                            te.wait_ge(ctx.sem("ea"), ea_pred + q4)
                        for i in range(4):
                            h = 4 * q4 + i
                            te.matmul(psO[:, 128 * i:128 * (i + 1)],
                                      S_bf[:, 64 * h:64 * (h + 1)],
                                      qf_c[:, 128 * h:128 * (h + 1)],
                                      start=True, stop=False)
                            inst = te.matmul(psO[:, 128 * i:128 * (i + 1)],
                                             Vt_all[:, 64 * h:64 * (h + 1)],
                                             Gm_all[:, 128 * h:128 * (h + 1)],
                                             start=False, stop=True)
                        ctx.inc(inst, "mm")
                    for h in range(H):
                        inst = te.matmul(psS[:, 64 * h:64 * (h + 1)],
                                         kt_c[:, 64 * h:64 * (h + 1)],
                                         Vt_all[:, 64 * h:64 * (h + 1)], start=True, stop=True)
                    ctx.inc(inst, "mm")
                o_mm = ctx.v("mm")

                @block.scalar
                def _(se, v_mm=v_mm, c=c):
                    for q4 in range(4):
                        se.wait_ge(ctx.sem("mm"), v_mm + q4 + 1)
                        for i in range(4):
                            h = 4 * q4 + i
                            fc2 = h // 2
                            pbase = 64 * (h % 2)
                            inst = se.copy(
                                o_fm[pbase:pbase + 64, T * fc2 + 128 * c:T * fc2 + 128 * (c + 1)],
                                psO[:, 128 * i:128 * (i + 1)])
                        ctx.inc(inst, "ea")

                @block.vector
                def _(ve, o_mm=o_mm):
                    ve.wait_ge(ctx.sem("mm"), o_mm)
                    inst = ve.tensor_add(S32[:, :], S32[:, :], psS[:, :])
                    ctx.inc(inst, "ev")
                s_ev = ctx.v("ev")

                @block.scalar
                def _(se, s_ev=s_ev):
                    se.wait_ge(ctx.sem("ev"), s_ev)
                    inst = se.copy(S_bf[:, :], S32[:, :])
                    ctx.inc(inst, "ea")

        wo_big = stack.enter_context(nc.sbuf_tensor("d_wo", [128, NC * D], BF16))
        ln2_sb = stack.enter_context(nc.sbuf_tensor("d_ln2", [128, NC], F32))
        res_sb = stack.enter_context(nc.sbuf_tensor("d_res", [128, NC * 512], F32))
        s32_sb = stack.enter_context(nc.sbuf_tensor("d_s32", [128, NC * 512], F32))
        sqbf2 = stack.enter_context(nc.sbuf_tensor("d_sqbf", [128, 512], BF16))
        row2 = stack.enter_context(nc.sbuf_tensor("d_row", [1, 1024], F32))
        rowb2 = stack.enter_context(nc.sbuf_tensor("d_rowb", [1, 1024], BF16))
        rb232 = stack.enter_context(nc.sbuf_tensor("d_rb32", [128, 512], F32))
        t32 = stack.enter_context(nc.sbuf_tensor("d_t32", [128, 512], F32))
        ybf_sb = stack.enter_context(nc.sbuf_tensor("d_ybf", [128, NC * 512], BF16))
        ones128b = stack.enter_context(nc.sbuf_tensor("d_ones128", [128, 1], BF16))
        onesrowb = stack.enter_context(nc.sbuf_tensor("d_onesrow", [1, 128], BF16))

        with nc.Block() as block:
            @block.gpsimd
            def _(gp):
                for i in range(NC):
                    ctx.inc(gp.dma_start(out=wo_big[:, D * i:D * (i + 1)],
                                         in_=Wo_in[128 * i:128 * (i + 1), :]), "ld", 16)
                ctx.inc(gp.dma_start(out=ln2_sb[:, :], in_=ln2_pc), "ld", 16)
                gp.memset(ones128b[:, :], 1.0)
                inst = gp.memset(onesrowb[:, :], 1.0)
                ctx.inc(inst, "gpc")
        ld1 = ctx.v("ld")
        gp1 = ctx.v("gpc")

        for tt in range(NT):
            t0, t1 = 512 * tt, 512 * (tt + 1)
            b_mm = ctx.v("mm")
            b_ea = ctx.v("ea")
            b_ev = ctx.v("ev")
            b_st = ctx.v("st")
            with nc.Block() as block:
                @block.gpsimd
                def _(gp, t0=t0, t1=t1, b_ev=b_ev):
                    gp.wait_ge(ctx.sem("ev"), b_ev)
                    for fc in range(NC):
                        inst = gp.dma_start(out=res_sb[:, 512 * fc:512 * (fc + 1)],
                                            in_=x2_32[128 * fc:128 * (fc + 1), t0:t1])
                        ctx.inc(inst, "ldx", 16)
                ld_v = ctx.v("ldx")

                @block.tensor
                def _(te, t0=t0, t1=t1, b_ea=b_ea):
                    te.wait_ge(ctx.sem("ld"), ld1)
                    te.wait_ge(ctx.sem("gpc"), gp1)
                    te.wait_ge(ctx.sem("ea"), b_ea)
                    for fc in range(NC):
                        ps = psX if fc % 2 == 0 else psY
                        if fc >= 2:
                            te.wait_ge(ctx.sem("ea"), b_ea + (fc - 1))
                        inst = mm_accum(te, ps[:, :],
                                        [(wo_big[:, D * ec + 128 * fc:D * ec + 128 * (fc + 1)],
                                          o_fm[:, T * ec + t0:T * ec + t1]) for ec in range(NC)])
                        ctx.inc(inst, "mm")

                @block.scalar
                def _(se, b_mm=b_mm, ld_v=ld_v, b_ev=b_ev, b_st=b_st):
                    se.wait_ge(ctx.sem("ldx"), ld_v)
                    se.wait_ge(ctx.sem("ev"), b_ev)
                    se.wait_ge(ctx.sem("st"), b_st)
                    for fc in range(NC):
                        ps = psX if fc % 2 == 0 else psY
                        se.wait_ge(ctx.sem("mm"), b_mm + fc + 1)
                        inst = se.copy(s32_sb[:, 512 * fc:512 * (fc + 1)], ps[:, :])
                        ctx.inc(inst, "ea")

                @block.vector
                def _(ve, b_ea=b_ea, b_mm=b_mm):
                    for fc in range(NC):
                        ve.wait_ge(ctx.sem("ea"), b_ea + fc + 1)
                        if fc >= 1:
                            ve.wait_ge(ctx.sem("mm"), b_mm + NC + fc)
                        sl = s32_sb[:, 512 * fc:512 * (fc + 1)]
                        ctx.seq(ve, "vv", ve.tensor_add(sl, sl, res_sb[:, 512 * fc:512 * (fc + 1)]))
                        inst = ve.tensor_mul(sqbf2[:, :], sl, sl)
                        ctx.inc(inst, "ev")

                @block.tensor
                def _(te, b_ev=b_ev):
                    for fc in range(NC):
                        te.wait_ge(ctx.sem("ev"), b_ev + fc + 1)
                        inst = te.matmul(psS[0:1, 0:512], ones128b[:, 0:1], sqbf2[:, :],
                                         start=(fc == 0), stop=(fc == NC - 1))
                        ctx.inc(inst, "mm")
                c_mm = ctx.v("mm")

                @block.vector
                def _(ve, c_mm=c_mm):
                    ve.wait_ge(ctx.sem("mm"), c_mm)
                    ctx.seq(ve, "vv", ve.tensor_scalar_mul(row2[0:1, 0:512], psS[0:1, 0:512], 1.0 / D))
                    inst = ve.tensor_scalar_add(row2[0:1, 0:512], row2[0:1, 0:512], 1e-6)
                    ctx.inc(inst, "ev")
                c_ev = ctx.v("ev")

                @block.scalar
                def _(se, c_ev=c_ev):
                    se.wait_ge(ctx.sem("ev"), c_ev)
                    inst = se.activation(row2[0:1, 0:512], row2[0:1, 0:512], AF.Sqrt)
                    ctx.inc(inst, "ea")
                c_ea = ctx.v("ea")

                @block.vector
                def _(ve, c_ea=c_ea):
                    ve.wait_ge(ctx.sem("ea"), c_ea)
                    ctx.seq(ve, "vv", ve.reciprocal(row2[0:1, 512:1024], row2[0:1, 0:512]))
                    ctx.seq(ve, "vv", ve.tensor_copy(rowb2[0:1, 0:512], row2[0:1, 512:1024]))
                    inst = ve.tensor_sub(rowb2[0:1, 512:1024], row2[0:1, 512:1024],
                                         rowb2[0:1, 0:512])
                    ctx.inc(inst, "ev")
                c_ev2 = ctx.v("ev")

                @block.tensor
                def _(te, c_ev2=c_ev2, b_ea=b_ea):
                    te.wait_ge(ctx.sem("ev"), c_ev2)
                    te.wait_ge(ctx.sem("ea"), b_ea + NC)
                    te.matmul(psX[:, :], onesrowb[0:1, :], rowb2[0:1, 0:512],
                              start=True, stop=False)
                    inst = te.matmul(psX[:, :], onesrowb[0:1, :], rowb2[0:1, 512:1024],
                                     start=False, stop=True)
                    ctx.inc(inst, "mm")
                c_mm2 = ctx.v("mm")

                @block.scalar
                def _(se, c_mm2=c_mm2):
                    se.wait_ge(ctx.sem("mm"), c_mm2)
                    inst = se.copy(rb232[:, :], psX[:, :])
                    ctx.inc(inst, "ea")
                c_ea2 = ctx.v("ea")

                @block.vector
                def _(ve, c_ea2=c_ea2):
                    for fc in range(NC):
                        ve.wait_ge(ctx.sem("ea"), c_ea2 + fc)
                        ctx.seq(ve, "vv", ve.tensor_mul(t32[:, :], s32_sb[:, 512 * fc:512 * (fc + 1)],
                                                        rb232[:, :]))
                        inst = ve.tensor_add(s32_sb[:, 512 * fc:512 * (fc + 1)],
                                             s32_sb[:, 512 * fc:512 * (fc + 1)],
                                             res_sb[:, 512 * fc:512 * (fc + 1)])
                        ctx.inc(inst, "ev")

                @block.scalar
                def _(se, c_ev2=c_ev2):
                    for fc in range(NC):
                        se.wait_ge(ctx.sem("ev"), c_ev2 + fc + 1)
                        inst = se.activation(ybf_sb[:, 512 * fc:512 * (fc + 1)], t32[:, :],
                                             AF.Identity, scale=ln2_sb[:, fc:fc + 1])
                        ctx.inc(inst, "ea")
                c_ea3 = ctx.v("ea")

                @block.sync
                def _(sy, t0=t0, t1=t1, c_ea3=c_ea3):
                    sy.wait_ge(ctx.sem("ea"), c_ea3)
                    sy.wait_ge(ctx.sem("ev"), ctx.v("ev"))
                    for fc in range(NC):
                        inst = sy.dma_start(out=hh_32[128 * fc:128 * (fc + 1), t0:t1],
                                            in_=s32_sb[:, 512 * fc:512 * (fc + 1)])
                        ctx.inc(inst, "st", 16)
                        inst = sy.dma_start(out=y_dram[128 * fc:128 * (fc + 1), t0:t1],
                                            in_=ybf_sb[:, 512 * fc:512 * (fc + 1)])
                        ctx.inc(inst, "st", 16)

        gu_sb = o_fm[:, 0:NHC * 512]
        ybuf = o_fm[:, NHC * 512:NHC * 512 + NC * 512]
        wg_sb = [o_fm[:, 20480 + 1024 * i:20480 + 1024 * (i + 1)] for i in range(2)]
        wu_sb = [o_fm[:, 22528 + 1024 * i:22528 + 1024 * (i + 1)] for i in range(2)]
        wd_sb = [o_fm[:, 24576 + 4096 * i:24576 + 4096 * (i + 1)] for i in range(2)]
        g_bf = stack.enter_context(nc.sbuf_tensor("d_gbf", [128, 512], BF16))

        pl = PostLN(ctx, "p5f", n_add=1, eps=1e-5,
                    ps_share={"ps_x": [psX, psY], "ps_st": psS, "ps_mb": psG,
                              "ps_rb": psV},
                    sb_share={"res0": res_sb, "s32": s32_sb, "xbf": ybf_sb,
                              "sbf": sqbf2, "rowb": Gm_all[0:1, 0:2048],
                              "rb32": rb232})
        pl.emit_init()

        glu_state = {}

        def pre_tile(tt):
            t0 = 512 * tt
            h_mm = ctx.v("mm")
            h_ea = ctx.v("ea")
            h_ev = ctx.v("ev")
            h_ld = ctx.v("ldx")
            h_st = ctx.v("st")
            with nc.Block() as block:
                @block.gpsimd
                def _(gp, tt=tt, t0=t0, h_mm=h_mm, h_st=h_st):
                    gp.wait_ge(ctx.sem("mm"), h_mm)
                    gp.wait_ge(ctx.sem("st"), h_st)
                    for dc in range(NC):
                        inst = gp.dma_start(out=ybuf[:, 512 * dc:512 * (dc + 1)],
                                            in_=y_dram[128 * dc:128 * (dc + 1), t0:t0 + 512])
                        ctx.inc(inst, "ldx", 16)
                ld_y = ctx.v("ldx")

                @block.gpsimd
                def _(gp, tt=tt, h_mm=h_mm):
                    for fc in range(NHC):
                        if fc >= 2:
                            gp.wait_ge(ctx.sem("mm"), h_mm + 2 * (fc - 1))
                        ctx.inc(gp.dma_start(out=wg_sb[fc % 2][:, :], in_=wgh[fc]), "ldx", 16)
                        ctx.inc(gp.dma_start(out=wu_sb[fc % 2][:, :], in_=wuh[fc]), "ldx", 16)

                @block.tensor
                def _(te, tt=tt, ld_y=ld_y, h_ea=h_ea, h_ev=h_ev):
                    te.wait_ge(ctx.sem("ldx"), ld_y)
                    te.wait_ge(ctx.sem("ea"), h_ea)
                    te.wait_ge(ctx.sem("ev"), h_ev)
                    for fc in range(NHC):
                        te.wait_ge(ctx.sem("ldx"), ld_y + 32 * (fc + 1))
                        if fc >= 1:
                            te.wait_ge(ctx.sem("ev"), h_ev + fc)
                            te.wait_ge(ctx.sem("ea"), h_ea + fc)
                        inst = mm_accum(te, psX[:, :],
                                        [(wg_sb[fc % 2][:, 128 * dc:128 * (dc + 1)],
                                          ybuf[:, 512 * dc:512 * (dc + 1)]) for dc in range(NC)])
                        ctx.inc(inst, "mm")
                        inst = mm_accum(te, psY[:, :],
                                        [(wu_sb[fc % 2][:, 128 * dc:128 * (dc + 1)],
                                          ybuf[:, 512 * dc:512 * (dc + 1)]) for dc in range(NC)])
                        ctx.inc(inst, "mm")

                @block.scalar
                def _(se, tt=tt, h_mm=h_mm):
                    for fc in range(NHC):
                        se.wait_ge(ctx.sem("mm"), h_mm + 2 * fc + 1)
                        inst = se.activation(g_bf[:, :], psX[:, :], AF.Silu)
                        ctx.inc(inst, "ea")

                @block.vector
                def _(ve, tt=tt, h_mm=h_mm, h_ea=h_ea):
                    for fc in range(NHC):
                        ve.wait_ge(ctx.sem("mm"), h_mm + 2 * fc + 2)
                        ve.wait_ge(ctx.sem("ea"), h_ea + fc + 1)
                        inst = ve.tensor_mul(gu_sb[:, 512 * fc:512 * (fc + 1)], psY[:, :],
                                             g_bf[:, :])
                        ctx.inc(inst, "ev")

            glu_state["ea"] = ctx.v("ea")
            glu_state["ev"] = ctx.v("ev")

        def rhs_loader(gp, tt):
            base_mm = ctx.v("mm")
            glu_state["wld"] = ctx.v("ldx")
            for fc in range(NC):
                gp.wait_ge(ctx.sem("mm"), base_mm + (fc - 1 if fc >= 2 else 0))
                ctx.inc(gp.dma_start(out=wd_sb[fc % 2][:, :], in_=wdh[fc]), "ldx", 16)

        def producer(te, fc, tt, ps):
            te.wait_ge(ctx.sem("ev"), glu_state["ev"])
            te.wait_ge(ctx.sem("ldx"), glu_state["wld"] + 16 * (fc + 1))
            return mm_accum(te, ps[:, :],
                            [(wd_sb[fc % 2][:, 128 * ic:128 * (ic + 1)],
                              gu_sb[:, 512 * ic:512 * (ic + 1)]) for ic in range(NHC)])

        pl.emit(producer, rhs_loader, [hh_32], x3_32, x3_bf,
                pre_tile=pre_tile)

        with nc.Block() as block:
            @block.sync
            def _(sy):
                sy.wait_ge(ctx.sem("st"), ctx.v("st"))
    return


def _pretile(w, no, ni):
    w = np.asarray(w, dtype=np.float32)
    return np.ascontiguousarray(w.reshape(no, 128, ni, 128).transpose(0, 3, 2, 1)
                                .reshape(no, 128, ni * 128))


def build_fused():
    nc = bass.Bass("TRN2", target_bir_lowering=False)
    outer = ExitStack()
    with outer:
        ctx = Ctx(nc, outer)

        def EI(name, shape, dt=F32):
            return nc.dram_tensor(name, shape, dt, kind="ExternalInput").ap()

        def IN(name, shape, dt):
            return nc.dram_tensor(name, shape, dt, kind="Internal").ap()

        g = {}
        g["xT"] = EI("xT", [D, T])
        for nm in ("WqT", "WkT", "WvT", "WoT", "L1Wq", "L1Wk", "L1Wv", "L1Wo"):
            g[nm] = EI(nm, [D, D])
        for nm in ("qg2", "qb2", "kg2", "kb2"):
            g[nm] = EI(nm, [128, 1])
        g["gbrows"] = EI("gbrows", [1, 256])
        g["bo_pc"] = EI("bo_pc", [128, NC])
        g["slot_flat"] = EI("slot_flat", [T, D])
        g["gs_row"] = EI("gs_row", [1, D])
        g["gx_pc"] = EI("gx_pc", [128, NC])
        g["w1"] = EI("w1", [D, HID])
        g["w2"] = EI("w2", [HID, D])
        g["ln1_pc"] = EI("ln1_pc", [128, NC])
        g["ln2_pc"] = EI("ln2_pc", [128, NC])
        g["L1Wb"] = EI("L1Wb", [D, H])
        for nm in ("wgh", "wuh", "w1h", "w2h"):
            g[nm] = EI(nm, [NHC, 128, NC * 128])
        for nm in ("h1h", "h2h"):
            g[nm] = EI(nm, [NHC, 128, NHC * 128])
        for nm in ("wdh", "w3h"):
            g[nm] = EI(nm, [NC, 128, NHC * 128])
        for nm in ("s1_pc", "s2_pc", "sh1_pc", "sh2_pc"):
            g[nm] = EI(nm, [128, NHC])
        g["s3_pc"] = EI("s3_pc", [128, NC])
        g["C_ones2"] = EI("C_ones2", [D, H])
        g["C_sel"] = EI("C_sel", [H, D])
        g["C_stair"] = EI("C_stair", [128, 896])
        g["C_stril4"] = EI("C_stril4", [128, 512])
        g["C_triu4"] = EI("C_triu4", [128, 512])
        g["C_eye"] = EI("C_eye", [128, 128])

        g["q_fm0"] = IN("q_fm0", [D, T], BF16)
        g["k_fm0"] = IN("k_fm0", [D, T], BF16)
        g["v_tm0"] = IN("v_tm0", [H, T, HS], BF16)
        g["attn_out"] = IN("attn_out", [D, T], BF16)
        g["x1_32"] = IN("x1_32", [D, T], F32)
        g["x1_bf"] = IN("x1_bf", [D, T], BF16)
        g["se_tm"] = IN("se_tm", [T, D], BF16)
        g["se_fm"] = IN("se_fm", [D, T], BF16)
        g["xn_fm"] = IN("xn_fm", [D, T], BF16)
        g["xn_tm"] = IN("xn_tm", [T, D], BF16)
        g["m1_t"] = IN("m1_t", [32, 128], F32)
        g["mx2_sc"] = IN("mx2_sc", [32, 128], F32)
        g["rs2_sc"] = IN("rs2_sc", [1, T], F32)
        g["x2_32"] = IN("x2_32", [D, T], F32)
        g["x2_bf"] = IN("x2_bf", [D, T], BF16)
        g["q_fm1"] = IN("q_fm1", [D, T], BF16)
        g["k_fm1"] = IN("k_fm1", [D, T], BF16)
        g["k_tm1"] = IN("k_tm1", [T, D], BF16)
        g["v_tm1"] = IN("v_tm1", [T, D], BF16)
        g["beta_tm"] = IN("beta_tm", [T, H], F32)
        g["xn1_bf"] = IN("xn1_bf", [D, T], BF16)
        g["x3_32"] = IN("x3_32", [D, T], F32)
        g["x3_bf"] = IN("x3_bf", [D, T], BF16)
        g["hh_32"] = IN("hh_32", [D, T], F32)
        g["y_bfd"] = IN("y_bfd", [D, T], BF16)
        g["y"] = nc.dram_tensor("y", [D, T], F32, kind="ExternalOutput").ap()

        m1_row = bass.AP(g["m1_t"].tensor, 0, [[T, 1], [1, T]])

        phases = [
            (emit_p1, {"xT": g["xT"], "WqT": g["WqT"], "WkT": g["WkT"], "WvT": g["WvT"],
                       "C_ones2": g["C_ones2"], "C_sel": g["C_sel"], "qg2": g["qg2"],
                       "qb2": g["qb2"], "kg2": g["kg2"], "kb2": g["kb2"],
                       "q_fm": g["q_fm0"], "k_fm": g["k_fm0"], "v_tm": g["v_tm0"]}),
            (emit_p2, {"q_fm": g["q_fm0"], "k_fm": g["k_fm0"], "v_tm": g["v_tm0"],
                       "gbrows": g["gbrows"], "C_stair": g["C_stair"],
                       "attn_out": g["attn_out"]}),
            (emit_p3, {"attn_out": g["attn_out"], "WoT": g["WoT"], "bo_pc": g["bo_pc"],
                       "xT": g["xT"], "x1_32": g["x1_32"], "x1_bf": g["x1_bf"]}),
            (emit_p4a, {"slot_flat": g["slot_flat"], "gs_row": g["gs_row"],
                        "gx_pc": g["gx_pc"], "x1_bf": g["x1_bf"], "se_tm": g["se_tm"],
                        "se_fm": g["se_fm"], "xn_fm": g["xn_fm"], "xn_tm": g["xn_tm"],
                        "m1_out": g["m1_t"], "C_eye": g["C_eye"]}),
            (emit_p4b, {"se_fm": g["se_fm"], "xn_fm": g["xn_fm"], "xn_tm": g["xn_tm"],
                        "m1_row": m1_row, "w1": g["w1"], "w2": g["w2"],
                        "x1_32": g["x1_32"], "C_eye": g["C_eye"], "x2_32": g["x2_32"],
                        "x2_bf": g["x2_bf"], "mx2_sc": g["mx2_sc"], "rs2_sc": g["rs2_sc"]}),
            (emit_p5a, {"x2_bf": g["x2_bf"], "ln1_pc": g["ln1_pc"], "Wq": g["L1Wq"],
                        "Wk": g["L1Wk"], "Wv": g["L1Wv"], "Wb": g["L1Wb"],
                        "C_ones2": g["C_ones2"], "C_sel": g["C_sel"],
                        "q_fm": g["q_fm1"], "k_fm": g["k_fm1"], "v_tm": g["v_tm1"],
                        "k_tm": g["k_tm1"], "beta_tm": g["beta_tm"],
                        "xn1_bf": g["xn1_bf"]}),
            (emit_p5b, {"q_fm": g["q_fm1"], "k_fm": g["k_fm1"], "k_tm": g["k_tm1"],
                        "v_tm": g["v_tm1"], "beta_tm": g["beta_tm"], "x2_32": g["x2_32"],
                        "Wo": g["L1Wo"], "ln2_pc": g["ln2_pc"], "wgh": g["wgh"],
                        "wuh": g["wuh"], "wdh": g["wdh"], "C_stril4": g["C_stril4"],
                        "C_triu4": g["C_triu4"], "C_eye": g["C_eye"],
                        "x3_32": g["x3_32"], "x3_bf": g["x3_bf"], "hh_32": g["hh_32"],
                        "y_bfd": g["y_bfd"]}),
            (emit_p6, {"x3_bf": g["x3_bf"], "x3_32": g["x3_32"], "w1h": g["w1h"],
                       "w2h": g["w2h"], "h1h": g["h1h"], "h2h": g["h2h"],
                       "w3h": g["w3h"], "s1_pc": g["s1_pc"], "s2_pc": g["s2_pc"],
                       "sh1_pc": g["sh1_pc"], "sh2_pc": g["sh2_pc"], "s3_pc": g["s3_pc"],
                       "x4_32": g["y"]}),
        ]
        for fn, io in phases:
            with ExitStack() as pstack:
                fn(ctx, pstack, io)
            ctx.barrier()
    return nc


def kernel(**ins):
    SCALE = float(D) ** (-0.25)
    C = host_consts()
    B = 8
    x = np.asarray(ins["L0_attn_x"], dtype=np.float32)

    base = {
        "WqT": np.ascontiguousarray(np.asarray(ins["L0_attn_Wq"]).T),
        "WkT": np.ascontiguousarray(np.asarray(ins["L0_attn_Wk"]).T),
        "WvT": np.ascontiguousarray(np.asarray(ins["L0_attn_Wv"]).T),
        "WoT": np.ascontiguousarray(np.asarray(ins["L0_attn_Wo"]).T),
        "qg2": np.tile(np.asarray(ins["L0_attn_qln_g"]), 2).reshape(128, 1) * SCALE,
        "qb2": np.tile(np.asarray(ins["L0_attn_qln_b"]), 2).reshape(128, 1) * SCALE,
        "kg2": np.tile(np.asarray(ins["L0_attn_kln_g"]), 2).reshape(128, 1) * SCALE,
        "kb2": np.tile(np.asarray(ins["L0_attn_kln_b"]), 2).reshape(128, 1) * SCALE,
        "gbrows": np.concatenate([np.asarray(ins["L0_attn_qln_g"]),
                                  np.asarray(ins["L0_attn_qln_b"]),
                                  np.asarray(ins["L0_attn_kln_g"]),
                                  np.asarray(ins["L0_attn_kln_b"])]).reshape(1, 256).astype(np.float32),
        "bo_pc": to_pc(ins["L0_attn_bo"]),
        "slot_flat": np.ascontiguousarray(np.asarray(ins["L0_ffn_slot_embeds"],
                                                     dtype=np.float32).reshape(T, D)),
        "gs_row": np.asarray(ins["L0_ffn_gamma_slot"], dtype=np.float32).reshape(1, D),
        "gx_pc": to_pc(ins["L0_ffn_gamma_x"]),
        "w1": np.asarray(ins["L0_ffn_w1"], dtype=np.float32),
        "w2": np.asarray(ins["L0_ffn_w2"], dtype=np.float32),
        "ln1_pc": to_pc(ins["L1_attn_ln1"]),
        "ln2_pc": to_pc(ins["L1_attn_ln2"]),
        "L1Wq": np.asarray(ins["L1_attn_Wq"], dtype=np.float32),
        "L1Wk": np.asarray(ins["L1_attn_Wk"], dtype=np.float32),
        "L1Wv": np.asarray(ins["L1_attn_Wv"], dtype=np.float32),
        "L1Wb": np.asarray(ins["L1_attn_Wb"], dtype=np.float32),
        "L1Wo": np.asarray(ins["L1_attn_Wo"], dtype=np.float32),
        "wgh": _pretile(np.asarray(ins["L1_attn_Wg"]).T, NHC, NC),
        "wuh": _pretile(np.asarray(ins["L1_attn_Wu"]).T, NHC, NC),
        "wdh": _pretile(np.asarray(ins["L1_attn_Wd"]).T, NC, NHC),
        "w1h": _pretile(ins["L1_ffn_w1_w"], NHC, NC),
        "w2h": _pretile(ins["L1_ffn_w2_w"], NHC, NC),
        "h1h": _pretile(ins["L1_ffn_h1_w"], NHC, NHC),
        "h2h": _pretile(ins["L1_ffn_h2_w"], NHC, NHC),
        "w3h": _pretile(ins["L1_ffn_w3_w"], NC, NHC),
        "s1_pc": to_pc(ins["L1_ffn_w1_s"]), "s2_pc": to_pc(ins["L1_ffn_w2_s"]),
        "sh1_pc": to_pc(ins["L1_ffn_h1_s"]), "sh2_pc": to_pc(ins["L1_ffn_h2_s"]),
        "s3_pc": to_pc(ins["L1_ffn_w3_s"]),
        "C_ones2": C["C_ones2"], "C_sel": C["C_sel"], "C_stair": C["C_stair"],
        "C_stril4": C["C_stril4"], "C_triu4": C["C_triu4"], "C_eye": C["C_eye"],
    }
    nc = build_fused()
    in_maps = [dict(base, xT=np.ascontiguousarray(x[b].T)) for b in range(B)]
    res = run_bass_kernel_spmd(nc, in_maps, core_ids=list(range(B)))
    out = np.stack([res.results[b]["y"].T for b in range(B)], axis=0)
    return out.astype(np.float32)


# version 07
# speedup vs baseline: 1.0340x; 1.0340x over previous
import math
from collections import defaultdict
from contextlib import ExitStack

import numpy as np
import ml_dtypes
import concourse.bass as bass
import concourse.mybir as mybir
from concourse.bass_utils import run_bass_kernel_spmd

F32 = mybir.dt.float32
BF16 = mybir.dt.bfloat16
AF = mybir.ActivationFunctionType
ALU = mybir.AluOpType
AX = mybir.AxisListType

D = 1024
T = 4096
NC = 8
NT = 8
NS = 32
H = 16
HS = 64
HID = 4096
NHC = 32


class Ctx:

    def __init__(self, nc, stack):
        self.nc = nc
        self.stack = stack
        self.sems = {}
        self.cnt = defaultdict(int)

    def sem(self, name):
        if name not in self.sems:
            self.sems[name] = self.stack.enter_context(self.nc.semaphore(name))
        return self.sems[name]

    def inc(self, inst, name, n=1):
        inst.then_inc(self.sem(name), n)
        self.cnt[name] += n
        return inst

    def v(self, name):
        self.sem(name)
        return self.cnt[name]

    def barrier(self):
        nc = self.nc
        names = [n for n in ("mm", "ea", "ev", "vv", "gpc", "ld", "ldx", "st")
                 if n in self.sems]
        vals = [(self.sem(n), self.cnt[n]) for n in names]
        with nc.Block() as block:
            def waits(eng):
                for sm, v in vals:
                    eng.wait_ge(sm, v)
            block.tensor(lambda te: waits(te))
            block.vector(lambda ve: waits(ve))
            block.scalar(lambda se: waits(se))
            block.gpsimd(lambda gp: waits(gp))
            block.sync(lambda sy: waits(sy))

    def seq(self, eng, name, inst):
        self.inc(inst, name)
        eng.wait_ge(self.sem(name), self.cnt[name])
        return inst


def _bf(x):
    return np.asarray(x, dtype=np.float32).astype(ml_dtypes.bfloat16)


def host_consts():
    c = {}
    ones2 = np.zeros((D, H), dtype=np.float32)
    for fc in range(NC):
        ones2[128 * fc:128 * fc + 64, 2 * fc] = 1.0
        ones2[128 * fc + 64:128 * fc + 128, 2 * fc + 1] = 1.0
    c["C_ones2"] = ones2
    sel = np.zeros((H, D), dtype=np.float32)
    for f in range(D):
        sel[f // HS, f] = 1.0
    c["C_sel"] = sel
    c["C_stair"] = (np.arange(896)[None, :] >= np.arange(128)[:, None] + 384).astype(np.float32)
    c["C_stril"] = (np.arange(128)[:, None] > np.arange(128)[None, :]).astype(np.float32)
    c["C_triu_incl"] = (np.arange(128)[:, None] <= np.arange(128)[None, :]).astype(np.float32)
    c["C_stril4"] = np.tile(c["C_stril"], (1, 4))
    c["C_triu4"] = np.tile(c["C_triu_incl"], (1, 4))
    c["C_eye"] = np.eye(128, dtype=np.float32)
    return c


def to_pc(vec):
    v = np.asarray(vec, dtype=np.float32).reshape(-1)
    n = v.shape[0] // 128
    return np.ascontiguousarray(v.reshape(n, 128).T)


def mm_accum(te, ps, pairs):
    inst = None
    n = len(pairs)
    for i, (l, r) in enumerate(pairs):
        inst = te.matmul(ps, l, r, start=(i == 0), stop=(i == n - 1))
    return inst


def emit_proj_headnorm(ctx, bufs, w_sb, out_ap, mode, gcol=None, bcol=None, eps=1e-5):
    nc = ctx.nc
    (x_sb, ones2_sb, sel_sb, pre32, sq_sb, val_bf, st16, st16b,
     rb_sb, t1_sb, out_bf, ps_a, ps_b, ps_ss, ps_mb, ps_rb) = bufs
    need_mean = (mode == "qkln")
    OPS = 3 if need_mean else 2

    for tt in range(NT):
        t0, t1 = 512 * tt, 512 * (tt + 1)
        b_mm = ctx.v("mm")
        b_ea = ctx.v("ea")
        b_ev = ctx.v("ev")
        b_st = ctx.v("st")

        with nc.Block() as block:
            @block.tensor
            def _(te):
                te.wait_ge(ctx.sem("ea"), b_ea)
                te.wait_ge(ctx.sem("ev"), b_ev)
                for fc in range(NC):
                    ps = ps_a if fc % 2 == 0 else ps_b
                    if fc >= 2:
                        te.wait_ge(ctx.sem("ea"), b_ea + OPS * (fc - 1))
                    inst = mm_accum(te, ps[:, :],
                                    [(w_sb[:, D * kc + 128 * fc:D * kc + 128 * (fc + 1)],
                                      x_sb[:, T * kc + t0:T * kc + t1]) for kc in range(NC)])
                    ctx.inc(inst, "mm")

            @block.scalar
            def _(se):
                se.wait_ge(ctx.sem("st"), b_st)
                se.wait_ge(ctx.sem("ev"), b_ev)
                for fc in range(NC):
                    ps = ps_a if fc % 2 == 0 else ps_b
                    se.wait_ge(ctx.sem("mm"), b_mm + fc + 1)
                    if mode == "qkln":
                        se.copy(pre32[:, 512 * fc:512 * (fc + 1)], ps[:, :])
                        se.activation(sq_sb[:, 512 * fc:512 * (fc + 1)], ps[:, :], AF.Square)
                        inst = se.copy(val_bf[:, 512 * fc:512 * (fc + 1)], ps[:, :])
                    else:
                        ctx.seq(se, "ea", se.activation(pre32[:, 512 * fc:512 * (fc + 1)],
                                                        ps[:, :], AF.Silu))
                        inst = se.activation(sq_sb[:, 512 * fc:512 * (fc + 1)],
                                             pre32[:, 512 * fc:512 * (fc + 1)], AF.Square)
                    ctx.inc(inst, "ea", OPS - (1 if mode != "qkln" else 0))

            @block.tensor
            def _(te):
                for fc in range(NC):
                    te.wait_ge(ctx.sem("ea"), b_ea + OPS * (fc + 1))
                    inst = te.matmul(ps_ss[:, :], ones2_sb[:, H * fc:H * (fc + 1)],
                                     sq_sb[:, 512 * fc:512 * (fc + 1)],
                                     start=(fc == 0), stop=(fc == NC - 1))
                    if need_mean:
                        inst = te.matmul(ps_mb[0:H, :], ones2_sb[:, H * fc:H * (fc + 1)],
                                         val_bf[:, 512 * fc:512 * (fc + 1)],
                                         start=(fc == 0), stop=(fc == NC - 1))
                ctx.inc(inst, "mm")

            @block.vector
            def _(ve):
                ve.wait_ge(ctx.sem("mm"), b_mm + NC + 1)
                if need_mean:
                    ctx.seq(ve, "vv", ve.tensor_scalar_mul(st16[:, 0:512], ps_mb[0:H, :], 1.0 / HS))
                    ctx.seq(ve, "vv", ve.tensor_mul(st16[:, 1024:1536], st16[:, 0:512], st16[:, 0:512]))
                    ctx.seq(ve, "vv", ve.scalar_tensor_tensor(st16[:, 512:1024], ps_ss[:, :], 1.0 / HS,
                                                              st16[:, 1024:1536], ALU.mult, ALU.subtract))
                else:
                    ctx.seq(ve, "vv", ve.tensor_copy(st16[:, 512:1024], ps_ss[:, :]))
                inst = ve.tensor_scalar_add(st16[:, 512:1024], st16[:, 512:1024], eps)
                ctx.inc(inst, "ev")

            @block.scalar
            def _(se):
                se.wait_ge(ctx.sem("ev"), b_ev + 1)
                inst = se.activation(st16[:, 512:1024], st16[:, 512:1024], AF.Sqrt)
                ctx.inc(inst, "ea")

            @block.vector
            def _(ve):
                ve.wait_ge(ctx.sem("ea"), b_ea + OPS * NC + 1)
                ctx.seq(ve, "vv", ve.reciprocal(st16[:, 512:1024], st16[:, 512:1024]))
                inst = ve.tensor_copy(st16b[:, :], st16[:, 0:1024])
                ctx.inc(inst, "ev")

            for fc in range(NC):
                c_mm = ctx.v("mm")
                c_ea = ctx.v("ea")
                c_ev = ctx.v("ev")

                @block.tensor
                def _(te, fc=fc, c_ev=c_ev):
                    te.wait_ge(ctx.sem("ev"), b_ev + 2)
                    if fc > 0:
                        te.wait_ge(ctx.sem("ev"), c_ev)
                    if need_mean:
                        te.matmul(ps_mb[:, :], sel_sb[:, 128 * fc:128 * (fc + 1)],
                                  st16b[:, 0:512], start=True, stop=True)
                    inst = te.matmul(ps_rb[:, :], sel_sb[:, 128 * fc:128 * (fc + 1)],
                                     st16b[:, 512:1024], start=True, stop=True)
                    ctx.inc(inst, "mm")

                @block.scalar
                def _(se, c_mm=c_mm):
                    se.wait_ge(ctx.sem("mm"), c_mm + 1)
                    inst = se.copy(rb_sb[:, :], ps_rb[:, :])
                    ctx.inc(inst, "ea")

                @block.vector
                def _(ve, fc=fc, c_mm=c_mm, c_ea=c_ea):
                    ve.wait_ge(ctx.sem("ea"), c_ea + 1)
                    if need_mean:
                        ctx.seq(ve, "vv", ve.tensor_sub(t1_sb[:, :],
                                                        pre32[:, 512 * fc:512 * (fc + 1)], ps_mb[:, :]))
                        inst = ve.tensor_mul(t1_sb[:, :], t1_sb[:, :], rb_sb[:, :])
                    else:
                        inst = ve.tensor_mul(t1_sb[:, :], pre32[:, 512 * fc:512 * (fc + 1)],
                                             rb_sb[:, :])
                    ctx.inc(inst, "ev")

                @block.scalar
                def _(se, fc=fc, c_ev=c_ev):
                    se.wait_ge(ctx.sem("ev"), c_ev + 1)
                    if mode == "qkln":
                        inst = se.activation(out_bf[:, 512 * fc:512 * (fc + 1)], t1_sb[:, :],
                                             AF.Identity, bias=bcol[:, 0:1], scale=gcol[:, 0:1])
                    else:
                        inst = se.copy(out_bf[:, 512 * fc:512 * (fc + 1)], t1_sb[:, :])
                    ctx.inc(inst, "ea")

            last_ea = ctx.v("ea")

            @block.sync
            def _(sy):
                sy.wait_ge(ctx.sem("ea"), last_ea)
                for fc in range(NC):
                    inst = sy.dma_start(out=out_ap[128 * fc:128 * (fc + 1), t0:t1],
                                        in_=out_bf[:, 512 * fc:512 * (fc + 1)])
                    ctx.inc(inst, "st", 16)


def emit_vproj(ctx, bufs, w_sb, v_tm, act=None):
    nc = ctx.nc
    (x_sb, v_bf, ps_a, ps_b) = bufs
    for ts in range(NS):
        b_mm = ctx.v("mm")
        b_ea = ctx.v("ea")
        b_st = ctx.v("st")
        with nc.Block() as block:
            @block.tensor
            def _(te, ts=ts):
                te.wait_ge(ctx.sem("ea"), b_ea)
                for fh in range(2):
                    ps = ps_a if fh == 0 else ps_b
                    inst = mm_accum(te, ps[:, :],
                                    [(x_sb[:, T * kc + 128 * ts:T * kc + 128 * (ts + 1)],
                                      w_sb[:, D * kc + 512 * fh:D * kc + 512 * (fh + 1)])
                                     for kc in range(NC)])
                    ctx.inc(inst, "mm")

            @block.scalar
            def _(se):
                se.wait_ge(ctx.sem("st"), b_st)
                for fh in range(2):
                    ps = ps_a if fh == 0 else ps_b
                    se.wait_ge(ctx.sem("mm"), b_mm + fh + 1)
                    if act is None:
                        inst = se.copy(v_bf[:, 512 * fh:512 * (fh + 1)], ps[:, :])
                    else:
                        inst = se.activation(v_bf[:, 512 * fh:512 * (fh + 1)], ps[:, :], act)
                    ctx.inc(inst, "ea")

            last_ea = ctx.v("ea")

            @block.sync
            def _(sy, ts=ts):
                sy.wait_ge(ctx.sem("ea"), last_ea)
                dst = bass.AP(v_tm.tensor, v_tm.offset + 128 * ts * HS,
                              [[HS, 128], [T * HS, H], [1, HS]])
                src = bass.AP(v_bf, 0, [[1024, 128], [HS, H], [1, HS]])
                inst = sy.dma_start(out=dst, in_=src)
                ctx.inc(inst, "st", 16)


def emit_p1(ctx, stack, io):
    nc = ctx.nc
    ctx.cur_stack = stack

    xT = io["xT"]
    WqT = io["WqT"]
    WkT = io["WkT"]
    WvT = io["WvT"]
    ones2_in = io["C_ones2"]
    sel_in = io["C_sel"]
    qg2 = io["qg2"]
    qb2 = io["qb2"]
    kg2 = io["kg2"]
    kb2 = io["kb2"]

    q_fm = io["q_fm"]
    k_fm = io["k_fm"]
    v_tm = io["v_tm"]

    x_sb = stack.enter_context(nc.sbuf_tensor("x_sb", [128, NC * T], BF16))
    wq_sb = stack.enter_context(nc.sbuf_tensor("wq_sb", [128, NC * D], BF16))
    wk_sb = stack.enter_context(nc.sbuf_tensor("wk_sb", [128, NC * D], BF16))
    wv_sb = stack.enter_context(nc.sbuf_tensor("wv_sb", [128, NC * D], BF16))
    ones2_sb = stack.enter_context(nc.sbuf_tensor("ones2_sb", [128, NC * H], BF16))
    sel_sb = stack.enter_context(nc.sbuf_tensor("sel_sb", [H, D], BF16))
    qg2_sb = stack.enter_context(nc.sbuf_tensor("qg2_sb", [128, 1], F32))
    qb2_sb = stack.enter_context(nc.sbuf_tensor("qb2_sb", [128, 1], F32))
    kg2_sb = stack.enter_context(nc.sbuf_tensor("kg2_sb", [128, 1], F32))
    kb2_sb = stack.enter_context(nc.sbuf_tensor("kb2_sb", [128, 1], F32))

    pre32 = stack.enter_context(nc.sbuf_tensor("pre32", [128, NC * 512], F32))
    sq_sb = stack.enter_context(nc.sbuf_tensor("sq_sb", [128, NC * 512], BF16))
    val_bf = stack.enter_context(nc.sbuf_tensor("val_bf", [128, NC * 512], BF16))
    st16 = stack.enter_context(nc.sbuf_tensor("st16", [H, 3 * 512], F32))
    st16b = stack.enter_context(nc.sbuf_tensor("st16b", [H, 2 * 512], BF16))
    rb_sb = stack.enter_context(nc.sbuf_tensor("rb_sb", [128, 512], F32))
    t1_sb = stack.enter_context(nc.sbuf_tensor("t1_sb", [128, 512], F32))
    out_bf = stack.enter_context(nc.sbuf_tensor("out_bf", [128, NC * 512], BF16))
    v_bf = stack.enter_context(nc.sbuf_tensor("v_bf", [128, 2 * 512], BF16))

    ps_a = stack.enter_context(nc.psum_tensor("ps_a", [128, 512], F32))
    ps_b = stack.enter_context(nc.psum_tensor("ps_b", [128, 512], F32))
    ps_ss = stack.enter_context(nc.psum_tensor("ps_ss", [H, 512], F32))
    ps_mb = stack.enter_context(nc.psum_tensor("ps_mb", [128, 512], F32))
    ps_rb = stack.enter_context(nc.psum_tensor("ps_rb", [128, 512], F32))

    if True:
        with nc.Block() as block:
            @block.gpsimd
            def _(gp):
                for i in range(NC):
                    ctx.inc(gp.dma_start(out=x_sb[:, T * i:T * (i + 1)], in_=xT[128 * i:128 * (i + 1), :]), "ld", 16)
                    ctx.inc(gp.dma_start(out=wq_sb[:, D * i:D * (i + 1)], in_=WqT[128 * i:128 * (i + 1), :]), "ld", 16)
                    ctx.inc(gp.dma_start(out=wk_sb[:, D * i:D * (i + 1)], in_=WkT[128 * i:128 * (i + 1), :]), "ld", 16)
                    ctx.inc(gp.dma_start(out=wv_sb[:, D * i:D * (i + 1)], in_=WvT[128 * i:128 * (i + 1), :]), "ld", 16)
                    ctx.inc(gp.dma_start(out=ones2_sb[:, H * i:H * (i + 1)],
                                         in_=ones2_in[128 * i:128 * (i + 1), :]), "ld", 16)
                ctx.inc(gp.dma_start(out=sel_sb[:, :], in_=sel_in), "ld", 16)
                ctx.inc(gp.dma_start(out=qg2_sb[:, :], in_=qg2), "ld", 16)
                ctx.inc(gp.dma_start(out=qb2_sb[:, :], in_=qb2), "ld", 16)
                ctx.inc(gp.dma_start(out=kg2_sb[:, :], in_=kg2), "ld", 16)
                ctx.inc(gp.dma_start(out=kb2_sb[:, :], in_=kb2), "ld", 16)

        ld_done = ctx.v("ld")
        with nc.Block() as block:
            @block.tensor
            def _(te):
                te.wait_ge(ctx.sem("ld"), ld_done)

        bufs = (x_sb, ones2_sb, sel_sb, pre32, sq_sb, val_bf, st16, st16b,
                rb_sb, t1_sb, out_bf, ps_a, ps_b, ps_ss, ps_mb, ps_rb)
        emit_proj_headnorm(ctx, bufs, wq_sb, q_fm, "qkln", qg2_sb, qb2_sb, eps=1e-5)
        emit_proj_headnorm(ctx, bufs, wk_sb, k_fm, "qkln", kg2_sb, kb2_sb, eps=1e-5)
        emit_vproj(ctx, (x_sb, v_bf, ps_a, ps_b), wv_sb, v_tm)

        st_all = ctx.v("st")
        with nc.Block() as block:
            @block.sync
            def _(sy):
                sy.wait_ge(ctx.sem("st"), st_all)

    return



def emit_m0_prep(ctx, row, negM0_col, ps_small, ones_bf, scale2):
    nc = ctx.nc
    b_ev = ctx.v("ev")
    b_ea = ctx.v("ea")
    b_mm = ctx.v("mm")
    with nc.Block() as block:
        @block.scalar
        def _(se):
            inst = se.activation(row[0:1, 256:512], row[0:1, 0:256], AF.Square)
            ctx.inc(inst, "ea")

        @block.vector
        def _(ve):
            ve.wait_ge(ctx.sem("ea"), b_ea + 1)
            ve.reduce_max(row[0:1, 512:513], row[0:1, 0:64], axis=AX.X, apply_absolute_value=True)
            ve.reduce_max(row[0:1, 514:515], row[0:1, 128:192], axis=AX.X, apply_absolute_value=True)
            ve.reduce_sum(row[0:1, 513:514], row[0:1, 320:384], axis=AX.X)
            inst = ve.reduce_sum(row[0:1, 515:516], row[0:1, 448:512], axis=AX.X)
            ctx.inc(inst, "ev")

        @block.scalar
        def _(se):
            se.wait_ge(ctx.sem("ev"), b_ev + 1)
            inst = se.activation(row[0:1, 513:514], row[0:1, 513:514], AF.Sqrt)
            inst = se.activation(row[0:1, 515:516], row[0:1, 515:516], AF.Sqrt)
            ctx.inc(inst, "ea")

        @block.vector
        def _(ve):
            ve.wait_ge(ctx.sem("ea"), b_ea + 2)
            ctx.seq(ve, "vv", ve.scalar_tensor_tensor(row[0:1, 516:517], row[0:1, 512:513], 8.0,
                                                      row[0:1, 513:514], ALU.mult, ALU.add))
            ctx.seq(ve, "vv", ve.scalar_tensor_tensor(row[0:1, 517:518], row[0:1, 514:515], 8.0,
                                                      row[0:1, 515:516], ALU.mult, ALU.add))
            ctx.seq(ve, "vv", ve.tensor_mul(row[0:1, 518:519], row[0:1, 516:517], row[0:1, 517:518]))
            ctx.seq(ve, "vv", ve.tensor_scalar_mul(row[0:1, 518:519], row[0:1, 518:519], scale2))
            inst = ve.tensor_copy(ones_bf[0:1, 64:65], row[0:1, 518:519])
            ctx.inc(inst, "ev")

        @block.tensor
        def _(te):
            te.wait_ge(ctx.sem("ev"), b_ev + 2)
            inst = te.matmul(ps_small[:, 0:1], ones_bf[0:1, 0:64],
                             ones_bf[0:1, 64:65], start=True, stop=True)
            ctx.inc(inst, "mm")

        @block.scalar
        def _(se):
            se.wait_ge(ctx.sem("mm"), b_mm + 1)
            se.activation(negM0_col[0:64, 0:1], ps_small[0:64, 0:1], AF.Identity, scale=-1.0)
            inst = se.activation(negM0_col[64:128, 0:1], ps_small[0:64, 0:1],
                                 AF.Identity, scale=-1.0)
            ctx.inc(inst, "ea")


def emit_flash_head(ctx, bufs, h, par, q_fm, k_fm, v_tm, attn_out, load_wait_mm):
    nc = ctx.nc
    (q_sb, k_sb, vb_sb, e_bf, l_sb, rhl, rb_sb, ao_sb, stair_sb, negM0_col,
     ps_s4, ps_O, ps_rb) = bufs
    q_s = q_sb[64 * par:64 * (par + 1), :] if False else q_sb

    with nc.Block() as block:
        @block.gpsimd
        def _(gp):
            gp.wait_ge(ctx.sem("mm"), load_wait_mm)
            ctx.inc(gp.dma_start(out=q_sb[:, :], in_=q_fm[64 * h:64 * (h + 1), :]), "ld", 16)
            ctx.inc(gp.dma_start(out=k_sb[:, :], in_=k_fm[64 * h:64 * (h + 1), :]), "ld", 16)
            src = bass.AP(v_tm.tensor, v_tm.offset + h * T * HS,
                          [[HS, 128], [128 * HS, NS], [1, HS]])
            dst = bass.AP(vb_sb, 0, [[65 * NS, 128], [65, NS], [1, HS]])
            ctx.inc(gp.dma_start(out=dst, in_=src), "ld", 16)

    ld_done = ctx.v("ld")

    for qt in range(NT):
        nkt = 4 * qt + 4
        b_mm = ctx.v("mm")
        b_ea = ctx.v("ea")
        b_ev = ctx.v("ev")
        b_st = ctx.v("st")

        plan_te = []
        for kt in range(nkt):
            plan_te.append(("S", kt))
            if kt >= 1:
                plan_te.append(("AV", kt - 1))
            if kt == nkt - 1:
                plan_te.append(("AV", kt))
        mm_S = {}
        mm_AV = {}
        for i, (kind, kt) in enumerate(plan_te):
            (mm_S if kind == "S" else mm_AV)[kt] = b_mm + i + 1
        ea_exp = {kt: b_ea + kt + 1 for kt in range(nkt)}
        ev_msk = {kt: b_ev + (kt - 4 * qt) + 1 for kt in range(4 * qt, nkt)}

        with nc.Block() as block:
            @block.tensor
            def _(te, qt=qt, nkt=nkt, plan_te=plan_te):
                te.wait_ge(ctx.sem("ld"), ld_done)
                te.wait_ge(ctx.sem("ea"), b_ea)
                te.wait_ge(ctx.sem("ev"), b_ev)
                for kind, kt in plan_te:
                    if kind == "S":
                        ps = ps_s4[kt % 2]
                        if kt >= 2:
                            te.wait_ge(ctx.sem("ea"), ea_exp[kt - 2])
                        inst = te.matmul(ps[:, :], k_sb[:, 128 * kt:128 * (kt + 1)],
                                         q_sb[:, 512 * qt:512 * (qt + 1)],
                                         start=True, stop=True)
                    else:
                        kt_ = kt
                        te.wait_ge(ctx.sem("ea"), ea_exp[kt_])
                        if kt_ in ev_msk:
                            te.wait_ge(ctx.sem("ev"), ev_msk[kt_])
                        inst = te.matmul(ps_O[:, :], vb_sb[:, 65 * kt_:65 * (kt_ + 1)],
                                         e_bf[:, 512 * (kt_ % 2):512 * (kt_ % 2 + 1)],
                                         start=(kt_ == 0), stop=(kt_ == nkt - 1))
                    ctx.inc(inst, "mm")

            @block.scalar
            def _(se, qt=qt, nkt=nkt):
                for kt in range(nkt):
                    se.wait_ge(ctx.sem("mm"), mm_S[kt])
                    if kt >= 2:
                        se.wait_ge(ctx.sem("mm"), mm_AV[kt - 2])
                    inst = se.activation(e_bf[:, 512 * (kt % 2):512 * (kt % 2 + 1)],
                                         ps_s4[kt % 2][:, :],
                                         AF.Exp, bias=negM0_col[:, 0:1])
                    ctx.inc(inst, "ea")
                assert ctx.v("ea") == ea_exp[nkt - 1]

            @block.vector
            def _(ve, qt=qt, nkt=nkt):
                for kt in range(4 * qt, nkt):
                    j = kt - 4 * qt
                    ve.wait_ge(ctx.sem("ea"), ea_exp[kt])
                    sl = e_bf[:, 512 * (kt % 2):512 * (kt % 2 + 1)]
                    inst = ve.tensor_mul(sl, sl, stair_sb[:, 384 - 128 * j:896 - 128 * j])
                    ctx.inc(inst, "ev")
                assert ctx.v("ev") == ev_msk[nkt - 1]

            e_mm = ctx.v("mm")

            @block.scalar
            def _(se):
                se.wait_ge(ctx.sem("mm"), e_mm)
                inst = se.copy(l_sb[0:1, 0:512], ps_O[64:65, :])
                ctx.inc(inst, "ea")

            c_ea = ctx.v("ea")

            @block.vector
            def _(ve):
                ve.wait_ge(ctx.sem("ea"), c_ea)
                ctx.seq(ve, "vv", ve.tensor_scalar_add(l_sb[0:1, 0:512], l_sb[0:1, 0:512], 1e-30))
                ctx.seq(ve, "vv", ve.reciprocal(l_sb[0:1, 512:1024], l_sb[0:1, 0:512]))
                ctx.seq(ve, "vv", ve.tensor_copy(rhl[0:1, 0:512], l_sb[0:1, 512:1024]))
                inst = ve.tensor_sub(rhl[0:1, 512:1024], l_sb[0:1, 512:1024], rhl[0:1, 0:512])
                ctx.inc(inst, "ev")

            c_ev = ctx.v("ev")

            @block.tensor
            def _(te):
                te.wait_ge(ctx.sem("ev"), c_ev)
                te.wait_ge(ctx.sem("ea"), c_ea + 0)
                te.matmul(ps_rb[:, :], rhl[0:1, 1024:1088], rhl[0:1, 0:512],
                          start=True, stop=False)
                inst = te.matmul(ps_rb[:, :], rhl[0:1, 1024:1088], rhl[0:1, 512:1024],
                                 start=False, stop=True)
                ctx.inc(inst, "mm")

            c_mm = ctx.v("mm")

            @block.scalar
            def _(se):
                se.wait_ge(ctx.sem("mm"), c_mm)
                se.wait_ge(ctx.sem("st"), b_st)
                inst = se.copy(rb_sb[:, :], ps_rb[:, :])
                ctx.inc(inst, "ea")

            c_ea2 = ctx.v("ea")

            @block.vector
            def _(ve):
                ve.wait_ge(ctx.sem("ea"), c_ea2)
                inst = ve.tensor_mul(ao_sb[:, :], ps_O[0:64, :], rb_sb[:, :])
                ctx.inc(inst, "ev")

            c_ev2 = ctx.v("ev")

            @block.sync
            def _(sy, qt=qt, h=h):
                sy.wait_ge(ctx.sem("ev"), c_ev2)
                inst = sy.dma_start(out=attn_out[64 * h:64 * (h + 1), 512 * qt:512 * (qt + 1)],
                                    in_=ao_sb[:, :])
                ctx.inc(inst, "st", 16)


def emit_l0_attention(ctx, q_fm, k_fm, v_tm, gbrows, stair_in, attn_out):
    nc = ctx.nc
    q_sb = [ctx.cur_stack.enter_context(nc.sbuf_tensor(f"fa_q{i}", [64, T], BF16)) for i in range(2)]
    k_sb = [ctx.cur_stack.enter_context(nc.sbuf_tensor(f"fa_k{i}", [64, T], BF16)) for i in range(2)]
    vb_sb = [ctx.cur_stack.enter_context(nc.sbuf_tensor(f"fa_v{i}", [128, 65 * NS], BF16)) for i in range(2)]
    e_bf = ctx.cur_stack.enter_context(nc.sbuf_tensor("fa_e", [128, 4 * 512], BF16))
    l_sb = ctx.cur_stack.enter_context(nc.sbuf_tensor("fa_l", [1, 1024], F32))
    rhl = ctx.cur_stack.enter_context(nc.sbuf_tensor("fa_rhl", [1, 1088], BF16))
    rb_sb = ctx.cur_stack.enter_context(nc.sbuf_tensor("fa_rb", [64, 512], F32))
    ao_sb = ctx.cur_stack.enter_context(nc.sbuf_tensor("fa_ao", [64, 512], BF16))
    stair_sb = ctx.cur_stack.enter_context(nc.sbuf_tensor("fa_stair", [128, 896], BF16))
    negM0_col = ctx.cur_stack.enter_context(nc.sbuf_tensor("fa_negM0", [128, 1], F32))
    row = ctx.cur_stack.enter_context(nc.sbuf_tensor("fa_m0row", [1, 1024], F32))
    ones_bf = ctx.cur_stack.enter_context(nc.sbuf_tensor("fa_ones", [1, 65], BF16))

    ps_sA = ctx.cur_stack.enter_context(nc.psum_tensor("fa_psA", [128, 512], F32))
    ps_sB = ctx.cur_stack.enter_context(nc.psum_tensor("fa_psB", [128, 512], F32))
    ps_sC = ctx.cur_stack.enter_context(nc.psum_tensor("fa_psC", [128, 512], F32))
    ps_sD = ctx.cur_stack.enter_context(nc.psum_tensor("fa_psD", [128, 512], F32))
    ps_O = ctx.cur_stack.enter_context(nc.psum_tensor("fa_psO", [65, 512], F32))
    ps_rb = ctx.cur_stack.enter_context(nc.psum_tensor("fa_psrb", [64, 512], F32))
    ps_small = ctx.cur_stack.enter_context(nc.psum_tensor("fa_pssm", [64, 1], F32))

    with nc.Block() as block:
        @block.gpsimd
        def _(gp):
            ctx.inc(gp.dma_start(out=row[0:1, 0:256], in_=gbrows), "ld", 16)
            ctx.inc(gp.dma_start(out=stair_sb[:, :], in_=stair_in), "ld", 16)
            gp.memset(vb_sb[0][:, :], 1.0)
            gp.memset(vb_sb[1][:, :], 1.0)
            gp.memset(rhl[0:1, 1024:1088], 1.0)
            inst = gp.memset(ones_bf[:, :], 1.0)
            ctx.inc(inst, "gpc")

    ldv = ctx.v("ld")
    gpv = ctx.v("gpc")
    with nc.Block() as block:
        @block.scalar
        def _(se):
            se.wait_ge(ctx.sem("ld"), ldv)
            se.wait_ge(ctx.sem("gpc"), gpv)
        @block.tensor
        def _(te):
            te.wait_ge(ctx.sem("ld"), ldv)
            te.wait_ge(ctx.sem("gpc"), gpv)

    emit_m0_prep(ctx, row, negM0_col, ps_small, ones_bf, float(D) ** (-0.5))

    head_mm_start = []
    for h in range(H):
        par = h % 2
        bufs = (q_sb[par], k_sb[par], vb_sb[par], e_bf, l_sb, rhl, rb_sb, ao_sb,
                stair_sb, negM0_col, (ps_sA, ps_sB, ps_sC, ps_sD), ps_O, ps_rb)
        load_wait = head_mm_start[h - 1] if h >= 1 else 0
        head_mm_start.append(ctx.v("mm"))
        emit_flash_head(ctx, bufs, h, par, q_fm, k_fm, v_tm, attn_out, load_wait)


def emit_p2(ctx, stack, io):
    nc = ctx.nc
    ctx.cur_stack = stack

    q_fm = io["q_fm"]
    k_fm = io["k_fm"]
    v_tm = io["v_tm"]
    gbrows = io["gbrows"]
    stair_in = io["C_stair"]
    attn_out = io["attn_out"]

    if True:
        emit_l0_attention(ctx, q_fm, k_fm, v_tm, gbrows, stair_in, attn_out)
        with nc.Block() as block:
            @block.sync
            def _(sy):
                sy.wait_ge(ctx.sem("st"), ctx.v("st"))
    return



class PostLN:
    def __init__(self, ctx, name, n_add=1, bias_pc=None, eps=1e-5, scale_rows=None,
                 with_producer=True, ps_share=None, sb_share=None):
        nc = ctx.nc
        _alloc = ctx.cur_stack.enter_context
        self.ctx = ctx
        self.name = name
        self.n_add = n_add
        self.bias_pc = bias_pc
        self.eps = eps
        self.with_producer = with_producer
        self.scale_rows = scale_rows
        sb_share = sb_share or {}
        self.res_sb = [sb_share.get(f"res{i}") or
                       _alloc(nc.sbuf_tensor(f"{name}_res{i}", [128, NC * 512], F32))
                       for i in range(n_add)]
        self.s32 = sb_share.get("s32") or _alloc(nc.sbuf_tensor(f"{name}_s32", [128, NC * 512], F32))
        sb_share0 = sb_share or {}
        self.sbf = sb_share0.get("sbf") or _alloc(nc.sbuf_tensor(f"{name}_sbf", [128, 512], BF16))
        self.sqbf = sb_share0.get("sqbf") or _alloc(nc.sbuf_tensor(f"{name}_sqbf", [128, 512], BF16))
        self.row = _alloc(nc.sbuf_tensor(f"{name}_row", [1, (4 if scale_rows is not None else 3) * 512], F32))
        self.rowb = sb_share.get("rowb") or _alloc(nc.sbuf_tensor(f"{name}_rowb", [1, 4 * 512], BF16))
        self.ones128 = _alloc(nc.sbuf_tensor(f"{name}_ones", [128, 1], BF16))
        self.onesrow = _alloc(nc.sbuf_tensor(f"{name}_onesrow", [1, 128], BF16))
        self.rb32 = sb_share.get("rb32") or _alloc(nc.sbuf_tensor(f"{name}_rb32", [128, 512], F32))
        self.sb_share_ = sb_share = sb_share or {}
        self.xbf = sb_share.get("xbf") or _alloc(nc.sbuf_tensor(f"{name}_xbf", [128, NC * 512], BF16))
        self.srow = (sb_share.get("srow") or _alloc(nc.sbuf_tensor(f"{name}_srow", [128, 512], F32))) if scale_rows is not None else None
        self.debug_sum_ap = None
        self.debug_row_ap = None
        if ps_share is None:
            ps_share = {}
        if with_producer:
            self.ps_x = ps_share.get("ps_x") or [
                _alloc(nc.psum_tensor(f"{name}_psx{i}", [128, 512], F32)) for i in range(2)]
        else:
            self.ps_x = None
        self.ps_st = ps_share.get("ps_st") or _alloc(nc.psum_tensor(f"{name}_psst", [1, 1024], F32))
        self.ps_mb = ps_share.get("ps_mb") or _alloc(nc.psum_tensor(f"{name}_psmb", [128, 512], F32))
        self.ps_rb = ps_share.get("ps_rb") or _alloc(nc.psum_tensor(f"{name}_psrb", [128, 512], F32))

    def emit_init(self):
        ctx = self.ctx
        with ctx.nc.Block() as block:
            @block.gpsimd
            def _(gp):
                gp.memset(self.ones128[:, :], 1.0)
                inst = gp.memset(self.onesrow[:, :], 1.0)
                ctx.inc(inst, "gpc")

    def emit(self, producer, rhs_loader, add_srcs, out32_ap, outbf_ap, bias_pc_sb=None,
             pre_tile=None, extra_gp_wait=None, scale_pc_sb=None, tts=None):
        ctx = self.ctx
        nc = ctx.nc
        gpc_v = ctx.v("gpc")
        for tt in (range(NT) if tts is None else tts):
            if pre_tile is not None:
                pre_tile(tt)
            t0, t1 = 512 * tt, 512 * (tt + 1)
            b_mm = ctx.v("mm")
            b_ea = ctx.v("ea")
            b_ev = ctx.v("ev")
            b_st = ctx.v("st")
            b_ld = ctx.v("ldx")
            ld_mid = [None]

            with nc.Block() as block:
                @block.gpsimd
                def _(gp, tt=tt):
                    gp.wait_ge(ctx.sem("ev"), b_ev)
                    if extra_gp_wait is not None:
                        nm, val = extra_gp_wait()
                        gp.wait_ge(ctx.sem(nm), val)
                    for i, src in enumerate(add_srcs):
                        for fc in range(NC):
                            inst = gp.dma_start(out=self.res_sb[i][:, 512 * fc:512 * (fc + 1)],
                                                in_=src[128 * fc:128 * (fc + 1), t0:t1])
                            ctx.inc(inst, "ldx", 16)
                    if self.scale_rows is not None:
                        inst = gp.dma_start(out=self.row[0:1, 1536:2048],
                                            in_=self.scale_rows[0:1, t0:t1])
                        ctx.inc(inst, "ldx", 16)
                    ld_mid[0] = ctx.v("ldx")
                    rhs_loader(gp, tt)

                ld_v = ld_mid[0]

                if self.scale_rows is not None:
                    @block.vector
                    def _(ve, tt=tt):
                        ve.wait_ge(ctx.sem("ldx"), ld_v)
                        inst = ve.tensor_copy(self.rowb[0:1, 1536:2048], self.row[0:1, 1536:2048])
                        ctx.inc(inst, "ev")

                    @block.tensor
                    def _(te, tt=tt):
                        te.wait_ge(ctx.sem("ev"), ctx.v("ev"))
                        te.wait_ge(ctx.sem("gpc"), gpc_v)
                        te.wait_ge(ctx.sem("ea"), b_ea)
                        inst = te.matmul(self.ps_rb[:, :], self.onesrow[0:1, :],
                                         self.rowb[0:1, 1536:2048], start=True, stop=True)
                        ctx.inc(inst, "mm")

                    @block.scalar
                    def _(se, tt=tt):
                        se.wait_ge(ctx.sem("mm"), ctx.v("mm"))
                        inst = se.copy(self.srow[:, :], self.ps_rb[:, :])
                        ctx.inc(inst, "ea")

                b2_mm = ctx.v("mm")
                b2_ea = ctx.v("ea")

                if self.with_producer:
                    @block.tensor
                    def _(te, tt=tt):
                        te.wait_ge(ctx.sem("ldx"), ld_v)
                        te.wait_ge(ctx.sem("gpc"), gpc_v)
                        te.wait_ge(ctx.sem("ea"), b_ea)
                        for fc in range(NC):
                            ps = self.ps_x[fc % 2]
                            if fc >= 2:
                                te.wait_ge(ctx.sem("ea"), b2_ea + (fc - 1))
                            inst = producer(te, fc, tt, ps)
                            ctx.inc(inst, "mm")

                    @block.scalar
                    def _(se, tt=tt):
                        se.wait_ge(ctx.sem("st"), b_st)
                        se.wait_ge(ctx.sem("ev"), b_ev)
                        for fc in range(NC):
                            ps = self.ps_x[fc % 2]
                            se.wait_ge(ctx.sem("mm"), b2_mm + fc + 1)
                            if bias_pc_sb is not None:
                                inst = se.activation(self.s32[:, 512 * fc:512 * (fc + 1)], ps[:, :],
                                                     AF.Identity, bias=bias_pc_sb[:, fc:fc + 1])
                            elif scale_pc_sb is not None:
                                inst = se.activation(self.s32[:, 512 * fc:512 * (fc + 1)], ps[:, :],
                                                     AF.Identity, scale=scale_pc_sb[:, fc:fc + 1])
                            else:
                                inst = se.copy(self.s32[:, 512 * fc:512 * (fc + 1)], ps[:, :])
                            ctx.inc(inst, "ea", 1)

                @block.vector
                def _(ve, tt=tt):
                    stats_mm0 = b2_mm + (NC if self.with_producer else 0)
                    ve.wait_ge(ctx.sem("ldx"), ld_v)
                    for fc in range(NC):
                        if self.with_producer:
                            ve.wait_ge(ctx.sem("ea"), b2_ea + fc + 1)
                        if fc >= 1:
                            ve.wait_ge(ctx.sem("mm"), stats_mm0 + fc)
                        sl = self.s32[:, 512 * fc:512 * (fc + 1)]
                        if self.scale_rows is not None:
                            ctx.seq(ve, "vv", ve.tensor_mul(sl, sl, self.srow[:, :]))
                        for i in range(len(add_srcs)):
                            ctx.seq(ve, "vv", ve.tensor_add(sl, sl,
                                                            self.res_sb[i][:, 512 * fc:512 * (fc + 1)]))
                        inst = ve.tensor_copy(self.sbf[:, :], sl)
                        ctx.inc(inst, "ev", 1)
                        inst = ve.tensor_mul(self.sqbf[:, :], sl, sl)
                        ctx.inc(inst, "ev", 1)
                    pass

                @block.tensor
                def _(te, tt=tt):
                    for fc in range(NC):
                        te.wait_ge(ctx.sem("ev"), b_ev + (1 if self.scale_rows is not None else 0) + 2 * (fc + 1))
                        te.matmul(self.ps_st[0:1, 0:512], self.ones128[:, 0:1],
                                  self.sbf[:, :], start=(fc == 0), stop=(fc == NC - 1))
                        inst = te.matmul(self.ps_st[0:1, 512:1024], self.ones128[:, 0:1],
                                         self.sqbf[:, :], start=(fc == 0), stop=(fc == NC - 1))
                        ctx.inc(inst, "mm")

                c_mm = ctx.v("mm")

                if self.debug_sum_ap is not None:
                    dbg_ev = b_ev + (1 if self.scale_rows is not None else 0) + 2 * NC

                    @block.sync
                    def _(sy, tt=tt):
                        sy.wait_ge(ctx.sem("ev"), dbg_ev)
                        for fc in range(NC):
                            inst = sy.dma_start(
                                out=self.debug_sum_ap[128 * fc:128 * (fc + 1), t0:t1],
                                in_=self.s32[:, 512 * fc:512 * (fc + 1)])
                            ctx.inc(inst, "st", 16)
                dbg_st = ctx.v("st")

                @block.vector
                def _(ve, tt=tt):
                    ve.wait_ge(ctx.sem("mm"), c_mm)
                    if self.debug_sum_ap is not None:
                        ve.wait_ge(ctx.sem("st"), dbg_st)
                    ctx.seq(ve, "vv", ve.tensor_scalar_mul(self.row[0:1, 0:512],
                                                           self.ps_st[0:1, 0:512], 1.0 / D))
                    ctx.seq(ve, "vv", ve.tensor_mul(self.row[0:1, 512:1024], self.row[0:1, 0:512],
                                                    self.row[0:1, 0:512]))
                    ctx.seq(ve, "vv", ve.scalar_tensor_tensor(self.row[0:1, 512:1024],
                                                              self.ps_st[0:1, 512:1024],
                                                              1.0 / D, self.row[0:1, 512:1024],
                                                              ALU.mult, ALU.subtract))
                    inst = ve.tensor_scalar_add(self.row[0:1, 512:1024],
                                                self.row[0:1, 512:1024], self.eps)
                    ctx.inc(inst, "ev")

                c_ev = ctx.v("ev")

                @block.scalar
                def _(se, tt=tt):
                    se.wait_ge(ctx.sem("ev"), c_ev)
                    inst = se.activation(self.row[0:1, 512:1024], self.row[0:1, 512:1024], AF.Sqrt)
                    ctx.inc(inst, "ea")

                c_ea = ctx.v("ea")

                @block.vector
                def _(ve, tt=tt):
                    ve.wait_ge(ctx.sem("ea"), c_ea)
                    ctx.seq(ve, "vv", ve.reciprocal(self.row[0:1, 1024:1536],
                                                    self.row[0:1, 512:1024]))
                    ctx.seq(ve, "vv", ve.tensor_copy(self.rowb[0:1, 0:512], self.row[0:1, 0:512]))
                    ctx.seq(ve, "vv", ve.tensor_sub(self.rowb[0:1, 512:1024], self.row[0:1, 0:512],
                                                    self.rowb[0:1, 0:512]))
                    ctx.seq(ve, "vv", ve.tensor_copy(self.rowb[0:1, 1024:1536],
                                                     self.row[0:1, 1024:1536]))
                    inst = ve.tensor_sub(self.rowb[0:1, 1536:2048], self.row[0:1, 1024:1536],
                                         self.rowb[0:1, 1024:1536])
                    ctx.inc(inst, "ev")

                c_ev2 = ctx.v("ev")

                if self.debug_row_ap is not None:
                    @block.sync
                    def _(sy, tt=tt):
                        sy.wait_ge(ctx.sem("ev"), c_ev2)
                        inst = sy.dma_start(out=self.debug_row_ap[tt:tt + 1, 0:1536],
                                            in_=self.row[0:1, 0:1536])
                        ctx.inc(inst, "st", 16)

                @block.tensor
                def _(te, tt=tt):
                    te.wait_ge(ctx.sem("ev"), c_ev2)
                    te.wait_ge(ctx.sem("ea"), b2_ea + (NC if self.with_producer else 0))
                    te.matmul(self.ps_mb[:, :], self.onesrow[0:1, :], self.rowb[0:1, 0:512],
                              start=True, stop=False)
                    te.matmul(self.ps_mb[:, :], self.onesrow[0:1, :], self.rowb[0:1, 512:1024],
                              start=False, stop=True)
                    te.matmul(self.ps_rb[:, :], self.onesrow[0:1, :], self.rowb[0:1, 1024:1536],
                              start=True, stop=False)
                    inst = te.matmul(self.ps_rb[:, :], self.onesrow[0:1, :],
                                     self.rowb[0:1, 1536:2048], start=False, stop=True)
                    ctx.inc(inst, "mm")

                c_mm2 = ctx.v("mm")

                @block.scalar
                def _(se, tt=tt):
                    se.wait_ge(ctx.sem("mm"), c_mm2)
                    inst = se.copy(self.rb32[:, :], self.ps_rb[:, :])
                    ctx.inc(inst, "ea")

                c_ea2 = ctx.v("ea")

                @block.vector
                def _(ve, tt=tt):
                    ve.wait_ge(ctx.sem("ea"), c_ea2)
                    for fc in range(NC):
                        sl = self.s32[:, 512 * fc:512 * (fc + 1)]
                        ctx.seq(ve, "vv", ve.tensor_sub(sl, sl, self.ps_mb[:, :]))
                        if outbf_ap is not None:
                            ctx.seq(ve, "vv", ve.tensor_mul(sl, sl, self.rb32[:, :]))
                            inst = ve.tensor_copy(self.xbf[:, 512 * fc:512 * (fc + 1)], sl)
                        else:
                            inst = ve.tensor_mul(sl, sl, self.rb32[:, :])
                        ctx.inc(inst, "ev", 1)

                c_ev3 = ctx.v("ev")

                @block.sync
                def _(sy, tt=tt):
                    sy.wait_ge(ctx.sem("ev"), c_ev3)
                    for fc in range(NC):
                        if out32_ap is not None:
                            inst = sy.dma_start(out=out32_ap[128 * fc:128 * (fc + 1), t0:t1],
                                                in_=self.s32[:, 512 * fc:512 * (fc + 1)])
                            ctx.inc(inst, "st", 16)
                        if outbf_ap is not None:
                            inst = sy.dma_start(out=outbf_ap[128 * fc:128 * (fc + 1), t0:t1],
                                                in_=self.xbf[:, 512 * fc:512 * (fc + 1)])
                            ctx.inc(inst, "st", 16)



def emit_p3(ctx, stack, io):
    nc = ctx.nc
    ctx.cur_stack = stack

    attn_out = io["attn_out"]
    WoT = io["WoT"]
    bo_pc = io["bo_pc"]
    xT = io["xT"]
    x1_32 = io["x1_32"]
    x1_bf = io["x1_bf"]

    wo_sb = stack.enter_context(nc.sbuf_tensor("wo_sb", [128, NC * D], BF16))
    bo_sb = stack.enter_context(nc.sbuf_tensor("bo_sb", [128, NC], F32))
    ao_sb = stack.enter_context(nc.sbuf_tensor("ao_sb", [128, NC * 512], BF16))

    if True:
        with nc.Block() as block:
            @block.gpsimd
            def _(gp):
                for i in range(NC):
                    ctx.inc(gp.dma_start(out=wo_sb[:, D * i:D * (i + 1)],
                                         in_=WoT[128 * i:128 * (i + 1), :]), "ld", 16)
                ctx.inc(gp.dma_start(out=bo_sb[:, :], in_=bo_pc), "ld", 16)

        ldv = ctx.v("ld")
        pl = PostLN(ctx, "p3", n_add=1)
        import os
        if os.environ.get("P3DEBUG"):
            pl.debug_sum_ap = io["dbg_sum"]
            pl.debug_row_ap = io["dbg_row"]
        pl.emit_init()

        p3_state = {}

        def producer(te, fc, tt, ps):
            te.wait_ge(ctx.sem("ld"), ldv)
            te.wait_ge(ctx.sem("ldx"), p3_state["ld"])
            return mm_accum(te, ps[:, :],
                            [(wo_sb[:, D * ec + 128 * fc:D * ec + 128 * (fc + 1)],
                              ao_sb[:, 512 * ec:512 * (ec + 1)]) for ec in range(NC)])

        def rhs_loader(gp, tt):
            for ec in range(NC):
                ctx.inc(gp.dma_start(out=ao_sb[:, 512 * ec:512 * (ec + 1)],
                                     in_=attn_out[128 * ec:128 * (ec + 1),
                                                  512 * tt:512 * (tt + 1)]), "ldx", 16)
            p3_state["ld"] = ctx.v("ldx")

        pl.emit(producer, rhs_loader, [xT], x1_32, x1_bf, bias_pc_sb=bo_sb)

        with nc.Block() as block:
            @block.sync
            def _(sy):
                sy.wait_ge(ctx.sem("st"), ctx.v("st"))
    return



def emit_p4a(ctx, stack, io):
    nc = ctx.nc
    ctx.cur_stack = stack

    slot_flat = io["slot_flat"]
    gs_row = io["gs_row"]
    gx_pc = io["gx_pc"]
    x1_bf = io["x1_bf"]

    se_tm = io["se_tm"]
    se_fm = io["se_fm"]
    xn_fm = io["xn_fm"]
    xn_tm = io["xn_tm"]
    m1_out = io["m1_out"]

    grow_sb = stack.enter_context(nc.sbuf_tensor("grow_sb", [1, D], F32))
    growb_sb = stack.enter_context(nc.sbuf_tensor("growb_sb", [1, D], BF16))
    onesrow = stack.enter_context(nc.sbuf_tensor("p4_onesrow", [1, 128], BF16))
    ones128 = stack.enter_context(nc.sbuf_tensor("p4_ones128", [128, 1], BF16))
    gsl32 = stack.enter_context(nc.sbuf_tensor("gsl32", [128, D], F32))
    gx_sb = stack.enter_context(nc.sbuf_tensor("gx_sb", [128, NC], F32))
    eye_sb = stack.enter_context(nc.sbuf_tensor("p4_eye", [128, 128], BF16))

    tile32 = stack.enter_context(nc.sbuf_tensor("tile32", [128, D], F32))
    sq32 = stack.enter_context(nc.sbuf_tensor("sq32", [128, D], F32))
    col_sb = stack.enter_context(nc.sbuf_tensor("col_sb", [128, 4], F32))
    sebf_sb = stack.enter_context(nc.sbuf_tensor("sebf_sb", [128, D], BF16))

    x1t_sb = stack.enter_context(nc.sbuf_tensor("x1t_sb", [128, NC * 512], BF16))
    sqbf_sb = stack.enter_context(nc.sbuf_tensor("p4_sqbf", [128, 512], BF16))
    row_sb = stack.enter_context(nc.sbuf_tensor("p4_row", [1, 4 * 512], F32))
    rowb_sb = stack.enter_context(nc.sbuf_tensor("p4_rowb", [1, 2 * 512], BF16))
    rb32 = stack.enter_context(nc.sbuf_tensor("p4_rb32", [128, 512], F32))
    t1_sb = stack.enter_context(nc.sbuf_tensor("p4_t1", [128, 512], F32))
    xnbf_sb = stack.enter_context(nc.sbuf_tensor("xnbf_sb", [128, NC * 512], BF16))

    sef_sb = [stack.enter_context(nc.sbuf_tensor(f"sef{i}", [128, T], BF16)) for i in range(NC)]
    xnf_sb = stack.enter_context(nc.sbuf_tensor("xnf_sb", [128, NC * 512], BF16))
    trans_sb = stack.enter_context(nc.sbuf_tensor("trans_sb", [128, 1024], BF16))
    m1cols = stack.enter_context(nc.sbuf_tensor("m1cols", [128, 32], F32))
    m1colb = stack.enter_context(nc.sbuf_tensor("m1colb", [128, 32], BF16))
    m1T_sb = stack.enter_context(nc.sbuf_tensor("m1T_sb", [32, 128], F32))
    mx_sb = stack.enter_context(nc.sbuf_tensor("mx_sb", [128, 1], F32))

    ps_a = stack.enter_context(nc.psum_tensor("p4_psa", [128, 512], F32))
    ps_b = stack.enter_context(nc.psum_tensor("p4_psb", [128, 512], F32))
    ps_g = stack.enter_context(nc.psum_tensor("p4_psg", [128, 1024], F32))
    ps_st = stack.enter_context(nc.psum_tensor("p4_psst", [1, 512], F32))
    ps_tr = stack.enter_context(nc.psum_tensor("p4_pstr", [32, 128], BF16))

    EPS = 1e-12
    RT = 32.0

    if True:
        with nc.Block() as block:
            @block.gpsimd
            def _(gp):
                ctx.inc(gp.dma_start(out=grow_sb[:, :], in_=gs_row), "ld", 16)
                ctx.inc(gp.dma_start(out=gx_sb[:, :], in_=gx_pc), "ld", 16)
                gp.memset(onesrow[:, :], 1.0)
                gp.memset(ones128[:, :], 1.0)
                gp.memset(eye_sb[:, :], 0.0)
                inst = gp.memset(m1cols[:, :], -3.0e38)
                ctx.inc(inst, "gpc")

        eye_in = io["C_eye"]
        with nc.Block() as block:
            @block.gpsimd
            def _(gp):
                ctx.inc(gp.dma_start(out=eye_sb[:, :], in_=eye_in), "ld", 16)

        ldv = ctx.v("ld")
        gpv = ctx.v("gpc")

        with nc.Block() as block:
            @block.vector
            def _(ve):
                ve.wait_ge(ctx.sem("ld"), ldv)
                inst = ve.tensor_copy(growb_sb[:, :], grow_sb[:, :])
                ctx.inc(inst, "ev")

            c_ev = ctx.v("ev")

            @block.tensor
            def _(te):
                te.wait_ge(ctx.sem("ev"), c_ev)
                te.wait_ge(ctx.sem("gpc"), gpv)
                te.matmul(ps_g[:, 0:512], onesrow[0:1, :], growb_sb[0:1, 0:512],
                          start=True, stop=True)
                inst = te.matmul(ps_g[:, 512:1024], onesrow[0:1, :], growb_sb[0:1, 512:1024],
                                 start=True, stop=True)
                ctx.inc(inst, "mm")

            c_mm = ctx.v("mm")

            @block.scalar
            def _(se):
                se.wait_ge(ctx.sem("mm"), c_mm)
                inst = se.copy(gsl32[:, :], ps_g[:, :])
                ctx.inc(inst, "ea")

        for ts in range(NS):
            b_ld = ctx.v("ldx")
            b_ea = ctx.v("ea")
            b_ev = ctx.v("ev")
            b_st = ctx.v("st")
            with nc.Block() as block:
                @block.gpsimd
                def _(gp, ts=ts):
                    gp.wait_ge(ctx.sem("ev"), b_ev)
                    inst = gp.dma_start(out=tile32[:, :],
                                        in_=slot_flat[128 * ts:128 * (ts + 1), :])
                    ctx.inc(inst, "ldx", 16)

                ld_v = ctx.v("ldx")

                @block.scalar
                def _(se, ts=ts):
                    se.wait_ge(ctx.sem("ldx"), ld_v)
                    inst = se.activation(sq32[:, :], tile32[:, :], AF.Square)
                    ctx.inc(inst, "ea")

                @block.vector
                def _(ve, ts=ts):
                    ve.wait_ge(ctx.sem("ea"), b_ea + 1)
                    inst = ve.reduce_sum(col_sb[:, 0:1], sq32[:, :], axis=AX.X)
                    ctx.inc(inst, "ev")

                @block.scalar
                def _(se, ts=ts):
                    se.wait_ge(ctx.sem("ev"), b_ev + 1)
                    inst = se.activation(col_sb[:, 1:2], col_sb[:, 0:1], AF.Sqrt)
                    ctx.inc(inst, "ea")

                @block.vector
                def _(ve, ts=ts):
                    ve.wait_ge(ctx.sem("ea"), b_ea + 2)
                    ctx.seq(ve, "vv", ve.tensor_scalar_max(col_sb[:, 1:2], col_sb[:, 1:2], EPS))
                    ctx.seq(ve, "vv", ve.reciprocal(col_sb[:, 2:3], col_sb[:, 1:2]))
                    inst = ve.tensor_scalar_mul(col_sb[:, 2:3], col_sb[:, 2:3], RT)
                    ctx.inc(inst, "ev")

                @block.scalar
                def _(se, ts=ts):
                    se.wait_ge(ctx.sem("ev"), b_ev + 2)
                    inst = se.activation(tile32[:, :], tile32[:, :], AF.Identity,
                                         scale=col_sb[:, 2:3])
                    ctx.inc(inst, "ea")

                @block.vector
                def _(ve, ts=ts):
                    ve.wait_ge(ctx.sem("ea"), b_ea + 3)
                    ve.wait_ge(ctx.sem("st"), b_st)
                    inst = ve.tensor_mul(sebf_sb[:, :], tile32[:, :], gsl32[:, :])
                    ctx.inc(inst, "ev")

                c_ev2 = ctx.v("ev")

                @block.sync
                def _(sy, ts=ts):
                    sy.wait_ge(ctx.sem("ev"), c_ev2)
                    inst = sy.dma_start(out=se_tm[128 * ts:128 * (ts + 1), :], in_=sebf_sb[:, :])
                    ctx.inc(inst, "st", 16)

        st_se = ctx.v("st")
        for dc in range(NC):
            with nc.Block() as block:
                @block.sync
                def _(sy, dc=dc):
                    if dc == 0:
                        sy.wait_ge(ctx.sem("st"), st_se)
                    inst = sy.dma_start_transpose(sef_sb[dc][:, :],
                                                  se_tm[:, 128 * dc:128 * (dc + 1)])
                    ctx.inc(inst, "ldx", 16)

                ld_v = ctx.v("ldx")

                @block.sync
                def _(sy, dc=dc):
                    sy.wait_ge(ctx.sem("ldx"), ld_v)
                    inst = sy.dma_start(out=se_fm[128 * dc:128 * (dc + 1), :], in_=sef_sb[dc][:, :])
                    ctx.inc(inst, "st", 16)

        for tt in range(NT):
            t0, t1 = 512 * tt, 512 * (tt + 1)
            b_mm = ctx.v("mm")
            b_ea = ctx.v("ea")
            b_ev = ctx.v("ev")
            b_st = ctx.v("st")
            with nc.Block() as block:
                @block.gpsimd
                def _(gp, tt=tt):
                    gp.wait_ge(ctx.sem("mm"), b_mm)
                    gp.wait_ge(ctx.sem("ev"), b_ev)
                    for fc in range(NC):
                        inst = gp.dma_start(out=x1t_sb[:, 512 * fc:512 * (fc + 1)],
                                            in_=x1_bf[128 * fc:128 * (fc + 1), t0:t1])
                        ctx.inc(inst, "ldx", 16)

                ld_v = ctx.v("ldx")

                @block.scalar
                def _(se, tt=tt):
                    se.wait_ge(ctx.sem("ldx"), ld_v)
                    for fc in range(NC):
                        if fc >= 1:
                            se.wait_ge(ctx.sem("mm"), b_mm + fc)
                        inst = se.activation(sqbf_sb[:, :], x1t_sb[:, 512 * fc:512 * (fc + 1)],
                                             AF.Square)
                        ctx.inc(inst, "ea")

                @block.tensor
                def _(te, tt=tt):
                    for fc in range(NC):
                        te.wait_ge(ctx.sem("ea"), b_ea + fc + 1)
                        inst = te.matmul(ps_st[0:1, :], ones128[:, 0:1], sqbf_sb[:, :],
                                         start=(fc == 0), stop=(fc == NC - 1))
                        ctx.inc(inst, "mm")

                c_mm = ctx.v("mm")

                @block.scalar
                def _(se, tt=tt):
                    se.wait_ge(ctx.sem("mm"), c_mm)
                    inst = se.activation(row_sb[0:1, 0:512], ps_st[0:1, :], AF.Sqrt)
                    ctx.inc(inst, "ea")

                c_ea = ctx.v("ea")

                @block.vector
                def _(ve, tt=tt):
                    ve.wait_ge(ctx.sem("ea"), c_ea)
                    ctx.seq(ve, "vv", ve.tensor_scalar_max(row_sb[0:1, 0:512], row_sb[0:1, 0:512], EPS))
                    ctx.seq(ve, "vv", ve.reciprocal(row_sb[0:1, 512:1024], row_sb[0:1, 0:512]))
                    ctx.seq(ve, "vv", ve.tensor_scalar_mul(row_sb[0:1, 512:1024],
                                                           row_sb[0:1, 512:1024], RT))
                    ctx.seq(ve, "vv", ve.tensor_copy(rowb_sb[0:1, 0:512], row_sb[0:1, 512:1024]))
                    inst = ve.tensor_sub(rowb_sb[0:1, 512:1024], row_sb[0:1, 512:1024],
                                         rowb_sb[0:1, 0:512])
                    ctx.inc(inst, "ev")

                c_ev = ctx.v("ev")

                @block.tensor
                def _(te, tt=tt):
                    te.wait_ge(ctx.sem("ev"), c_ev)
                    te.matmul(ps_a[:, :], onesrow[0:1, :], rowb_sb[0:1, 0:512],
                              start=True, stop=False)
                    inst = te.matmul(ps_a[:, :], onesrow[0:1, :], rowb_sb[0:1, 512:1024],
                                     start=False, stop=True)
                    ctx.inc(inst, "mm")

                c_mm2 = ctx.v("mm")

                @block.scalar
                def _(se, tt=tt):
                    se.wait_ge(ctx.sem("mm"), c_mm2)
                    inst = se.copy(rb32[:, :], ps_a[:, :])
                    ctx.inc(inst, "ea")

                c_ea2 = ctx.v("ea")

                @block.vector
                def _(ve, tt=tt):
                    for fc in range(NC):
                        ve.wait_ge(ctx.sem("ea"), c_ea2 + fc)
                        inst = ve.tensor_mul(t1_sb[:, :], x1t_sb[:, 512 * fc:512 * (fc + 1)],
                                             rb32[:, :])
                        ctx.inc(inst, "ev")

                @block.scalar
                def _(se, tt=tt):
                    for fc in range(NC):
                        se.wait_ge(ctx.sem("ev"), c_ev + fc + 1)
                        inst = se.activation(xnbf_sb[:, 512 * fc:512 * (fc + 1)], t1_sb[:, :],
                                             AF.Identity, scale=gx_sb[:, fc:fc + 1])
                        ctx.inc(inst, "ea")

                last_ea = ctx.v("ea")

                @block.sync
                def _(sy, tt=tt):
                    sy.wait_ge(ctx.sem("ea"), last_ea)
                    for fc in range(NC):
                        inst = sy.dma_start(out=xn_fm[128 * fc:128 * (fc + 1), t0:t1],
                                            in_=xnbf_sb[:, 512 * fc:512 * (fc + 1)])
                        ctx.inc(inst, "st", 16)

        st_xn = ctx.v("st")
        for ts in range(NS):
            b_st_ts = ctx.v("st")
            with nc.Block() as block:
                @block.sync
                def _(sy, ts=ts):
                    sy.wait_ge(ctx.sem("st"), b_st_ts if ts > 0 else st_xn)
                    inst = sy.dma_start_transpose(trans_sb[:, :],
                                                  xn_fm[:, 128 * ts:128 * (ts + 1)])
                    ctx.inc(inst, "ldx", 16)

                ld_v = ctx.v("ldx")

                @block.sync
                def _(sy, ts=ts):
                    sy.wait_ge(ctx.sem("ldx"), ld_v)
                    inst = sy.dma_start(out=xn_tm[128 * ts:128 * (ts + 1), :], in_=trans_sb[:, :])
                    ctx.inc(inst, "st", 16)

        for sg in range(NT):
            t0, t1 = 512 * sg, 512 * (sg + 1)
            b_mm = ctx.v("mm")
            b_ev = ctx.v("ev")
            with nc.Block() as block:
                @block.gpsimd
                def _(gp, sg=sg):
                    gp.wait_ge(ctx.sem("mm"), b_mm)
                    for dc in range(NC):
                        inst = gp.dma_start(out=xnf_sb[:, 512 * dc:512 * (dc + 1)],
                                            in_=xn_fm[128 * dc:128 * (dc + 1), t0:t1])
                        ctx.inc(inst, "ldx", 16)

                ld_v = ctx.v("ldx")

                @block.tensor
                def _(te, sg=sg):
                    te.wait_ge(ctx.sem("ldx"), ld_v)
                    for nc_i in range(32):
                        ps = ps_a if nc_i % 2 == 0 else ps_b
                        if nc_i >= 2:
                            te.wait_ge(ctx.sem("ev"), b_ev + nc_i - 1)
                        inst = mm_accum(te, ps[:, :],
                                        [(sef_sb[dc][:, 128 * nc_i:128 * (nc_i + 1)],
                                          xnf_sb[:, 512 * dc:512 * (dc + 1)]) for dc in range(NC)])
                        ctx.inc(inst, "mm")

                @block.vector
                def _(ve, sg=sg):
                    ve.wait_ge(ctx.sem("gpc"), gpv)
                    for nc_i in range(32):
                        ps = ps_a if nc_i % 2 == 0 else ps_b
                        ve.wait_ge(ctx.sem("mm"), b_mm + nc_i + 1)
                        ctx.seq(ve, "vv", ve.reduce_max(mx_sb[:, 0:1], ps[:, :], axis=AX.X))
                        inst = ve.tensor_max(m1cols[:, nc_i:nc_i + 1], m1cols[:, nc_i:nc_i + 1],
                                             mx_sb[:, 0:1])
                        ctx.inc(inst, "ev")

        b_ev = ctx.v("ev")
        with nc.Block() as block:
            @block.vector
            def _(ve):
                inst = ve.tensor_copy(m1colb[:, :], m1cols[:, :])
                ctx.inc(inst, "ev")

            @block.tensor
            def _(te):
                te.wait_ge(ctx.sem("ev"), b_ev + 1)
                inst = te.transpose(ps_tr[:, :], m1colb[:, :], eye_sb[:, :])
                ctx.inc(inst, "mm")

            c_mm = ctx.v("mm")

            @block.scalar
            def _(se):
                se.wait_ge(ctx.sem("mm"), c_mm)
                inst = se.copy(m1T_sb[:, :], ps_tr[:, :])
                ctx.inc(inst, "ea")

            c_ea = ctx.v("ea")

            @block.sync
            def _(sy):
                sy.wait_ge(ctx.sem("ea"), c_ea)
                inst = sy.dma_start(out=m1_out, in_=m1T_sb[:, :])
                ctx.inc(inst, "st", 16)

        with nc.Block() as block:
            @block.sync
            def _(sy):
                sy.wait_ge(ctx.sem("st"), ctx.v("st"))
    return



def emit_p4b(ctx, stack, io):
    nc = ctx.nc
    ctx.cur_stack = stack

    se_fm = io["se_fm"]
    xn_fm = io["xn_fm"]
    xn_tm = io["xn_tm"]
    m1_row = io["m1_row"]
    w1_in = io["w1"]
    w2_in = io["w2"]
    x1_32 = io["x1_32"]
    eye_in = io["C_eye"]

    x2_32 = io["x2_32"]
    x2_bf = io["x2_bf"]

    mx2_sc = io["mx2_sc"]
    rs2_sc = io["rs2_sc"]

    ptm_sb = stack.enter_context(nc.sbuf_tensor("ptm_sb", [128, 32 * D], BF16))
    ebuf = stack.enter_context(nc.sbuf_tensor("ebuf", [128, 32 * 512], BF16))
    zg_sb = stack.enter_context(nc.sbuf_tensor("zg_sb", [128, NC * 512], BF16))
    seg_sb = stack.enter_context(nc.sbuf_tensor("seg_sb", [128, NC * 512], BF16))
    xnbuf = [stack.enter_context(nc.sbuf_tensor(f"xnbuf{i}", [128, NC * 128], BF16)) for i in range(2)]
    xtm_one = stack.enter_context(nc.sbuf_tensor("xtm_one", [128, 32 * 128], BF16))
    xtm_sb = [xtm_one, xtm_one]
    wbuf = [stack.enter_context(nc.sbuf_tensor(f"wbuf{i}", [128, 1024], BF16)) for i in range(2)]
    lm32 = [stack.enter_context(nc.sbuf_tensor(f"lm32_{i}", [128, 512], F32)) for i in range(2)]
    m1b_sb = stack.enter_context(nc.sbuf_tensor("m1b_sb", [128, 512], F32))
    rz_sb = stack.enter_context(nc.sbuf_tensor("rz_sb", [128, 512], F32))
    mx2cols = stack.enter_context(nc.sbuf_tensor("mx2cols", [128, 32], F32))
    mx_tmp = stack.enter_context(nc.sbuf_tensor("mx_tmp", [128, 1], F32))
    mx2b = stack.enter_context(nc.sbuf_tensor("mx2b", [128, 32], BF16))
    mx2T = stack.enter_context(nc.sbuf_tensor("mx2T", [32, 128], F32))
    m1r_sb = stack.enter_context(nc.sbuf_tensor("m1r_sb", [1, T], BF16))
    row4 = stack.enter_context(nc.sbuf_tensor("row4", [1, 2 * 512], F32))
    rowb4 = stack.enter_context(nc.sbuf_tensor("rowb4", [1, 2 * 512], BF16))
    onesrow = stack.enter_context(nc.sbuf_tensor("m_onesrow", [1, 128], BF16))
    ones128 = stack.enter_context(nc.sbuf_tensor("m_ones128", [128, 1], BF16))
    eye_sb = stack.enter_context(nc.sbuf_tensor("m_eye", [128, 128], BF16))
    xnf_sb = stack.enter_context(nc.sbuf_tensor("m_xnf", [128, NC * 512], BF16))

    psA = stack.enter_context(nc.psum_tensor("m_psA", [128, 512], F32))
    psB = stack.enter_context(nc.psum_tensor("m_psB", [128, 512], F32))
    psCD = stack.enter_context(nc.psum_tensor("m_psCD", [128, 1024], F32))
    ps_row = stack.enter_context(nc.psum_tensor("m_psrow", [1, 512], F32))
    ps_rb = stack.enter_context(nc.psum_tensor("m_psrb", [128, 512], F32))
    ps_tr = stack.enter_context(nc.psum_tensor("m_pstr", [32, 128], BF16))
    ps_mb = stack.enter_context(nc.psum_tensor("m_psmb", [128, 512], F32))

    if True:
        with nc.Block() as block:
            @block.gpsimd
            def _(gp):
                ctx.inc(gp.dma_start(out=m1r_sb[:, :], in_=m1_row), "ld", 16)
                ctx.inc(gp.dma_start(out=eye_sb[:, :], in_=eye_in), "ld", 16)
                gp.memset(onesrow[:, :], 1.0)
                gp.memset(ones128[:, :], 1.0)
                inst = gp.memset(mx2cols[:, :], -3.0e38)
                ctx.inc(inst, "gpc")

        ld0 = ctx.v("ld")
        gp0 = ctx.v("gpc")

        for g in range(NT):
            n0 = 512 * g
            b_mm = ctx.v("mm")
            b_ev = ctx.v("ev")
            b_ea = ctx.v("ea")
            with nc.Block() as block:
                @block.gpsimd
                def _(gp, g=g):
                    gp.wait_ge(ctx.sem("mm"), b_mm)
                    for dc in range(NC):
                        inst = gp.dma_start(out=seg_sb[:, 512 * dc:512 * (dc + 1)],
                                            in_=se_fm[128 * dc:128 * (dc + 1), n0:n0 + 512])
                        ctx.inc(inst, "ldx", 16)

                ld_se = ctx.v("ldx")

                @block.vector
                def _(ve, g=g):
                    ve.wait_ge(ctx.sem("ld"), ld0)
                    ve.wait_ge(ctx.sem("ea"), b_ea)
                    ve.wait_ge(ctx.sem("ev"), b_ev)
                    inst = ve.tensor_copy(rowb4[0:1, 0:512], m1r_sb[0:1, n0:n0 + 512])
                    ctx.inc(inst, "ev")

                @block.tensor
                def _(te, g=g):
                    te.wait_ge(ctx.sem("ev"), ctx.v("ev"))
                    te.wait_ge(ctx.sem("gpc"), gp0)
                    te.wait_ge(ctx.sem("ea"), b_ea)
                    inst = te.matmul(ps_rb[:, :], onesrow[0:1, :], rowb4[0:1, 0:512],
                                     start=True, stop=True)
                    ctx.inc(inst, "mm")

                @block.scalar
                def _(se, g=g):
                    se.wait_ge(ctx.sem("mm"), ctx.v("mm"))
                    inst = se.copy(m1b_sb[:, :], ps_rb[:, :])
                    ctx.inc(inst, "ea")

            b2_mm = ctx.v("mm")
            b2_ea = ctx.v("ea")
            b2_ev = ctx.v("ev")
            b2_ld = ctx.v("ldx")
            with nc.Block() as block:
                @block.gpsimd
                def _(gp, g=g):
                    for si in range(NS):
                        if si >= 2:
                            gp.wait_ge(ctx.sem("mm"), b2_mm + 2 * (si - 1))
                        for half in range(2):
                            inst = gp.dma_start(
                                out=bass.AP(xnbuf[si % 2], 512 * half,
                                            [[1024, 128], [128, 4], [1, 128]]),
                                in_=bass.AP(xn_fm.tensor, xn_fm.offset + 128 * si
                                            + half * 4 * T * 128,
                                            [[T, 128], [T * 128, 4], [1, 128]]))
                            ctx.inc(inst, "ldx", 16)

                @block.tensor
                def _(te, g=g):
                    te.wait_ge(ctx.sem("ldx"), ld_se)
                    te.wait_ge(ctx.sem("ea"), b2_ea)
                    for si in range(NS):
                        te.wait_ge(ctx.sem("ldx"), b2_ld + 32 * (si + 1))
                        ps = psA if si % 2 == 0 else psB
                        if si >= 2:
                            te.wait_ge(ctx.sem("ev"), b2_ev + 2 * (si - 1))
                        inst = mm_accum(te, ps[:, :],
                                        [(xnbuf[si % 2][:, 128 * dc:128 * (dc + 1)],
                                          seg_sb[:, 512 * dc:512 * (dc + 1)]) for dc in range(NC)])
                        ctx.inc(inst, "mm")
                        te.wait_ge(ctx.sem("ea"), b2_ea + si + 1)
                        inst = te.matmul(ps_row[0:1, :], ones128[:, 0:1],
                                         ebuf[:, 512 * si:512 * (si + 1)],
                                         start=(si == 0), stop=(si == NS - 1))
                        ctx.inc(inst, "mm")

                @block.vector
                def _(ve, g=g):
                    for si in range(NS):
                        ps = psA if si % 2 == 0 else psB
                        ve.wait_ge(ctx.sem("mm"), b2_mm + 2 * si + 1)
                        ctx.seq(ve, "vv", ve.reduce_max(mx_tmp[:, 0:1], ps[:, :], axis=AX.X))
                        ctx.seq(ve, "vv", ve.tensor_max(mx2cols[:, si:si + 1],
                                                        mx2cols[:, si:si + 1], mx_tmp[:, 0:1]))
                        inst = ve.tensor_sub(lm32[si % 2][:, :], ps[:, :], m1b_sb[:, :])
                        ctx.inc(inst, "ev", 2)

                @block.scalar
                def _(se, g=g):
                    for si in range(NS):
                        se.wait_ge(ctx.sem("ev"), b2_ev + 2 * (si + 1))
                        if si >= 2:
                            se.wait_ge(ctx.sem("mm"), b2_mm + 2 * (si - 1))
                        inst = se.activation(ebuf[:, 512 * si:512 * (si + 1)], lm32[si % 2][:, :],
                                             AF.Exp)
                        ctx.inc(inst, "ea")

            c_mm = ctx.v("mm")
            c_ea = ctx.v("ea")
            c_ev = ctx.v("ev")
            with nc.Block() as block:
                @block.scalar
                def _(se, g=g):
                    se.wait_ge(ctx.sem("mm"), c_mm)
                    inst = se.copy(row4[0:1, 0:512], ps_row[0:1, :])
                    ctx.inc(inst, "ea")

                @block.vector
                def _(ve, g=g):
                    ve.wait_ge(ctx.sem("ea"), c_ea + 1)
                    ctx.seq(ve, "vv", ve.reciprocal(row4[0:1, 512:1024], row4[0:1, 0:512]))
                    ctx.seq(ve, "vv", ve.tensor_copy(rowb4[0:1, 0:512], row4[0:1, 512:1024]))
                    inst = ve.tensor_sub(rowb4[0:1, 512:1024], row4[0:1, 512:1024],
                                         rowb4[0:1, 0:512])
                    ctx.inc(inst, "ev")

                @block.tensor
                def _(te, g=g):
                    te.wait_ge(ctx.sem("ev"), c_ev + 1)
                    te.wait_ge(ctx.sem("ea"), c_ea + 1)
                    te.matmul(ps_rb[:, :], onesrow[0:1, :], rowb4[0:1, 0:512],
                              start=True, stop=False)
                    inst = te.matmul(ps_rb[:, :], onesrow[0:1, :], rowb4[0:1, 512:1024],
                                     start=False, stop=True)
                    ctx.inc(inst, "mm")

                @block.scalar
                def _(se, g=g):
                    se.wait_ge(ctx.sem("mm"), c_mm + 1)
                    inst = se.copy(rz_sb[:, :], ps_rb[:, :])
                    ctx.inc(inst, "ea")

            d_mm = ctx.v("mm")
            d_ea = ctx.v("ea")
            d_ev = ctx.v("ev")
            d_ld = ctx.v("ldx")
            with nc.Block() as block:
                @block.gpsimd
                def _(gp, g=g):
                    for dc in range(NC):
                        if dc >= 1:
                            gp.wait_ge(ctx.sem("mm"), d_mm + dc)
                        for half in range(2):
                            inst = gp.dma_start(
                                out=bass.AP(xtm_sb[dc % 2], 2048 * half,
                                            [[4096, 128], [128, 16], [1, 128]]),
                                in_=bass.AP(xn_tm.tensor, xn_tm.offset + 128 * dc
                                            + half * 16 * D * 128,
                                            [[D, 128], [D * 128, 16], [1, 128]]))
                            ctx.inc(inst, "ldx", 16)

                @block.tensor
                def _(te, g=g):
                    for dc in range(NC):
                        te.wait_ge(ctx.sem("ldx"), d_ld + 32 * (dc + 1))
                        ps = psA if dc % 2 == 0 else psB
                        if dc >= 2:
                            te.wait_ge(ctx.sem("ev"), d_ev + (dc - 1))
                        inst = None
                        for si in range(NS):
                            inst = te.matmul(ps[:, :], xtm_sb[dc % 2][:, 128 * si:128 * (si + 1)],
                                             ebuf[:, 512 * si:512 * (si + 1)],
                                             start=(si == 0), stop=(si == NS - 1))
                        ctx.inc(inst, "mm")

                @block.vector
                def _(ve, g=g):
                    ve.wait_ge(ctx.sem("ea"), d_ea)
                    for dc in range(NC):
                        ps = psA if dc % 2 == 0 else psB
                        ve.wait_ge(ctx.sem("mm"), d_mm + dc + 1)
                        inst = ve.tensor_mul(zg_sb[:, 512 * dc:512 * (dc + 1)], ps[:, :],
                                             rz_sb[:, :])
                        ctx.inc(inst, "ev")

            e_mm = ctx.v("mm")
            e_ea = ctx.v("ea")
            e_ev = ctx.v("ev")
            e_ld = ctx.v("ldx")
            with nc.Block() as block:
                @block.gpsimd
                def _(gp, g=g):
                    for hc in range(NHC):
                        if hc >= 2:
                            gp.wait_ge(ctx.sem("mm"), e_mm + (hc - 1))
                        for half in range(2):
                            inst = gp.dma_start(
                                out=bass.AP(wbuf[hc % 2], 512 * half,
                                            [[1024, 128], [128, 4], [1, 128]]),
                                in_=bass.AP(w1_in.tensor, w1_in.offset + 128 * hc
                                            + half * 4 * HID * 128,
                                            [[HID, 128], [HID * 128, 4], [1, 128]]))
                            ctx.inc(inst, "ldx", 16)

                @block.tensor
                def _(te, g=g):
                    te.wait_ge(ctx.sem("ev"), e_ev)
                    for hc in range(NHC):
                        te.wait_ge(ctx.sem("ldx"), e_ld + 32 * (hc + 1))
                        ps = psA if hc % 2 == 0 else psB
                        if hc >= 2:
                            te.wait_ge(ctx.sem("ea"), e_ea + (hc - 1))
                        inst = mm_accum(te, ps[:, :],
                                        [(wbuf[hc % 2][:, 128 * dc:128 * (dc + 1)],
                                          zg_sb[:, 512 * dc:512 * (dc + 1)]) for dc in range(NC)])
                        ctx.inc(inst, "mm")

                @block.scalar
                def _(se, g=g):
                    for hc in range(NHC):
                        ps = psA if hc % 2 == 0 else psB
                        se.wait_ge(ctx.sem("mm"), e_mm + hc + 1)
                        inst = se.activation(ebuf[:, 512 * hc:512 * (hc + 1)], ps[:, :], AF.Gelu)
                        ctx.inc(inst, "ea")

            f_mm = ctx.v("mm")
            f_ea = ctx.v("ea")
            f_ld = ctx.v("ldx")
            with nc.Block() as block:
                @block.gpsimd
                def _(gp, g=g):
                    for half in range(2):
                        for hc in range(NHC):
                            k = NHC * half + hc
                            gp.wait_ge(ctx.sem("mm"), f_mm + (4 * (k - 1) if k >= 2 else 0))
                            inst = gp.dma_start(out=wbuf[hc % 2][:, :],
                                                in_=w2_in[128 * hc:128 * (hc + 1), :])
                            ctx.inc(inst, "ldx", 16)

                @block.tensor
                def _(te, g=g):
                    te.wait_ge(ctx.sem("ea"), f_ea)
                    for half in range(2):
                        if half == 1:
                            te.wait_ge(ctx.sem("ea"), f_ea + 4)
                        for hc in range(NHC):
                            te.wait_ge(ctx.sem("ldx"), f_ld + 16 * (NHC * half + hc + 1))
                            for nci2 in range(2):
                                nci = 2 * half + nci2
                                for dh in range(2):
                                    ps = (psA[:, :], psB[:, :], psCD[:, 0:512], psCD[:, 512:1024])[2 * nci2 + dh]
                                    inst = te.matmul(
                                        ps, ebuf[:, 512 * hc + 128 * nci:512 * hc + 128 * (nci + 1)],
                                        wbuf[hc % 2][:, 512 * dh:512 * (dh + 1)],
                                        start=(hc == 0), stop=(hc == NHC - 1))
                            ctx.inc(inst, "mm", 4)

                @block.scalar
                def _(se, g=g):
                    for half in range(2):
                        se.wait_ge(ctx.sem("mm"), f_mm + 4 * NHC * (half + 1))
                        for nci2 in range(2):
                            nci = 2 * half + nci2
                            nc2 = 4 * g + nci
                            for dh in range(2):
                                ps = (psA[:, :], psB[:, :], psCD[:, 0:512], psCD[:, 512:1024])[2 * nci2 + dh]
                                inst = se.copy(ptm_sb[:, 1024 * nc2 + 512 * dh:
                                                      1024 * nc2 + 512 * (dh + 1)], ps)
                                ctx.inc(inst, "ea")

        g_ev = ctx.v("ev")
        g_mm = ctx.v("mm")
        g_ea = ctx.v("ea")
        g_ld = ctx.v("ldx")
        with nc.Block() as block:
            @block.vector
            def _(ve):
                inst = ve.tensor_copy(mx2b[:, :], mx2cols[:, :])
                ctx.inc(inst, "ev")

            @block.tensor
            def _(te):
                te.wait_ge(ctx.sem("ev"), g_ev + 1)
                te.wait_ge(ctx.sem("ea"), g_ea)
                inst = te.transpose(ps_tr[:, :], mx2b[:, :], eye_sb[:, :])
                ctx.inc(inst, "mm")

            @block.scalar
            def _(se):
                se.wait_ge(ctx.sem("mm"), g_mm + 1)
                inst = se.copy(mx2T[:, :], ps_tr[:, :])
                ctx.inc(inst, "ea")

            @block.sync
            def _(sy):
                sy.wait_ge(ctx.sem("ea"), g_ea + 1)
                inst = sy.dma_start(out=mx2_sc, in_=mx2T[:, :])
                ctx.inc(inst, "st", 16)

            st_v = ctx.v("st")

            @block.gpsimd
            def _(gp):
                gp.wait_ge(ctx.sem("st"), st_v)
                inst = gp.dma_start(out=row4[0:1, 0:512], in_=bass.AP(mx2_sc.tensor, 0, [[1, 1], [1, 512]]))
                ctx.inc(inst, "ldx", 16)
                for c in range(8):
                    inst = gp.dma_start(out=m1r_sb[0:1, 512 * c:512 * (c + 1)],
                                        in_=bass.AP(mx2_sc.tensor, 512 * c, [[1, 1], [1, 512]]))
                    ctx.inc(inst, "ldx", 16)

            ld_v2 = ctx.v("ldx")

            @block.vector
            def _(ve):
                ve.wait_ge(ctx.sem("ldx"), ld_v2)
                inst = ve.tensor_scalar_mul(m1r_sb[0:1, :], m1r_sb[0:1, :], -1.0)
                ctx.inc(inst, "ev")

        pl = PostLN(ctx, "p4f", n_add=1, eps=1e-5, scale_rows=rs2_sc,
                    with_producer=True,
                    ps_share={"ps_x": [psA, psB], "ps_st": psCD, "ps_mb": ps_mb,
                              "ps_rb": ps_rb},
                    sb_share={"xbf": zg_sb, "srow": rz_sb})
        pl.emit_init()

        ec_state = {}

        def pre_tile(sg):
            t0 = 512 * sg
            h_mm = ctx.v("mm")
            h_ea = ctx.v("ea")
            h_ev = ctx.v("ev")
            h_ld = ctx.v("ldx")
            with nc.Block() as block:
                @block.gpsimd
                def _(gp, sg=sg):
                    gp.wait_ge(ctx.sem("mm"), h_mm)
                    for dc in range(NC):
                        inst = gp.dma_start(out=xnf_sb[:, 512 * dc:512 * (dc + 1)],
                                            in_=xn_fm[128 * dc:128 * (dc + 1), t0:t0 + 512])
                        ctx.inc(inst, "ldx", 16)

                ld_a = ctx.v("ldx")

                @block.gpsimd
                def _(gp, sg=sg):
                    for nc2 in range(32):
                        if nc2 >= 2:
                            gp.wait_ge(ctx.sem("mm"), h_mm + 2 * (nc2 - 1))
                        for half in range(2):
                            inst = gp.dma_start(
                                out=xnbuf[nc2 % 2][:, 512 * half:512 * (half + 1)],
                                in_=bass.AP(se_fm.tensor, se_fm.offset + 128 * nc2
                                            + half * 4 * T * 128,
                                            [[T, 128], [T * 128, 4], [1, 128]]))
                            ctx.inc(inst, "ldx", 16)

                @block.tensor
                def _(te, sg=sg):
                    te.wait_ge(ctx.sem("ldx"), ld_a)
                    te.wait_ge(ctx.sem("ea"), h_ea)
                    te.wait_ge(ctx.sem("ev"), h_ev)
                    for nc2 in range(32):
                        te.wait_ge(ctx.sem("ldx"), ld_a + 32 * (nc2 + 1))
                        ps = psA if nc2 % 2 == 0 else psB
                        if nc2 >= 2:
                            te.wait_ge(ctx.sem("ea"), h_ea + (nc2 - 1))
                        for dc in range(NC):
                            te.matmul(ps[:, :], xnbuf[nc2 % 2][:, 128 * dc:128 * (dc + 1)],
                                      xnf_sb[:, 512 * dc:512 * (dc + 1)],
                                      start=(dc == 0), stop=False)
                        inst = te.matmul(ps[:, :], onesrow[0:1, :], m1r_sb[0:1, t0:t0 + 512],
                                         start=False, stop=True)
                        ctx.inc(inst, "mm")
                        te.wait_ge(ctx.sem("ea"), h_ea + nc2 + 1)
                        inst = te.matmul(ps_row[0:1, :], ones128[:, 0:1],
                                         ebuf[:, 512 * nc2:512 * (nc2 + 1)],
                                         start=(nc2 == 0), stop=(nc2 == 31))
                        ctx.inc(inst, "mm")

                @block.scalar
                def _(se, sg=sg):
                    for nc2 in range(32):
                        ps = psA if nc2 % 2 == 0 else psB
                        se.wait_ge(ctx.sem("mm"), h_mm + 2 * nc2 + 1)
                        inst = se.activation(ebuf[:, 512 * nc2:512 * (nc2 + 1)], ps[:, :], AF.Exp)
                        ctx.inc(inst, "ea")

                i_mm = ctx.v("mm")
                i_ea = ctx.v("ea")
                i_ev = ctx.v("ev")

                @block.scalar
                def _(se, sg=sg):
                    se.wait_ge(ctx.sem("mm"), i_mm)
                    inst = se.copy(row4[0:1, 0:512], ps_row[0:1, :])
                    ctx.inc(inst, "ea")

                @block.vector
                def _(ve, sg=sg):
                    ve.wait_ge(ctx.sem("ea"), i_ea + 1)
                    inst = ve.reciprocal(row4[0:1, 512:1024], row4[0:1, 0:512])
                    ctx.inc(inst, "ev")

                @block.sync
                def _(sy, sg=sg):
                    sy.wait_ge(ctx.sem("ev"), i_ev + 1)
                    inst = sy.dma_start(out=rs2_sc[0:1, t0:t0 + 512], in_=row4[0:1, 512:1024])
                    ctx.inc(inst, "st", 16)

            ec_state["ea"] = ctx.v("ea")
            ec_state["st"] = ctx.v("st")

        def producer(te, fc, tt, ps):
            te.wait_ge(ctx.sem("ea"), ec_state["ea"])
            te.wait_ge(ctx.sem("st"), ec_state["st"])
            inst = None
            for nc2 in range(32):
                inst = te.matmul(ps[:, :], ptm_sb[:, 1024 * nc2 + 128 * fc:1024 * nc2 + 128 * (fc + 1)],
                                 ebuf[:, 512 * nc2:512 * (nc2 + 1)],
                                 start=(nc2 == 0), stop=(nc2 == 31))
            return inst

        def rhs_loader(gp, tt):
            pass

        pl.emit(producer, rhs_loader, [x1_32], x2_32, x2_bf, bias_pc_sb=None,
                pre_tile=pre_tile, extra_gp_wait=lambda: ("st", ec_state["st"]))

        with nc.Block() as block:
            @block.sync
            def _(sy):
                sy.wait_ge(ctx.sem("st"), ctx.v("st"))
    return


def emit_p6(ctx, stack, io):
    nc = ctx.nc
    ctx.cur_stack = stack

    x3_bf = io["x3_bf"]
    x3_32 = io["x3_32"]
    w1h = io["w1h"]
    w2h = io["w2h"]
    h1h = io["h1h"]
    h2h = io["h2h"]
    w3h = io["w3h"]
    s1_pc = io["s1_pc"]
    s2_pc = io["s2_pc"]
    sh1_pc = io["sh1_pc"]
    sh2_pc = io["sh2_pc"]
    s3_pc = io["s3_pc"]
    x4_32 = io["x4_32"]

    a_sb = stack.enter_context(nc.sbuf_tensor("p6_a", [128, NHC * 1024], BF16))
    b_sb = stack.enter_context(nc.sbuf_tensor("p6_b", [128, NHC * 1024], BF16))
    x3t_sb = stack.enter_context(nc.sbuf_tensor("p6_x3", [128, NC * 512], BF16))
    wA = [stack.enter_context(nc.sbuf_tensor(f"p6_wA{i}", [128, NC * 128], BF16)) for i in range(2)]
    wB = [stack.enter_context(nc.sbuf_tensor(f"p6_wB{i}", [128, NC * 128], BF16)) for i in range(2)]
    wH = [stack.enter_context(nc.sbuf_tensor(f"p6_wH{i}", [128, NHC * 128], BF16)) for i in range(2)]
    g1_sb = stack.enter_context(nc.sbuf_tensor("p6_g1", [128, 512], F32))
    scl = {}
    for nm in ("s1", "s2", "sh1", "sh2"):
        scl[nm] = stack.enter_context(nc.sbuf_tensor(f"p6_{nm}", [128, NHC], F32))
    scl["s3"] = stack.enter_context(nc.sbuf_tensor("p6_s3", [128, NC], F32))

    psA = stack.enter_context(nc.psum_tensor("p6_psA", [128, 512], F32))
    psB = stack.enter_context(nc.psum_tensor("p6_psB", [128, 512], F32))

    if True:
        with nc.Block() as block:
            @block.gpsimd
            def _(gp):
                for nm, ap in (("s1", s1_pc), ("s2", s2_pc), ("sh1", sh1_pc),
                               ("sh2", sh2_pc), ("s3", s3_pc)):
                    ctx.inc(gp.dma_start(out=scl[nm][:, :], in_=ap), "ld", 16)
        ld0 = ctx.v("ld")

        pl = PostLN(ctx, "p6", n_add=1, eps=1e-5,
                    ps_share={"ps_x": [psA, psB]}, sb_share={"xbf": b_sb, "sbf": wA[0][:, 0:512], "sqbf": wA[1][:, 0:512]})
        pl.emit_init()

        for tb in range(4):
            tb0 = 1024 * tb
            for tg in range(2):
                t0g = tb0 + 512 * tg
                b_mm = ctx.v("mm")
                with nc.Block() as block:
                    @block.gpsimd
                    def _(gp, t0g=t0g, b_mm=b_mm):
                        gp.wait_ge(ctx.sem("mm"), b_mm)
                        for dc in range(NC):
                            inst = gp.dma_start(out=x3t_sb[:, 512 * dc:512 * (dc + 1)],
                                                in_=x3_bf[128 * dc:128 * (dc + 1), t0g:t0g + 512])
                            ctx.inc(inst, "ldx", 16)
                ld_x3 = ctx.v("ldx")

                a_mm = ctx.v("mm")
                a_ea = ctx.v("ea")
                a_ev = ctx.v("ev")
                a_ld = ctx.v("ldx")
                with nc.Block() as block:
                    @block.gpsimd
                    def _(gp, a_mm=a_mm):
                        for hc in range(NHC):
                            if hc >= 2:
                                gp.wait_ge(ctx.sem("mm"), a_mm + 2 * (hc - 1))
                            ctx.inc(gp.dma_start(out=wA[hc % 2][:, :], in_=w1h[hc]), "ldx", 16)
                            ctx.inc(gp.dma_start(out=wB[hc % 2][:, :], in_=w2h[hc]), "ldx", 16)

                    @block.tensor
                    def _(te, a_ld=a_ld, a_ea=a_ea, a_ev=a_ev):
                        te.wait_ge(ctx.sem("ldx"), ld_x3)
                        te.wait_ge(ctx.sem("ld"), ld0)
                        for hc in range(NHC):
                            te.wait_ge(ctx.sem("ldx"), a_ld + 32 * (hc + 1))
                            te.wait_ge(ctx.sem("ea"), a_ea + hc)
                            te.wait_ge(ctx.sem("ev"), a_ev + hc)
                            for w_sb, ps in ((wA[hc % 2], psA), (wB[hc % 2], psB)):
                                inst = mm_accum(te, ps[:, :],
                                                [(w_sb[:, 128 * dc:128 * (dc + 1)],
                                                  x3t_sb[:, 512 * dc:512 * (dc + 1)])
                                                 for dc in range(NC)])
                                ctx.inc(inst, "mm")

                    @block.scalar
                    def _(se, a_mm=a_mm, a_ev=a_ev):
                        for hc in range(NHC):
                            se.wait_ge(ctx.sem("mm"), a_mm + 2 * hc + 1)
                            if hc >= 1:
                                se.wait_ge(ctx.sem("ev"), a_ev + hc)
                            inst = se.activation(g1_sb[:, 0:512], psA[:, :],
                                                 AF.Silu, scale=scl["s1"][:, hc:hc + 1])
                            ctx.inc(inst, "ea")

                    @block.vector
                    def _(ve, a_mm=a_mm, a_ea=a_ea, tg=tg):
                        for hc in range(NHC):
                            ve.wait_ge(ctx.sem("mm"), a_mm + 2 * hc + 2)
                            ve.wait_ge(ctx.sem("ea"), a_ea + hc + 1)
                            inst = ve.scalar_tensor_tensor(
                                a_sb[:, 1024 * hc + 512 * tg:1024 * hc + 512 * (tg + 1)],
                                psB[:, :], scl["s2"][:, hc:hc + 1],
                                g1_sb[:, 0:512], ALU.mult, ALU.mult)
                            ctx.inc(inst, "ev")

            for src_sb, dst_sb, wdram, snm in ((a_sb, b_sb, h1h, "sh1"),
                                               (b_sb, a_sb, h2h, "sh2")):
                c_mm = ctx.v("mm")
                c_ea = ctx.v("ea")
                c_ld = ctx.v("ldx")
                with nc.Block() as block:
                    @block.gpsimd
                    def _(gp, wdram=wdram, c_mm=c_mm):
                        for oc in range(NHC):
                            if oc >= 2:
                                gp.wait_ge(ctx.sem("mm"), c_mm + 2 * (oc - 1))
                            ctx.inc(gp.dma_start(out=wH[oc % 2][:, :], in_=wdram[oc]), "ldx", 16)

                    @block.tensor
                    def _(te, src_sb=src_sb, c_ld=c_ld, c_ea=c_ea):
                        for oc in range(NHC):
                            te.wait_ge(ctx.sem("ldx"), c_ld + 16 * (oc + 1))
                            if oc >= 1:
                                te.wait_ge(ctx.sem("ea"), c_ea + 2 * oc)
                            for tg in range(2):
                                ps = psA if tg == 0 else psB
                                inst = mm_accum(te, ps[:, :],
                                                [(wH[oc % 2][:, 128 * ic:128 * (ic + 1)],
                                                  src_sb[:, 1024 * ic + 512 * tg:
                                                         1024 * ic + 512 * (tg + 1)])
                                                 for ic in range(NHC)])
                                ctx.inc(inst, "mm")

                    @block.scalar
                    def _(se, dst_sb=dst_sb, snm=snm, c_mm=c_mm):
                        for oc in range(NHC):
                            for tg in range(2):
                                ps = psA if tg == 0 else psB
                                se.wait_ge(ctx.sem("mm"), c_mm + 2 * oc + tg + 1)
                                inst = se.activation(
                                    dst_sb[:, 1024 * oc + 512 * tg:1024 * oc + 512 * (tg + 1)],
                                    ps[:, :], AF.Gelu, scale=scl[snm][:, oc:oc + 1])
                                ctx.inc(inst, "ea")

            w3_state = {}

            def rhs_loader(gp, tt, w3_state=w3_state):
                base_mm = ctx.v("mm")
                w3_state["ld"] = ctx.v("ldx")
                for fc in range(NC):
                    gp.wait_ge(ctx.sem("mm"), base_mm + (fc - 1 if fc >= 2 else 0))
                    ctx.inc(gp.dma_start(out=wH[fc % 2][:, :], in_=w3h[fc]), "ldx", 16)

            def producer(te, fc, tt, ps, tb=tb, w3_state=w3_state):
                te.wait_ge(ctx.sem("ldx"), w3_state["ld"] + 16 * (fc + 1))
                tg = tt - 2 * tb
                return mm_accum(te, ps[:, :],
                                [(wH[fc % 2][:, 128 * ic:128 * (ic + 1)],
                                  a_sb[:, 1024 * ic + 512 * tg:1024 * ic + 512 * (tg + 1)])
                                 for ic in range(NHC)])

            pl.emit(producer, rhs_loader, [x3_32], x4_32, None,
                    scale_pc_sb=scl["s3"], tts=range(2 * tb, 2 * tb + 2))

        with nc.Block() as block:
            @block.sync
            def _(sy):
                sy.wait_ge(ctx.sem("st"), ctx.v("st"))
    return


def emit_p5a(ctx, stack, io):
    nc = ctx.nc
    ctx.cur_stack = stack

    x2_bf = io["x2_bf"]
    ln1_pc = io["ln1_pc"]
    Wq_in = io["Wq"]
    Wk_in = io["Wk"]
    Wv_in = io["Wv"]
    Wb_in = io["Wb"]
    ones2_in = io["C_ones2"]
    sel_in = io["C_sel"]

    q_fm = io["q_fm"]
    k_fm = io["k_fm"]
    v_tm = io["v_tm"]
    k_tm = io["k_tm"]
    beta_tm = io["beta_tm"]
    xn1_out = io["xn1_bf"]

    x2t_sb = stack.enter_context(nc.sbuf_tensor("p5_x2", [128, NC * 512], BF16))
    xn1_sb = stack.enter_context(nc.sbuf_tensor("p5_xn1", [128, NC * T], BF16))
    wq_sb = stack.enter_context(nc.sbuf_tensor("p5_wq", [128, NC * D], BF16))
    wk_sb = stack.enter_context(nc.sbuf_tensor("p5_wk", [128, NC * D], BF16))
    wv_sb = stack.enter_context(nc.sbuf_tensor("p5_wv", [128, NC * D], BF16))
    wb_sb = stack.enter_context(nc.sbuf_tensor("p5_wb", [128, NC * H], BF16))
    ones2_sb = stack.enter_context(nc.sbuf_tensor("p5_ones2", [128, NC * H], BF16))
    sel_sb = stack.enter_context(nc.sbuf_tensor("p5_sel", [H, D], BF16))
    ln1_sb = stack.enter_context(nc.sbuf_tensor("p5_ln1", [128, NC], F32))
    onesrow = stack.enter_context(nc.sbuf_tensor("p5_onesrow", [1, 128], BF16))
    ones128 = stack.enter_context(nc.sbuf_tensor("p5_ones128", [128, 1], BF16))
    sqbf_sb = stack.enter_context(nc.sbuf_tensor("p5_sqbf", [128, 512], BF16))
    row_sb = stack.enter_context(nc.sbuf_tensor("p5_row", [1, 1024], F32))
    rowb_sb = stack.enter_context(nc.sbuf_tensor("p5_rowb", [1, 1024], BF16))
    rb32 = stack.enter_context(nc.sbuf_tensor("p5_rb32", [128, 512], F32))
    t1_sb = stack.enter_context(nc.sbuf_tensor("p5_t1", [128, 512], F32))

    pre32 = stack.enter_context(nc.sbuf_tensor("p5_pre32", [128, NC * 512], F32))
    sq_sb = stack.enter_context(nc.sbuf_tensor("p5_sq", [128, NC * 512], BF16))
    val_bf = stack.enter_context(nc.sbuf_tensor("p5_val", [128, NC * 512], BF16))
    st16 = stack.enter_context(nc.sbuf_tensor("p5_st16", [H, 3 * 512], F32))
    st16b = stack.enter_context(nc.sbuf_tensor("p5_st16b", [H, 2 * 512], BF16))
    out_bf = stack.enter_context(nc.sbuf_tensor("p5_outbf", [128, NC * 512], BF16))
    v_bf = stack.enter_context(nc.sbuf_tensor("p5_vbf", [128, 2 * 512], BF16))
    trans_sb = stack.enter_context(nc.sbuf_tensor("p5_trans", [128, 1024], BF16))

    ps_a = stack.enter_context(nc.psum_tensor("p5_psa", [128, 512], F32))
    ps_b = stack.enter_context(nc.psum_tensor("p5_psb", [128, 512], F32))
    ps_ss = stack.enter_context(nc.psum_tensor("p5_psss", [H, 512], F32))
    ps_mb = stack.enter_context(nc.psum_tensor("p5_psmb", [128, 512], F32))
    ps_rb = stack.enter_context(nc.psum_tensor("p5_psrb", [128, 512], F32))
    ps_st = stack.enter_context(nc.psum_tensor("p5_psst", [1, 512], F32))

    if True:
        with nc.Block() as block:
            @block.gpsimd
            def _(gp):
                for i in range(NC):
                    ctx.inc(gp.dma_start(out=wq_sb[:, D * i:D * (i + 1)], in_=Wq_in[128 * i:128 * (i + 1), :]), "ld", 16)
                    ctx.inc(gp.dma_start(out=wk_sb[:, D * i:D * (i + 1)], in_=Wk_in[128 * i:128 * (i + 1), :]), "ld", 16)
                    ctx.inc(gp.dma_start(out=wv_sb[:, D * i:D * (i + 1)], in_=Wv_in[128 * i:128 * (i + 1), :]), "ld", 16)
                    ctx.inc(gp.dma_start(out=wb_sb[:, H * i:H * (i + 1)], in_=Wb_in[128 * i:128 * (i + 1), :]), "ld", 16)
                    ctx.inc(gp.dma_start(out=ones2_sb[:, H * i:H * (i + 1)], in_=ones2_in[128 * i:128 * (i + 1), :]), "ld", 16)
                ctx.inc(gp.dma_start(out=sel_sb[:, :], in_=sel_in), "ld", 16)
                ctx.inc(gp.dma_start(out=ln1_sb[:, :], in_=ln1_pc), "ld", 16)
                gp.memset(onesrow[:, :], 1.0)
                inst = gp.memset(ones128[:, :], 1.0)
                ctx.inc(inst, "gpc")
        ld0 = ctx.v("ld")
        gp0 = ctx.v("gpc")

        for tt in range(NT):
            t0, t1 = 512 * tt, 512 * (tt + 1)
            b_mm = ctx.v("mm")
            b_ea = ctx.v("ea")
            b_ev = ctx.v("ev")
            with nc.Block() as block:
                @block.gpsimd
                def _(gp, tt=tt, t0=t0, t1=t1, b_mm=b_mm, b_ev=b_ev):
                    gp.wait_ge(ctx.sem("mm"), b_mm)
                    gp.wait_ge(ctx.sem("ev"), b_ev)
                    for fc in range(NC):
                        inst = gp.dma_start(out=x2t_sb[:, 512 * fc:512 * (fc + 1)],
                                            in_=x2_bf[128 * fc:128 * (fc + 1), t0:t1])
                        ctx.inc(inst, "ldx", 16)
                ld_v = ctx.v("ldx")

                @block.scalar
                def _(se, b_mm=b_mm, ld_v=ld_v):
                    se.wait_ge(ctx.sem("ldx"), ld_v)
                    for fc in range(NC):
                        if fc >= 1:
                            se.wait_ge(ctx.sem("mm"), b_mm + fc)
                        inst = se.activation(sqbf_sb[:, :], x2t_sb[:, 512 * fc:512 * (fc + 1)],
                                             AF.Square)
                        ctx.inc(inst, "ea")

                @block.tensor
                def _(te, b_ea=b_ea, gp0=gp0):
                    te.wait_ge(ctx.sem("gpc"), gp0)
                    for fc in range(NC):
                        te.wait_ge(ctx.sem("ea"), b_ea + fc + 1)
                        inst = te.matmul(ps_st[0:1, :], ones128[:, 0:1], sqbf_sb[:, :],
                                         start=(fc == 0), stop=(fc == NC - 1))
                        ctx.inc(inst, "mm")
                c_mm = ctx.v("mm")

                @block.vector
                def _(ve, c_mm=c_mm):
                    ve.wait_ge(ctx.sem("mm"), c_mm)
                    ctx.seq(ve, "vv", ve.tensor_scalar_mul(row_sb[0:1, 0:512], ps_st[0:1, :], 1.0 / D))
                    inst = ve.tensor_scalar_add(row_sb[0:1, 0:512], row_sb[0:1, 0:512], 1e-6)
                    ctx.inc(inst, "ev")
                c_ev = ctx.v("ev")

                @block.scalar
                def _(se, c_ev=c_ev):
                    se.wait_ge(ctx.sem("ev"), c_ev)
                    inst = se.activation(row_sb[0:1, 0:512], row_sb[0:1, 0:512], AF.Sqrt)
                    ctx.inc(inst, "ea")
                c_ea = ctx.v("ea")

                @block.vector
                def _(ve, c_ea=c_ea):
                    ve.wait_ge(ctx.sem("ea"), c_ea)
                    ctx.seq(ve, "vv", ve.reciprocal(row_sb[0:1, 512:1024], row_sb[0:1, 0:512]))
                    ctx.seq(ve, "vv", ve.tensor_copy(rowb_sb[0:1, 0:512], row_sb[0:1, 512:1024]))
                    inst = ve.tensor_sub(rowb_sb[0:1, 512:1024], row_sb[0:1, 512:1024],
                                         rowb_sb[0:1, 0:512])
                    ctx.inc(inst, "ev")
                c_ev2 = ctx.v("ev")

                @block.tensor
                def _(te, c_ev2=c_ev2, b_ea=b_ea):
                    te.wait_ge(ctx.sem("ev"), c_ev2)
                    te.wait_ge(ctx.sem("ea"), b_ea + NC + 1)
                    te.matmul(ps_a[:, :], onesrow[0:1, :], rowb_sb[0:1, 0:512],
                              start=True, stop=False)
                    inst = te.matmul(ps_a[:, :], onesrow[0:1, :], rowb_sb[0:1, 512:1024],
                                     start=False, stop=True)
                    ctx.inc(inst, "mm")
                c_mm2 = ctx.v("mm")

                @block.scalar
                def _(se, c_mm2=c_mm2):
                    se.wait_ge(ctx.sem("mm"), c_mm2)
                    inst = se.copy(rb32[:, :], ps_a[:, :])
                    ctx.inc(inst, "ea")
                c_ea2 = ctx.v("ea")

                @block.vector
                def _(ve, c_ea2=c_ea2):
                    for fc in range(NC):
                        ve.wait_ge(ctx.sem("ea"), c_ea2 + fc)
                        inst = ve.tensor_mul(t1_sb[:, :], x2t_sb[:, 512 * fc:512 * (fc + 1)],
                                             rb32[:, :])
                        ctx.inc(inst, "ev")

                @block.scalar
                def _(se, c_ev2=c_ev2, tt=tt, t0=t0, t1=t1):
                    for fc in range(NC):
                        se.wait_ge(ctx.sem("ev"), c_ev2 + fc + 1)
                        inst = se.activation(xn1_sb[:, T * fc + t0:T * fc + t1], t1_sb[:, :],
                                             AF.Identity, scale=ln1_sb[:, fc:fc + 1])
                        ctx.inc(inst, "ea")
                last_ea = ctx.v("ea")

                @block.sync
                def _(sy, last_ea=last_ea, tt=tt, t0=t0, t1=t1):
                    sy.wait_ge(ctx.sem("ea"), last_ea)
                    for fc in range(NC):
                        inst = sy.dma_start(out=xn1_out[128 * fc:128 * (fc + 1), t0:t1],
                                            in_=xn1_sb[:, T * fc + t0:T * fc + t1])
                        ctx.inc(inst, "st", 16)

        bufs = (xn1_sb, ones2_sb, sel_sb, pre32, sq_sb, val_bf, st16, st16b,
                rb32, t1_sb, out_bf, ps_a, ps_b, ps_ss, ps_mb, ps_rb)
        emit_proj_headnorm(ctx, bufs, wq_sb, q_fm, "l2", eps=1e-6)
        emit_proj_headnorm(ctx, bufs, wk_sb, k_fm, "l2", eps=1e-6)

        for ts in range(NS):
            b_mm = ctx.v("mm")
            b_ea = ctx.v("ea")
            b_st = ctx.v("st")
            with nc.Block() as block:
                @block.tensor
                def _(te, ts=ts, b_ea=b_ea):
                    te.wait_ge(ctx.sem("ea"), b_ea)
                    for fh in range(2):
                        ps = ps_a if fh == 0 else ps_b
                        inst = mm_accum(te, ps[:, :],
                                        [(xn1_sb[:, T * kc + 128 * ts:T * kc + 128 * (ts + 1)],
                                          wv_sb[:, D * kc + 512 * fh:D * kc + 512 * (fh + 1)])
                                         for kc in range(NC)])
                        ctx.inc(inst, "mm")
                    inst = mm_accum(te, ps_rb[:, 0:H],
                                    [(xn1_sb[:, T * kc + 128 * ts:T * kc + 128 * (ts + 1)],
                                      wb_sb[:, H * kc:H * (kc + 1)]) for kc in range(NC)])
                    ctx.inc(inst, "mm")

                @block.scalar
                def _(se, ts=ts, b_mm=b_mm, b_st=b_st):
                    se.wait_ge(ctx.sem("st"), b_st)
                    for fh in range(2):
                        ps = ps_a if fh == 0 else ps_b
                        se.wait_ge(ctx.sem("mm"), b_mm + fh + 1)
                        inst = se.activation(v_bf[:, 512 * fh:512 * (fh + 1)], ps[:, :], AF.Silu)
                        ctx.inc(inst, "ea")
                    se.wait_ge(ctx.sem("mm"), b_mm + 3)
                    inst = se.activation(t1_sb[:, 0:H], ps_rb[:, 0:H], AF.Sigmoid)
                    ctx.inc(inst, "ea")
                last_ea = ctx.v("ea")

                @block.sync
                def _(sy, ts=ts, last_ea=last_ea):
                    sy.wait_ge(ctx.sem("ea"), last_ea)
                    inst = sy.dma_start(out=v_tm[128 * ts:128 * (ts + 1), :], in_=v_bf[:, :])
                    ctx.inc(inst, "st", 16)
                    inst = sy.dma_start(out=beta_tm[128 * ts:128 * (ts + 1), :], in_=t1_sb[:, 0:H])
                    ctx.inc(inst, "st", 16)

        st_k = ctx.v("st")
        for ts in range(NS):
            b_st_ts = ctx.v("st")
            with nc.Block() as block:
                @block.sync
                def _(sy, ts=ts, b_st_ts=b_st_ts):
                    sy.wait_ge(ctx.sem("st"), b_st_ts if ts > 0 else st_k)
                    inst = sy.dma_start_transpose(trans_sb[:, :],
                                                  k_fm[:, 128 * ts:128 * (ts + 1)])
                    ctx.inc(inst, "ldx", 16)
                ld_v = ctx.v("ldx")

                @block.sync
                def _(sy, ts=ts, ld_v=ld_v):
                    sy.wait_ge(ctx.sem("ldx"), ld_v)
                    inst = sy.dma_start(out=k_tm[128 * ts:128 * (ts + 1), :], in_=trans_sb[:, :])
                    ctx.inc(inst, "st", 16)

        with nc.Block() as block:
            @block.sync
            def _(sy):
                sy.wait_ge(ctx.sem("st"), ctx.v("st"))
    return


def emit_p5b(ctx, stack, io):
    nc = ctx.nc
    ctx.cur_stack = stack

    q_fm = io["q_fm"]
    k_fm = io["k_fm"]
    k_tm = io["k_tm"]
    v_tm = io["v_tm"]
    beta_tm = io["beta_tm"]
    x2_32 = io["x2_32"]
    Wo_in = io["Wo"]
    ln2_pc = io["ln2_pc"]
    wgh = io["wgh"]
    wuh = io["wuh"]
    wdh = io["wdh"]
    stril4_in = io["C_stril4"]
    triu4_in = io["C_triu4"]
    eye_in = io["C_eye"]

    x3_32 = io["x3_32"]
    x3_bf = io["x3_bf"]
    hh_32 = io["hh_32"]
    y_dram = io["y_bfd"]

    o_fm = stack.enter_context(nc.sbuf_tensor("d_ofm", [128, NC * T], BF16))
    kf_c = stack.enter_context(nc.sbuf_tensor("d_kf", [64, 2048], BF16))
    qf_c = stack.enter_context(nc.sbuf_tensor("d_qf", [64, 2048], BF16))
    kt_c = stack.enter_context(nc.sbuf_tensor("d_kt", [128, 1024], BF16))
    vt_c = stack.enter_context(nc.sbuf_tensor("d_vt", [128, 1024], BF16))
    beta_c = stack.enter_context(nc.sbuf_tensor("d_beta", [128, H], F32))
    N_all = stack.enter_context(nc.sbuf_tensor("d_N", [128, H * 128], F32))
    P_all = stack.enter_context(nc.sbuf_tensor("d_P", [128, H * 128], F32))
    Y_all = stack.enter_context(nc.sbuf_tensor("d_Y", [128, H * 128], F32))
    Gm_all = stack.enter_context(nc.sbuf_tensor("d_Gm", [128, H * 128], BF16))
    TKf = stack.enter_context(nc.sbuf_tensor("d_TKf", [64, H * 128], BF16))
    Vt_all = stack.enter_context(nc.sbuf_tensor("d_Vt", [128, H * 64], BF16))
    S32 = stack.enter_context(nc.sbuf_tensor("d_S32", [64, H * 64], F32))
    S_bf = stack.enter_context(nc.sbuf_tensor("d_Sbf", [64, H * 64], BF16))
    stril4 = stack.enter_context(nc.sbuf_tensor("d_stril4", [128, 512], BF16))
    triu4 = stack.enter_context(nc.sbuf_tensor("d_triu4", [128, 512], BF16))
    eye_sb = stack.enter_context(nc.sbuf_tensor("d_eye", [128, 128], BF16))
    eye32_sb = stack.enter_context(nc.sbuf_tensor("d_eye32", [128, 128], F32))

    psG = stack.enter_context(nc.psum_tensor("d_psG", [128, 512], F32))
    psX = stack.enter_context(nc.psum_tensor("d_psX", [128, 512], F32))
    psY = stack.enter_context(nc.psum_tensor("d_psY", [128, 512], F32))
    psT = stack.enter_context(nc.psum_tensor("d_psT", [128, 512], F32))
    psV = stack.enter_context(nc.psum_tensor("d_psV", [128, 512], F32))
    psS = stack.enter_context(nc.psum_tensor("d_psS", [64, 1024], F32))
    psO = stack.enter_context(nc.psum_tensor("d_psO", [64, 512], F32))

    if True:
        with nc.Block() as block:
            @block.gpsimd
            def _(gp):
                ctx.inc(gp.dma_start(out=stril4[:, :], in_=stril4_in), "ld", 16)
                ctx.inc(gp.dma_start(out=triu4[:, :], in_=triu4_in), "ld", 16)
                ctx.inc(gp.dma_start(out=eye_sb[:, :], in_=eye_in), "ld", 16)
                ctx.inc(gp.dma_start(out=eye32_sb[:, :], in_=eye_in), "ld", 16)
                gp.memset(S32[:, :], 0.0)
                inst = gp.memset(S_bf[:, :], 0.0)
                ctx.inc(inst, "gpc")
        ld0 = ctx.v("ld")
        gp0 = ctx.v("gpc")
        with nc.Block() as block:
            @block.tensor
            def _(te):
                te.wait_ge(ctx.sem("ld"), ld0)
                te.wait_ge(ctx.sem("gpc"), gp0)
            @block.vector
            def _(ve):
                ve.wait_ge(ctx.sem("ld"), ld0)
                ve.wait_ge(ctx.sem("gpc"), gp0)
            @block.scalar
            def _(se):
                se.wait_ge(ctx.sem("ld"), ld0)
                se.wait_ge(ctx.sem("gpc"), gp0)

        for c in range(NS):
            c0 = 128 * c
            b_mm = ctx.v("mm")
            b_ea = ctx.v("ea")
            with nc.Block() as block:
                @block.gpsimd
                def _(gp, c0=c0, b_mm=b_mm, b_ea=b_ea):
                    gp.wait_ge(ctx.sem("mm"), b_mm)
                    gp.wait_ge(ctx.sem("ea"), b_ea)
                    for half in range(2):
                        inst = gp.dma_start(
                            out=bass.AP(kf_c, 1024 * half, [[2048, 64], [128, 8], [1, 128]]),
                            in_=bass.AP(k_fm.tensor, k_fm.offset + c0 + half * 8 * 64 * T,
                                        [[T, 64], [64 * T, 8], [1, 128]]))
                        ctx.inc(inst, "ldx", 16)
                        inst = gp.dma_start(
                            out=bass.AP(qf_c, 1024 * half, [[2048, 64], [128, 8], [1, 128]]),
                            in_=bass.AP(q_fm.tensor, q_fm.offset + c0 + half * 8 * 64 * T,
                                        [[T, 64], [64 * T, 8], [1, 128]]))
                        ctx.inc(inst, "ldx", 16)
                    ctx.inc(gp.dma_start(out=kt_c[:, :], in_=k_tm[c0:c0 + 128, :]), "ldx", 16)
                    ctx.inc(gp.dma_start(out=vt_c[:, :], in_=v_tm[c0:c0 + 128, :]), "ldx", 16)
                    ctx.inc(gp.dma_start(out=beta_c[:, :], in_=beta_tm[c0:c0 + 128, :]), "ldx", 16)
            ld_c = ctx.v("ldx")

            for q4 in range(4):
                with nc.Block() as block:
                    @block.tensor
                    def _(te, q4=q4, ld_c=ld_c):
                        te.wait_ge(ctx.sem("ldx"), ld_c)
                        te.wait_ge(ctx.sem("ev"), ctx.v("ev"))
                        te.wait_ge(ctx.sem("ea"), ctx.v("ea"))
                        for i in range(4):
                            h = 4 * q4 + i
                            sl = kf_c[:, 128 * h:128 * (h + 1)]
                            inst = te.matmul(psG[:, 128 * i:128 * (i + 1)], sl, sl,
                                             start=True, stop=True)
                        ctx.inc(inst, "mm")
                    v_mm = ctx.v("mm")

                    @block.scalar
                    def _(se, q4=q4):
                        for i in range(4):
                            h = 4 * q4 + i
                            se.activation(Y_all[:, 128 * h:128 * h + 64],
                                          vt_c[:, 64 * h:64 * h + 64],
                                          AF.Identity, scale=beta_c[:, h:h + 1])
                            inst = se.activation(Y_all[:, 128 * h + 64:128 * h + 128],
                                                 kt_c[:, 64 * h:64 * h + 64],
                                                 AF.Identity, scale=beta_c[:, h:h + 1])
                        ctx.inc(inst, "ea")

                    @block.vector
                    def _(ve, q4=q4, v_mm=v_mm):
                        ve.wait_ge(ctx.sem("mm"), v_mm)
                        ctx.seq(ve, "vv", ve.scalar_tensor_tensor(
                            N_all[:, 512 * q4:512 * (q4 + 1)], psG[:, :], -1.0,
                            stril4[:, :], ALU.mult, ALU.mult))
                        bexp = bass.AP(beta_c, 4 * q4, [[H, 128], [1, 4], [0, 128]])
                        inst = ve.tensor_mul(N_all[:, 512 * q4:512 * (q4 + 1)],
                                             N_all[:, 512 * q4:512 * (q4 + 1)], bexp)
                        ctx.inc(inst, "ev")
                    v_ev = ctx.v("ev")

                    @block.tensor
                    def _(te, q4=q4, v_ev=v_ev):
                        te.wait_ge(ctx.sem("ev"), v_ev)
                        for i in range(4):
                            h = 4 * q4 + i
                            inst = te.matmul(psY[:, 128 * i:128 * (i + 1)],
                                             kf_c[:, 128 * h:128 * (h + 1)],
                                             qf_c[:, 128 * h:128 * (h + 1)],
                                             start=True, stop=True)
                        ctx.inc(inst, "mm")
                    kq_mm = ctx.v("mm")

                    @block.vector
                    def _(ve, q4=q4, kq_mm=kq_mm):
                        ve.wait_ge(ctx.sem("mm"), kq_mm)
                        inst = ve.tensor_mul(Gm_all[:, 512 * q4:512 * (q4 + 1)],
                                             psY[:, :], triu4[:, :])
                        ctx.inc(inst, "ev")

                    @block.tensor
                    def _(te, q4=q4, v_ev=v_ev):
                        te.wait_ge(ctx.sem("ev"), v_ev)
                        for i in range(4):
                            h = 4 * q4 + i
                            inst = te.transpose(psT[:, 128 * i:128 * (i + 1)],
                                                N_all[:, 128 * h:128 * (h + 1)], eye32_sb[:, :])
                        ctx.inc(inst, "mm")
                    t_mm = ctx.v("mm")

                    @block.vector
                    def _(ve, q4=q4, t_mm=t_mm):
                        ve.wait_ge(ctx.sem("mm"), t_mm)
                        inst = ve.tensor_copy(P_all[:, 512 * q4:512 * (q4 + 1)], psT[:, :])
                        ctx.inc(inst, "ev")

            for j in range(7):
                for q4 in range(4):
                    with nc.Block() as block:
                        @block.tensor
                        def _(te, j=j, q4=q4):
                            te.wait_ge(ctx.sem("ev"), ctx.v("ev"))
                            te.wait_ge(ctx.sem("ea"), ctx.v("ea"))
                            for i in range(4):
                                h = 4 * q4 + i
                                inst = te.matmul(psX[:, 128 * i:128 * (i + 1)],
                                                 P_all[:, 128 * h:128 * (h + 1)],
                                                 Y_all[:, 128 * h:128 * (h + 1)],
                                                 start=True, stop=True)
                            ctx.inc(inst, "mm")
                            if j < 6:
                                for i in range(4):
                                    h = 4 * q4 + i
                                    inst = te.matmul(psY[:, 128 * i:128 * (i + 1)],
                                                     P_all[:, 128 * h:128 * (h + 1)],
                                                     N_all[:, 128 * h:128 * (h + 1)],
                                                     start=True, stop=True)
                                ctx.inc(inst, "mm")
                                for i in range(4):
                                    h = 4 * q4 + i
                                    inst = te.matmul(psV[:, 128 * i:128 * (i + 1)],
                                                     N_all[:, 128 * h:128 * (h + 1)],
                                                     P_all[:, 128 * h:128 * (h + 1)],
                                                     start=True, stop=True)
                                ctx.inc(inst, "mm")
                        v_mm = ctx.v("mm")

                        @block.vector
                        def _(ve, j=j, q4=q4, v_mm=v_mm):
                            ve.wait_ge(ctx.sem("mm"), v_mm - (2 if j < 6 else 0))
                            ctx.seq(ve, "vv", ve.tensor_add(Y_all[:, 512 * q4:512 * (q4 + 1)],
                                                            Y_all[:, 512 * q4:512 * (q4 + 1)],
                                                            psX[:, :]))
                            if j < 6:
                                ve.wait_ge(ctx.sem("mm"), v_mm)
                                ctx.seq(ve, "vv", ve.tensor_copy(
                                    N_all[:, 512 * q4:512 * (q4 + 1)], psY[:, :]))
                                inst = ve.tensor_copy(P_all[:, 512 * q4:512 * (q4 + 1)],
                                                      psV[:, :])
                            else:
                                inst = ve.tensor_copy(
                                    bass.AP(Vt_all, 256 * q4, [[1024, 128], [64, 4], [1, 64]]),
                                    bass.AP(Y_all, 512 * q4, [[2048, 128], [128, 4], [1, 64]]))
                            ctx.inc(inst, "ev")

            for q4 in range(4):
                with nc.Block() as block:
                    @block.tensor
                    def _(te, q4=q4):
                        te.wait_ge(ctx.sem("ev"), ctx.v("ev"))
                        for i in range(4):
                            h = 4 * q4 + i
                            inst = te.transpose(psT[0:64, 128 * i:128 * (i + 1)],
                                                Y_all[:, 128 * h + 64:128 * h + 128],
                                                eye32_sb[:, :])
                        ctx.inc(inst, "mm")
                    t_mm = ctx.v("mm")

                    @block.vector
                    def _(ve, q4=q4, t_mm=t_mm):
                        ve.wait_ge(ctx.sem("mm"), t_mm)
                        inst = ve.tensor_copy(TKf[0:64, 512 * q4:512 * (q4 + 1)], psT[0:64, :])
                        ctx.inc(inst, "ev")

            with nc.Block() as block:
                @block.tensor
                def _(te):
                    te.wait_ge(ctx.sem("ev"), ctx.v("ev"))
                    te.wait_ge(ctx.sem("ea"), ctx.v("ea"))
                    for h in range(8):
                        inst = te.matmul(psV[:, 64 * h:64 * (h + 1)],
                                         TKf[0:64, 128 * h:128 * (h + 1)],
                                         S_bf[:, 64 * h:64 * (h + 1)], start=True, stop=True)
                    ctx.inc(inst, "mm")
                    for h in range(8, H):
                        inst = te.matmul(psG[:, 64 * (h - 8):64 * (h - 7)],
                                         TKf[0:64, 128 * h:128 * (h + 1)],
                                         S_bf[:, 64 * h:64 * (h + 1)], start=True, stop=True)
                    ctx.inc(inst, "mm")
                v_mm = ctx.v("mm")

                @block.vector
                def _(ve, v_mm=v_mm):
                    ve.wait_ge(ctx.sem("mm"), v_mm)
                    yv0 = bass.AP(Y_all, 0, [[2048, 128], [128, 8], [1, 64]])
                    yv1 = bass.AP(Y_all, 128 * 8, [[2048, 128], [128, 8], [1, 64]])
                    ctx.seq(ve, "vv", ve.tensor_sub(Vt_all[:, 0:512], yv0, psV[:, :]))
                    inst = ve.tensor_sub(Vt_all[:, 512:1024], yv1, psG[:, :])
                    ctx.inc(inst, "ev")
                v_ev = ctx.v("ev")

                ea_pred = ctx.v("ea")

                @block.tensor
                def _(te, v_ev=v_ev, ea_pred=ea_pred):
                    te.wait_ge(ctx.sem("ev"), v_ev)
                    for q4 in range(4):
                        if q4 > 0:
                            te.wait_ge(ctx.sem("ea"), ea_pred + q4)
                        for i in range(4):
                            h = 4 * q4 + i
                            te.matmul(psO[:, 128 * i:128 * (i + 1)],
                                      S_bf[:, 64 * h:64 * (h + 1)],
                                      qf_c[:, 128 * h:128 * (h + 1)],
                                      start=True, stop=False)
                            inst = te.matmul(psO[:, 128 * i:128 * (i + 1)],
                                             Vt_all[:, 64 * h:64 * (h + 1)],
                                             Gm_all[:, 128 * h:128 * (h + 1)],
                                             start=False, stop=True)
                        ctx.inc(inst, "mm")
                    for h in range(H):
                        inst = te.matmul(psS[:, 64 * h:64 * (h + 1)],
                                         kt_c[:, 64 * h:64 * (h + 1)],
                                         Vt_all[:, 64 * h:64 * (h + 1)], start=True, stop=True)
                    ctx.inc(inst, "mm")
                o_mm = ctx.v("mm")

                @block.scalar
                def _(se, v_mm=v_mm, c=c):
                    for q4 in range(4):
                        se.wait_ge(ctx.sem("mm"), v_mm + q4 + 1)
                        for i in range(4):
                            h = 4 * q4 + i
                            fc2 = h // 2
                            pbase = 64 * (h % 2)
                            inst = se.copy(
                                o_fm[pbase:pbase + 64, T * fc2 + 128 * c:T * fc2 + 128 * (c + 1)],
                                psO[:, 128 * i:128 * (i + 1)])
                        ctx.inc(inst, "ea")

                @block.vector
                def _(ve, o_mm=o_mm):
                    ve.wait_ge(ctx.sem("mm"), o_mm)
                    inst = ve.tensor_add(S32[:, :], S32[:, :], psS[:, :])
                    ctx.inc(inst, "ev")
                s_ev = ctx.v("ev")

                @block.scalar
                def _(se, s_ev=s_ev):
                    se.wait_ge(ctx.sem("ev"), s_ev)
                    inst = se.copy(S_bf[:, :], S32[:, :])
                    ctx.inc(inst, "ea")

        wo_big = stack.enter_context(nc.sbuf_tensor("d_wo", [128, NC * D], BF16))
        ln2_sb = stack.enter_context(nc.sbuf_tensor("d_ln2", [128, NC], F32))
        res_sb = stack.enter_context(nc.sbuf_tensor("d_res", [128, NC * 512], F32))
        s32_sb = stack.enter_context(nc.sbuf_tensor("d_s32", [128, NC * 512], F32))
        sqbf2 = stack.enter_context(nc.sbuf_tensor("d_sqbf", [128, 512], BF16))
        row2 = stack.enter_context(nc.sbuf_tensor("d_row", [1, 1024], F32))
        rowb2 = stack.enter_context(nc.sbuf_tensor("d_rowb", [1, 1024], BF16))
        rb232 = stack.enter_context(nc.sbuf_tensor("d_rb32", [128, 512], F32))
        t32 = stack.enter_context(nc.sbuf_tensor("d_t32", [128, 512], F32))
        ybf_sb = stack.enter_context(nc.sbuf_tensor("d_ybf", [128, NC * 512], BF16))
        ones128b = stack.enter_context(nc.sbuf_tensor("d_ones128", [128, 1], BF16))
        onesrowb = stack.enter_context(nc.sbuf_tensor("d_onesrow", [1, 128], BF16))

        with nc.Block() as block:
            @block.gpsimd
            def _(gp):
                for i in range(NC):
                    ctx.inc(gp.dma_start(out=wo_big[:, D * i:D * (i + 1)],
                                         in_=Wo_in[128 * i:128 * (i + 1), :]), "ld", 16)
                ctx.inc(gp.dma_start(out=ln2_sb[:, :], in_=ln2_pc), "ld", 16)
                gp.memset(ones128b[:, :], 1.0)
                inst = gp.memset(onesrowb[:, :], 1.0)
                ctx.inc(inst, "gpc")
        ld1 = ctx.v("ld")
        gp1 = ctx.v("gpc")

        for tt in range(NT):
            t0, t1 = 512 * tt, 512 * (tt + 1)
            b_mm = ctx.v("mm")
            b_ea = ctx.v("ea")
            b_ev = ctx.v("ev")
            b_st = ctx.v("st")
            with nc.Block() as block:
                @block.gpsimd
                def _(gp, t0=t0, t1=t1, b_ev=b_ev):
                    gp.wait_ge(ctx.sem("ev"), b_ev)
                    for fc in range(NC):
                        inst = gp.dma_start(out=res_sb[:, 512 * fc:512 * (fc + 1)],
                                            in_=x2_32[128 * fc:128 * (fc + 1), t0:t1])
                        ctx.inc(inst, "ldx", 16)
                ld_v = ctx.v("ldx")

                @block.tensor
                def _(te, t0=t0, t1=t1, b_ea=b_ea):
                    te.wait_ge(ctx.sem("ld"), ld1)
                    te.wait_ge(ctx.sem("gpc"), gp1)
                    te.wait_ge(ctx.sem("ea"), b_ea)
                    for fc in range(NC):
                        ps = psX if fc % 2 == 0 else psY
                        if fc >= 2:
                            te.wait_ge(ctx.sem("ea"), b_ea + (fc - 1))
                        inst = mm_accum(te, ps[:, :],
                                        [(wo_big[:, D * ec + 128 * fc:D * ec + 128 * (fc + 1)],
                                          o_fm[:, T * ec + t0:T * ec + t1]) for ec in range(NC)])
                        ctx.inc(inst, "mm")

                @block.scalar
                def _(se, b_mm=b_mm, ld_v=ld_v, b_ev=b_ev, b_st=b_st):
                    se.wait_ge(ctx.sem("ldx"), ld_v)
                    se.wait_ge(ctx.sem("ev"), b_ev)
                    se.wait_ge(ctx.sem("st"), b_st)
                    for fc in range(NC):
                        ps = psX if fc % 2 == 0 else psY
                        se.wait_ge(ctx.sem("mm"), b_mm + fc + 1)
                        inst = se.copy(s32_sb[:, 512 * fc:512 * (fc + 1)], ps[:, :])
                        ctx.inc(inst, "ea")

                @block.vector
                def _(ve, b_ea=b_ea, b_mm=b_mm):
                    for fc in range(NC):
                        ve.wait_ge(ctx.sem("ea"), b_ea + fc + 1)
                        if fc >= 1:
                            ve.wait_ge(ctx.sem("mm"), b_mm + NC + fc)
                        sl = s32_sb[:, 512 * fc:512 * (fc + 1)]
                        ctx.seq(ve, "vv", ve.tensor_add(sl, sl, res_sb[:, 512 * fc:512 * (fc + 1)]))
                        inst = ve.tensor_mul(sqbf2[:, :], sl, sl)
                        ctx.inc(inst, "ev")

                @block.tensor
                def _(te, b_ev=b_ev):
                    for fc in range(NC):
                        te.wait_ge(ctx.sem("ev"), b_ev + fc + 1)
                        inst = te.matmul(psS[0:1, 0:512], ones128b[:, 0:1], sqbf2[:, :],
                                         start=(fc == 0), stop=(fc == NC - 1))
                        ctx.inc(inst, "mm")
                c_mm = ctx.v("mm")

                @block.vector
                def _(ve, c_mm=c_mm):
                    ve.wait_ge(ctx.sem("mm"), c_mm)
                    ctx.seq(ve, "vv", ve.tensor_scalar_mul(row2[0:1, 0:512], psS[0:1, 0:512], 1.0 / D))
                    inst = ve.tensor_scalar_add(row2[0:1, 0:512], row2[0:1, 0:512], 1e-6)
                    ctx.inc(inst, "ev")
                c_ev = ctx.v("ev")

                @block.scalar
                def _(se, c_ev=c_ev):
                    se.wait_ge(ctx.sem("ev"), c_ev)
                    inst = se.activation(row2[0:1, 0:512], row2[0:1, 0:512], AF.Sqrt)
                    ctx.inc(inst, "ea")
                c_ea = ctx.v("ea")

                @block.vector
                def _(ve, c_ea=c_ea):
                    ve.wait_ge(ctx.sem("ea"), c_ea)
                    ctx.seq(ve, "vv", ve.reciprocal(row2[0:1, 512:1024], row2[0:1, 0:512]))
                    ctx.seq(ve, "vv", ve.tensor_copy(rowb2[0:1, 0:512], row2[0:1, 512:1024]))
                    inst = ve.tensor_sub(rowb2[0:1, 512:1024], row2[0:1, 512:1024],
                                         rowb2[0:1, 0:512])
                    ctx.inc(inst, "ev")
                c_ev2 = ctx.v("ev")

                @block.tensor
                def _(te, c_ev2=c_ev2, b_ea=b_ea):
                    te.wait_ge(ctx.sem("ev"), c_ev2)
                    te.wait_ge(ctx.sem("ea"), b_ea + NC)
                    te.matmul(psX[:, :], onesrowb[0:1, :], rowb2[0:1, 0:512],
                              start=True, stop=False)
                    inst = te.matmul(psX[:, :], onesrowb[0:1, :], rowb2[0:1, 512:1024],
                                     start=False, stop=True)
                    ctx.inc(inst, "mm")
                c_mm2 = ctx.v("mm")

                @block.scalar
                def _(se, c_mm2=c_mm2):
                    se.wait_ge(ctx.sem("mm"), c_mm2)
                    inst = se.copy(rb232[:, :], psX[:, :])
                    ctx.inc(inst, "ea")
                c_ea2 = ctx.v("ea")

                @block.vector
                def _(ve, c_ea2=c_ea2):
                    for fc in range(NC):
                        ve.wait_ge(ctx.sem("ea"), c_ea2 + fc)
                        ctx.seq(ve, "vv", ve.tensor_mul(t32[:, :], s32_sb[:, 512 * fc:512 * (fc + 1)],
                                                        rb232[:, :]))
                        inst = ve.tensor_add(s32_sb[:, 512 * fc:512 * (fc + 1)],
                                             s32_sb[:, 512 * fc:512 * (fc + 1)],
                                             res_sb[:, 512 * fc:512 * (fc + 1)])
                        ctx.inc(inst, "ev")

                @block.scalar
                def _(se, c_ev2=c_ev2):
                    for fc in range(NC):
                        se.wait_ge(ctx.sem("ev"), c_ev2 + fc + 1)
                        inst = se.activation(ybf_sb[:, 512 * fc:512 * (fc + 1)], t32[:, :],
                                             AF.Identity, scale=ln2_sb[:, fc:fc + 1])
                        ctx.inc(inst, "ea")
                c_ea3 = ctx.v("ea")

                @block.sync
                def _(sy, t0=t0, t1=t1, c_ea3=c_ea3):
                    sy.wait_ge(ctx.sem("ea"), c_ea3)
                    sy.wait_ge(ctx.sem("ev"), ctx.v("ev"))
                    for fc in range(NC):
                        inst = sy.dma_start(out=hh_32[128 * fc:128 * (fc + 1), t0:t1],
                                            in_=s32_sb[:, 512 * fc:512 * (fc + 1)])
                        ctx.inc(inst, "st", 16)
                        inst = sy.dma_start(out=y_dram[128 * fc:128 * (fc + 1), t0:t1],
                                            in_=ybf_sb[:, 512 * fc:512 * (fc + 1)])
                        ctx.inc(inst, "st", 16)

        gu_sb = o_fm[:, 0:NHC * 512]
        ybuf = o_fm[:, NHC * 512:NHC * 512 + NC * 512]
        wg_sb = [o_fm[:, 20480 + 1024 * i:20480 + 1024 * (i + 1)] for i in range(2)]
        wu_sb = [o_fm[:, 22528 + 1024 * i:22528 + 1024 * (i + 1)] for i in range(2)]
        wd_sb = [o_fm[:, 24576 + 4096 * i:24576 + 4096 * (i + 1)] for i in range(2)]
        g_bf = stack.enter_context(nc.sbuf_tensor("d_gbf", [128, 512], BF16))

        pl = PostLN(ctx, "p5f", n_add=1, eps=1e-5,
                    ps_share={"ps_x": [psX, psY], "ps_st": psS, "ps_mb": psG,
                              "ps_rb": psV},
                    sb_share={"res0": res_sb, "s32": s32_sb, "xbf": ybf_sb,
                              "sbf": sqbf2, "rowb": Gm_all[0:1, 0:2048],
                              "rb32": rb232})
        pl.emit_init()

        glu_state = {}

        def pre_tile(tt):
            t0 = 512 * tt
            h_mm = ctx.v("mm")
            h_ea = ctx.v("ea")
            h_ev = ctx.v("ev")
            h_ld = ctx.v("ldx")
            h_st = ctx.v("st")
            with nc.Block() as block:
                @block.gpsimd
                def _(gp, tt=tt, t0=t0, h_mm=h_mm, h_st=h_st):
                    gp.wait_ge(ctx.sem("mm"), h_mm)
                    gp.wait_ge(ctx.sem("st"), h_st)
                    for dc in range(NC):
                        inst = gp.dma_start(out=ybuf[:, 512 * dc:512 * (dc + 1)],
                                            in_=y_dram[128 * dc:128 * (dc + 1), t0:t0 + 512])
                        ctx.inc(inst, "ldx", 16)
                ld_y = ctx.v("ldx")

                @block.gpsimd
                def _(gp, tt=tt, h_mm=h_mm):
                    for fc in range(NHC):
                        if fc >= 2:
                            gp.wait_ge(ctx.sem("mm"), h_mm + 2 * (fc - 1))
                        ctx.inc(gp.dma_start(out=wg_sb[fc % 2][:, :], in_=wgh[fc]), "ldx", 16)
                        ctx.inc(gp.dma_start(out=wu_sb[fc % 2][:, :], in_=wuh[fc]), "ldx", 16)

                @block.tensor
                def _(te, tt=tt, ld_y=ld_y, h_ea=h_ea, h_ev=h_ev):
                    te.wait_ge(ctx.sem("ldx"), ld_y)
                    te.wait_ge(ctx.sem("ea"), h_ea)
                    te.wait_ge(ctx.sem("ev"), h_ev)
                    for fc in range(NHC):
                        te.wait_ge(ctx.sem("ldx"), ld_y + 32 * (fc + 1))
                        if fc >= 1:
                            te.wait_ge(ctx.sem("ev"), h_ev + fc)
                            te.wait_ge(ctx.sem("ea"), h_ea + fc)
                        inst = mm_accum(te, psX[:, :],
                                        [(wg_sb[fc % 2][:, 128 * dc:128 * (dc + 1)],
                                          ybuf[:, 512 * dc:512 * (dc + 1)]) for dc in range(NC)])
                        ctx.inc(inst, "mm")
                        inst = mm_accum(te, psY[:, :],
                                        [(wu_sb[fc % 2][:, 128 * dc:128 * (dc + 1)],
                                          ybuf[:, 512 * dc:512 * (dc + 1)]) for dc in range(NC)])
                        ctx.inc(inst, "mm")

                @block.scalar
                def _(se, tt=tt, h_mm=h_mm):
                    for fc in range(NHC):
                        se.wait_ge(ctx.sem("mm"), h_mm + 2 * fc + 1)
                        inst = se.activation(g_bf[:, :], psX[:, :], AF.Silu)
                        ctx.inc(inst, "ea")

                @block.vector
                def _(ve, tt=tt, h_mm=h_mm, h_ea=h_ea):
                    for fc in range(NHC):
                        ve.wait_ge(ctx.sem("mm"), h_mm + 2 * fc + 2)
                        ve.wait_ge(ctx.sem("ea"), h_ea + fc + 1)
                        inst = ve.tensor_mul(gu_sb[:, 512 * fc:512 * (fc + 1)], psY[:, :],
                                             g_bf[:, :])
                        ctx.inc(inst, "ev")

            glu_state["ea"] = ctx.v("ea")
            glu_state["ev"] = ctx.v("ev")

        def rhs_loader(gp, tt):
            base_mm = ctx.v("mm")
            glu_state["wld"] = ctx.v("ldx")
            for fc in range(NC):
                gp.wait_ge(ctx.sem("mm"), base_mm + (fc - 1 if fc >= 2 else 0))
                ctx.inc(gp.dma_start(out=wd_sb[fc % 2][:, :], in_=wdh[fc]), "ldx", 16)

        def producer(te, fc, tt, ps):
            te.wait_ge(ctx.sem("ev"), glu_state["ev"])
            te.wait_ge(ctx.sem("ldx"), glu_state["wld"] + 16 * (fc + 1))
            return mm_accum(te, ps[:, :],
                            [(wd_sb[fc % 2][:, 128 * ic:128 * (ic + 1)],
                              gu_sb[:, 512 * ic:512 * (ic + 1)]) for ic in range(NHC)])

        pl.emit(producer, rhs_loader, [hh_32], x3_32, x3_bf,
                pre_tile=pre_tile)

        with nc.Block() as block:
            @block.sync
            def _(sy):
                sy.wait_ge(ctx.sem("st"), ctx.v("st"))
    return


def _pretile(w, no, ni):
    w = np.asarray(w, dtype=np.float32)
    return np.ascontiguousarray(w.reshape(no, 128, ni, 128).transpose(0, 3, 2, 1)
                                .reshape(no, 128, ni * 128))


def build_fused():
    nc = bass.Bass("TRN2", target_bir_lowering=False)
    outer = ExitStack()
    with outer:
        ctx = Ctx(nc, outer)

        def EI(name, shape, dt=F32):
            return nc.dram_tensor(name, shape, dt, kind="ExternalInput").ap()

        def IN(name, shape, dt):
            return nc.dram_tensor(name, shape, dt, kind="Internal").ap()

        g = {}
        g["xT"] = EI("xT", [D, T])
        for nm in ("WqT", "WkT", "WvT", "WoT", "L1Wq", "L1Wk", "L1Wv", "L1Wo"):
            g[nm] = EI(nm, [D, D])
        for nm in ("qg2", "qb2", "kg2", "kb2"):
            g[nm] = EI(nm, [128, 1])
        g["gbrows"] = EI("gbrows", [1, 256])
        g["bo_pc"] = EI("bo_pc", [128, NC])
        g["slot_flat"] = EI("slot_flat", [T, D])
        g["gs_row"] = EI("gs_row", [1, D])
        g["gx_pc"] = EI("gx_pc", [128, NC])
        g["w1"] = EI("w1", [D, HID])
        g["w2"] = EI("w2", [HID, D])
        g["ln1_pc"] = EI("ln1_pc", [128, NC])
        g["ln2_pc"] = EI("ln2_pc", [128, NC])
        g["L1Wb"] = EI("L1Wb", [D, H])
        for nm in ("wgh", "wuh", "w1h", "w2h"):
            g[nm] = EI(nm, [NHC, 128, NC * 128])
        for nm in ("h1h", "h2h"):
            g[nm] = EI(nm, [NHC, 128, NHC * 128])
        for nm in ("wdh", "w3h"):
            g[nm] = EI(nm, [NC, 128, NHC * 128])
        for nm in ("s1_pc", "s2_pc", "sh1_pc", "sh2_pc"):
            g[nm] = EI(nm, [128, NHC])
        g["s3_pc"] = EI("s3_pc", [128, NC])
        g["C_ones2"] = EI("C_ones2", [D, H])
        g["C_sel"] = EI("C_sel", [H, D])
        g["C_stair"] = EI("C_stair", [128, 896])
        g["C_stril4"] = EI("C_stril4", [128, 512])
        g["C_triu4"] = EI("C_triu4", [128, 512])
        g["C_eye"] = EI("C_eye", [128, 128])

        g["q_fm0"] = IN("q_fm0", [D, T], BF16)
        g["k_fm0"] = IN("k_fm0", [D, T], BF16)
        g["v_tm0"] = IN("v_tm0", [H, T, HS], BF16)
        g["attn_out"] = IN("attn_out", [D, T], BF16)
        g["x1_32"] = IN("x1_32", [D, T], F32)
        g["x1_bf"] = IN("x1_bf", [D, T], BF16)
        g["se_tm"] = IN("se_tm", [T, D], BF16)
        g["se_fm"] = IN("se_fm", [D, T], BF16)
        g["xn_fm"] = IN("xn_fm", [D, T], BF16)
        g["xn_tm"] = IN("xn_tm", [T, D], BF16)
        g["m1_t"] = IN("m1_t", [32, 128], F32)
        g["mx2_sc"] = IN("mx2_sc", [32, 128], F32)
        g["rs2_sc"] = IN("rs2_sc", [1, T], F32)
        g["x2_32"] = IN("x2_32", [D, T], F32)
        g["x2_bf"] = IN("x2_bf", [D, T], BF16)
        g["q_fm1"] = IN("q_fm1", [D, T], BF16)
        g["k_fm1"] = IN("k_fm1", [D, T], BF16)
        g["k_tm1"] = IN("k_tm1", [T, D], BF16)
        g["v_tm1"] = IN("v_tm1", [T, D], BF16)
        g["beta_tm"] = IN("beta_tm", [T, H], F32)
        g["xn1_bf"] = IN("xn1_bf", [D, T], BF16)
        g["x3_32"] = IN("x3_32", [D, T], F32)
        g["x3_bf"] = IN("x3_bf", [D, T], BF16)
        g["hh_32"] = IN("hh_32", [D, T], F32)
        g["y_bfd"] = IN("y_bfd", [D, T], BF16)
        g["y"] = nc.dram_tensor("y", [D, T], F32, kind="ExternalOutput").ap()

        m1_row = bass.AP(g["m1_t"].tensor, 0, [[T, 1], [1, T]])

        phases = [
            (emit_p1, {"xT": g["xT"], "WqT": g["WqT"], "WkT": g["WkT"], "WvT": g["WvT"],
                       "C_ones2": g["C_ones2"], "C_sel": g["C_sel"], "qg2": g["qg2"],
                       "qb2": g["qb2"], "kg2": g["kg2"], "kb2": g["kb2"],
                       "q_fm": g["q_fm0"], "k_fm": g["k_fm0"], "v_tm": g["v_tm0"]}),
            (emit_p2, {"q_fm": g["q_fm0"], "k_fm": g["k_fm0"], "v_tm": g["v_tm0"],
                       "gbrows": g["gbrows"], "C_stair": g["C_stair"],
                       "attn_out": g["attn_out"]}),
            (emit_p3, {"attn_out": g["attn_out"], "WoT": g["WoT"], "bo_pc": g["bo_pc"],
                       "xT": g["xT"], "x1_32": g["x1_32"], "x1_bf": g["x1_bf"]}),
            (emit_p4a, {"slot_flat": g["slot_flat"], "gs_row": g["gs_row"],
                        "gx_pc": g["gx_pc"], "x1_bf": g["x1_bf"], "se_tm": g["se_tm"],
                        "se_fm": g["se_fm"], "xn_fm": g["xn_fm"], "xn_tm": g["xn_tm"],
                        "m1_out": g["m1_t"], "C_eye": g["C_eye"]}),
            (emit_p4b, {"se_fm": g["se_fm"], "xn_fm": g["xn_fm"], "xn_tm": g["xn_tm"],
                        "m1_row": m1_row, "w1": g["w1"], "w2": g["w2"],
                        "x1_32": g["x1_32"], "C_eye": g["C_eye"], "x2_32": g["x2_32"],
                        "x2_bf": g["x2_bf"], "mx2_sc": g["mx2_sc"], "rs2_sc": g["rs2_sc"]}),
            (emit_p5a, {"x2_bf": g["x2_bf"], "ln1_pc": g["ln1_pc"], "Wq": g["L1Wq"],
                        "Wk": g["L1Wk"], "Wv": g["L1Wv"], "Wb": g["L1Wb"],
                        "C_ones2": g["C_ones2"], "C_sel": g["C_sel"],
                        "q_fm": g["q_fm1"], "k_fm": g["k_fm1"], "v_tm": g["v_tm1"],
                        "k_tm": g["k_tm1"], "beta_tm": g["beta_tm"],
                        "xn1_bf": g["xn1_bf"]}),
            (emit_p5b, {"q_fm": g["q_fm1"], "k_fm": g["k_fm1"], "k_tm": g["k_tm1"],
                        "v_tm": g["v_tm1"], "beta_tm": g["beta_tm"], "x2_32": g["x2_32"],
                        "Wo": g["L1Wo"], "ln2_pc": g["ln2_pc"], "wgh": g["wgh"],
                        "wuh": g["wuh"], "wdh": g["wdh"], "C_stril4": g["C_stril4"],
                        "C_triu4": g["C_triu4"], "C_eye": g["C_eye"],
                        "x3_32": g["x3_32"], "x3_bf": g["x3_bf"], "hh_32": g["hh_32"],
                        "y_bfd": g["y_bfd"]}),
            (emit_p6, {"x3_bf": g["x3_bf"], "x3_32": g["x3_32"], "w1h": g["w1h"],
                       "w2h": g["w2h"], "h1h": g["h1h"], "h2h": g["h2h"],
                       "w3h": g["w3h"], "s1_pc": g["s1_pc"], "s2_pc": g["s2_pc"],
                       "sh1_pc": g["sh1_pc"], "sh2_pc": g["sh2_pc"], "s3_pc": g["s3_pc"],
                       "x4_32": g["y"]}),
        ]
        for fn, io in phases:
            with ExitStack() as pstack:
                fn(ctx, pstack, io)
            ctx.barrier()
    return nc


def kernel(**ins):
    SCALE = float(D) ** (-0.25)
    C = host_consts()
    B = 8
    x = np.asarray(ins["L0_attn_x"], dtype=np.float32)

    base = {
        "WqT": np.ascontiguousarray(np.asarray(ins["L0_attn_Wq"]).T),
        "WkT": np.ascontiguousarray(np.asarray(ins["L0_attn_Wk"]).T),
        "WvT": np.ascontiguousarray(np.asarray(ins["L0_attn_Wv"]).T),
        "WoT": np.ascontiguousarray(np.asarray(ins["L0_attn_Wo"]).T),
        "qg2": np.tile(np.asarray(ins["L0_attn_qln_g"]), 2).reshape(128, 1) * SCALE,
        "qb2": np.tile(np.asarray(ins["L0_attn_qln_b"]), 2).reshape(128, 1) * SCALE,
        "kg2": np.tile(np.asarray(ins["L0_attn_kln_g"]), 2).reshape(128, 1) * SCALE,
        "kb2": np.tile(np.asarray(ins["L0_attn_kln_b"]), 2).reshape(128, 1) * SCALE,
        "gbrows": np.concatenate([np.asarray(ins["L0_attn_qln_g"]),
                                  np.asarray(ins["L0_attn_qln_b"]),
                                  np.asarray(ins["L0_attn_kln_g"]),
                                  np.asarray(ins["L0_attn_kln_b"])]).reshape(1, 256).astype(np.float32),
        "bo_pc": to_pc(ins["L0_attn_bo"]),
        "slot_flat": np.ascontiguousarray(np.asarray(ins["L0_ffn_slot_embeds"],
                                                     dtype=np.float32).reshape(T, D)),
        "gs_row": np.asarray(ins["L0_ffn_gamma_slot"], dtype=np.float32).reshape(1, D),
        "gx_pc": to_pc(ins["L0_ffn_gamma_x"]),
        "w1": np.asarray(ins["L0_ffn_w1"], dtype=np.float32),
        "w2": np.asarray(ins["L0_ffn_w2"], dtype=np.float32),
        "ln1_pc": to_pc(ins["L1_attn_ln1"]),
        "ln2_pc": to_pc(ins["L1_attn_ln2"]),
        "L1Wq": np.asarray(ins["L1_attn_Wq"], dtype=np.float32),
        "L1Wk": np.asarray(ins["L1_attn_Wk"], dtype=np.float32),
        "L1Wv": np.asarray(ins["L1_attn_Wv"], dtype=np.float32),
        "L1Wb": np.asarray(ins["L1_attn_Wb"], dtype=np.float32),
        "L1Wo": np.asarray(ins["L1_attn_Wo"], dtype=np.float32),
        "wgh": _pretile(np.asarray(ins["L1_attn_Wg"]).T, NHC, NC),
        "wuh": _pretile(np.asarray(ins["L1_attn_Wu"]).T, NHC, NC),
        "wdh": _pretile(np.asarray(ins["L1_attn_Wd"]).T, NC, NHC),
        "w1h": _pretile(ins["L1_ffn_w1_w"], NHC, NC),
        "w2h": _pretile(ins["L1_ffn_w2_w"], NHC, NC),
        "h1h": _pretile(ins["L1_ffn_h1_w"], NHC, NHC),
        "h2h": _pretile(ins["L1_ffn_h2_w"], NHC, NHC),
        "w3h": _pretile(ins["L1_ffn_w3_w"], NC, NHC),
        "s1_pc": to_pc(ins["L1_ffn_w1_s"]), "s2_pc": to_pc(ins["L1_ffn_w2_s"]),
        "sh1_pc": to_pc(ins["L1_ffn_h1_s"]), "sh2_pc": to_pc(ins["L1_ffn_h2_s"]),
        "s3_pc": to_pc(ins["L1_ffn_w3_s"]),
        "C_ones2": C["C_ones2"], "C_sel": C["C_sel"], "C_stair": C["C_stair"],
        "C_stril4": C["C_stril4"], "C_triu4": C["C_triu4"], "C_eye": C["C_eye"],
    }
    nc = build_fused()
    in_maps = [dict(base, xT=np.ascontiguousarray(x[b].T)) for b in range(B)]
    res = run_bass_kernel_spmd(nc, in_maps, core_ids=list(range(B)))
    out = np.stack([res.results[b]["y"].T for b in range(B)], axis=0)
    return out.astype(np.float32)
